# version 15
# baseline (speedup 1.0000x reference)
import math
import numpy as np
import ml_dtypes
from contextlib import ExitStack

import concourse.bass as bass
import concourse.mybir as mybir
from concourse.bass_utils import run_bass_kernel_spmd

F32 = mybir.dt.float32
BF16 = mybir.dt.bfloat16
AF = mybir.ActivationFunctionType
ALU = mybir.AluOpType
AX = mybir.AxisListType

D = 2048
NFC = 16
H = 16
DH = 128
S = 4096
NT_ALL = 32
NOWN = 2048
NT_OWN = 16
NQC = 4
DFF = 5504
NDC = 43
E8 = 8
RK = 8
ER = 64
SCL = 1.0 / math.sqrt(128.0)
EPS = 1e-5
FFN_W = 1024
NWIN = 2

_CACHE = {}


class SC:
    def __init__(self, nc, ctx, name):
        self.h = ctx.enter_context(nc.semaphore(name))
        self.n = 0

    def inc(self, instr, k=1):
        instr.then_inc(self.h, k)
        self.n += k
        return instr


def build_nc(dbg=False):
    nc = bass.Bass("TRN2", target_bir_lowering=False)

    def din(name, shape, dt=F32):
        return nc.dram_tensor(name, list(shape), dt, kind="ExternalInput").ap()

    xb = din("xb", [S, D])
    xq = din("xq", [NOWN, D])
    mq = din("mq", [8, 128, 512], BF16)
    bias_pp = din("bias_pp", [128, 64])
    ebias_bc = din("ebias_bc", [128, E8])
    ln_g = din("ln_g", [D])
    ln_b = din("ln_b", [D])
    wq_d = din("wq", [D, D])
    wk_d = din("wk", [D, D])
    wv_d = din("wv", [D, D])
    wo_d = din("wo", [D, D])
    bo_d = din("bo", [D])
    wr_d = din("w_router", [D, E8])
    wg_d = din("w_gate", [D, DFF])
    wu_d = din("w_up", [D, DFF])
    wd_d = din("w_down", [DFF, D])
    sag_d = din("sa_gate", [D, RK])
    sbg_d = din("sb_gate", [RK, DFF])
    sau_d = din("sa_up", [D, RK])
    sbu_d = din("sb_up", [RK, DFF])
    sad_d = din("sa_down", [DFF, RK])
    sbd_d = din("sb_down", [RK, D])
    ag_d = din("a_gate", [E8, D, RK])
    bg_d = din("b_gate", [E8, RK, DFF])
    au_d = din("a_up", [E8, D, RK])
    bu_d = din("b_up", [E8, RK, DFF])
    ad_d = din("a_down", [E8, DFF, RK])
    bd_d = din("b_down", [E8, RK, D])

    out_d = nc.dram_tensor("out", [NOWN, D], F32, kind="ExternalOutput").ap()

    skind = "ExternalOutput" if dbg else "Internal"
    xn_c = nc.dram_tensor("xn_c", [NFC, S, 128], BF16, kind=skind).ap()
    xnq_c = nc.dram_tensor("xnq_c", [NFC, NOWN, 128], BF16, kind=skind).ap()
    q_c = nc.dram_tensor("q_c", [H, 128, NOWN], BF16, kind=skind).ap()
    k_c = nc.dram_tensor("k_c", [H, 128, S], BF16, kind=skind).ap()
    v_c = nc.dram_tensor("v_c", [4, 8, 4, 128, 512], BF16, kind=skind).ap()
    r1_c = nc.dram_tensor("r1_c", [NOWN, D], F32, kind=skind).ap()
    hid_c = nc.dram_tensor("hid_c", [NWIN, NDC, 128, FFN_W], BF16, kind=skind).ap() if dbg else None
    wgb_c = nc.dram_tensor("wgb_c", [NDC, 128, NFC * 128], BF16, kind=skind).ap()
    wub_c = nc.dram_tensor("wub_c", [NDC, 128, NFC * 128], BF16, kind=skind).ap()
    wdb_c = nc.dram_tensor("wdb_c", [NDC, 4, 128, 512], BF16, kind=skind).ap()
    xn2_c = nc.dram_tensor("xn2_c", [NFC, NOWN, 128], BF16, kind=skind).ap()

    ident_c = nc.inline_tensor(np.eye(128, dtype=np.float32), "identf")
    rsel_np = np.zeros((8, 64), np.float32)
    for e in range(8):
        rsel_np[e, e * 8:(e + 1) * 8] = 1.0 / RK
    rsel_c = nc.inline_tensor(rsel_np, "rself")

    ctx = ExitStack()
    with ctx:
        def sb(name, shape, dt):
            return ctx.enter_context(nc.sbuf_tensor(name, list(shape), dt))

        ones_col = sb("ones_col", [128, 1], BF16)
        ones_row = sb("ones_row", [1, 128], BF16)
        ones_row_f = sb("ones_row_f", [1, 128], F32)
        bias_sb = sb("bias_sb", [128, 64], F32)
        bqs = sb("bqs", [128, 16], F32)
        ebias_sb = sb("ebias_sb", [128, E8], F32)
        negmu = sb("negmu", [128, 1], F32)
        mu_bf = sb("mu_bf", [1, 1], BF16)
        s0m = sb("s0m", [128, 1], F32)
        e0 = sb("e0", [128, 1], BF16)
        linv = sb("linv", [1, 512], F32)
        lbc = sb("lbc", [128, 512], F32)
        avn = sb("avn", [128, 512], F32)
        tmp512 = sb("tmp512", [128, 512], F32)
        sx1 = sb("sx1", [128, 1], F32)
        ssq1 = sb("ssq1", [128, 1], F32)
        sy2 = sb("sy2", [128, 1], F32)
        ssq2 = sb("ssq2", [128, 1], F32)
        m1v = sb("m1v", [128, 1], F32)
        t1v = sb("t1v", [128, 1], F32)
        msq = sb("msq", [128, 1], F32)
        v1v = sb("v1v", [128, 1], F32)
        vpe = sb("vpe", [128, 1], F32)
        r1c = sb("r1c", [128, 1], F32)
        r2c = sb("r2c", [128, 1], F32)
        s1v = sb("s1v", [128, 1], F32)
        s2v = sb("s2v", [128, 1], F32)
        nm1 = sb("nm1", [128, 1], F32)
        nm1n = sb("nm1n", [128, 1], F32)
        nm2 = sb("nm2", [128, 1], F32)
        nm2n = sb("nm2n", [128, 1], F32)
        mu2v = sb("mu2v", [128, 1], F32)
        ssq1p = [sb("ssq1p%d" % i, [128, 1], F32) for i in range(2)]
        sy2p = [sb("sy2p%d" % i, [128, 1], F32) for i in range(2)]
        m1p = [sb("m1p%d" % i, [128, 1], F32) for i in range(2)]
        r1p = [sb("r1p%d" % i, [128, 1], F32) for i in range(2)]
        r2p = [sb("r2p%d" % i, [128, 1], F32) for i in range(2)]
        s1p = [sb("s1p%d" % i, [128, 1], F32) for i in range(2)]
        s2p = [sb("s2p%d" % i, [128, 1], F32) for i in range(2)]
        nm1p = [sb("nm1p%d" % i, [128, 1], F32) for i in range(2)]
        nm2p = [sb("nm2p%d" % i, [128, 1], F32) for i in range(2)]
        mu2p = [sb("mu2p%d" % i, [128, 1], F32) for i in range(2)]

        def S_(name):
            return SC(nc, ctx, name)

        sCON = S_("sCON")
        sBAR = S_("sBAR")
        pSq1 = S_("pSq1")
        pS1 = S_("pS1")
        pSq2 = S_("pSq2")
        pS2 = S_("pS2")
        pD1 = S_("pD1")
        pD3 = S_("pD3")
        pD4 = S_("pD4")
        pDg = S_("pDg")
        pD6 = S_("pD6")
        pG = S_("pG")
        qWL = S_("qWL")
        qX = S_("qX")
        qMM = S_("qMM")
        qEV = S_("qEV")
        kX = S_("kX")
        kKmm = S_("kKmm")
        kVmm = S_("kVmm")
        kKev = S_("kKev")
        kVev = S_("kVev")
        aHD = S_("aHD")
        aS0 = S_("aS0")
        aRM = S_("aRM")
        aE0 = S_("aE0")
        aMUP = S_("aMUP")
        aMUV = S_("aMUV")
        aNBP = S_("aNBP")
        aMU = S_("aMU")
        aST = S_("aST")
        aEXP = S_("aEXP")
        aMSK = S_("aMSK")
        aKT = S_("aKT")
        aLINV = S_("aLINV")
        aLB = S_("aLB")
        aLBC = S_("aLBC")
        aAVD = S_("aAVD")
        wLD = S_("wLD")
        wMM = S_("wMM")
        wEVA = S_("wEVA")
        wSq1 = S_("wSq1")
        wS1 = S_("wS1")
        wD1 = S_("wD1")
        wDg = S_("wDg")
        wAPL = S_("wAPL")
        fLD = S_("fLD")
        fX2 = S_("fX2")
        fRL = S_("fRL")
        fSG = S_("fSG")
        fTOP = S_("fTOP")
        fWT = S_("fWT")
        fWTE = S_("fWTE")
        fWB = S_("fWB")
        fWBE = S_("fWBE")
        fXA = S_("fXA")
        fXAW = S_("fXAW")
        fSX = S_("fSX")
        fSXE = S_("fSXE")
        fGMM = S_("fGMM")
        fUMM = S_("fUMM")
        fSIL = S_("fSIL")
        fUCP = S_("fUCP")
        fHID = S_("fHID")
        fHX = S_("fHX")
        fSXD = S_("fSXD")
        fHXW = S_("fHXW")
        fWDU = S_("fWDU")
        fDN = S_("fDN")
        fOEV = S_("fOEV")
        fDg = S_("fDg")
        cLD = S_("cLD")
        cWR = S_("cWR")
        pXp = [S_("pXp0"), S_("pXp1")]
        pWp = [S_("pWp0"), S_("pWp1")]
        qWRp = [S_("qWRp0"), S_("qWRp1")]
        kWRp = [S_("kWRp0"), S_("kWRp1")]
        vWRp = [S_("vWRp0"), S_("vWRp1")]
        aLDp = [S_("aLDp0"), S_("aLDp1")]
        wXQp = [S_("wXQp0"), S_("wXQp1")]
        wWRp = [S_("wWRp0"), S_("wWRp1")]
        fWGUp = [S_("fWGUp0"), S_("fWGUp1")]
        fWDp = [S_("fWDp0"), S_("fWDp1"), S_("fWDp2")]
        fR1p = [S_("fR1p0"), S_("fR1p1")]
        fOUTp = [S_("fOUTp0"), S_("fOUTp1")]
        fSBp = [S_("fSBp0"), S_("fSBp1")]

        block = ctx.enter_context(nc.Block())
        nbar = [0]

        def barrier():
            nbar[0] += 1
            tgt = 5 * nbar[0]

            def f(eng):
                eng.drain()
                eng.sem_inc(sBAR.h, 1)
                eng.wait_ge(sBAR.h, tgt)
            block.sync(f)
            block.gpsimd(f)
            block.tensor(f)
            block.vector(f)
            block.scalar(f)
            sBAR.n = tgt

        def p0_sync(sy):
            sy.dma_start(out=bias_sb[:], in_=bias_pp).then_inc(sCON.h, 16)
            sy.dma_start(out=ebias_sb[:], in_=ebias_bc).then_inc(sCON.h, 16)
        block.sync(p0_sync)
        sCON.n += 32

        def p0_vec(v):
            v.memset(ones_col[:], 1.0)
            v.memset(ones_row[:], 1.0)
            v.memset(ones_row_f[:], 1.0)
            v.wait_ge(sCON.h, 32)
            v.tensor_scalar_mul(bqs[:], bias_sb[:, 0:16], SCL).then_inc(sCON.h, 1)
        block.vector(p0_vec)
        sCON.n += 1

        NTT = NT_ALL + NT_OWN

        with ExitStack() as c1:
            def s1b(name, shape, dt):
                return c1.enter_context(nc.sbuf_tensor(name, list(shape), dt))
            xt = [s1b("p1_x0", [128, D], F32), s1b("p1_x1", [128, D], F32)]
            yb = s1b("p1_y", [128, D], F32)
            sqscr = s1b("p1_sq", [128, D], F32)
            ttb = [s1b("p1_t0", [128, D], F32), s1b("p1_t1", [128, D], F32)]
            g1b = s1b("p1_g1", [128, D], F32)
            xnb = [s1b("p1_xn0", [128, D], BF16), s1b("p1_xn1", [128, D], BF16)]
            gam = s1b("p1_gam", [128, D], F32)
            bet = s1b("p1_bet", [128, D], F32)

            def src_ap(t):
                if t < NT_ALL:
                    return xb[t * 128:(t + 1) * 128, :]
                tt = t - NT_ALL
                return xq[tt * 128:(tt + 1) * 128, :]

            def dst_ap(t):
                if t < NT_ALL:
                    return xn_c[:, t * 128:(t + 1) * 128, :].rearrange("c t f -> t c f")
                tt = t - NT_ALL
                return xnq_c[:, tt * 128:(tt + 1) * 128, :].rearrange("c t f -> t c f")

            def p1_sync(sy):
                sy.dma_start(out=gam[:], in_=ln_g[None, :].broadcast_to([128, D])).then_inc(sCON.h, 16)
                sy.dma_start(out=bet[:], in_=ln_b[None, :].broadcast_to([128, D])).then_inc(sCON.h, 16)
                for t in range(NTT):
                    if t >= 2:
                        sy.wait_ge(pD3.h, t - 1)
                    pXp[t % 2].inc(sy.dma_start(out=xt[t % 2][:], in_=src_ap(t)), 16)
                    if t >= 1:
                        sy.wait_ge(pG.h, t)
                        pWp[(t - 1) % 2].inc(sy.dma_start(
                            out=dst_ap(t - 1),
                            in_=xnb[(t - 1) % 2].ap().rearrange("p (c f) -> p c f", f=128)), 16)
                sy.wait_ge(pG.h, NTT)
                pWp[(NTT - 1) % 2].inc(sy.dma_start(
                    out=dst_ap(NTT - 1),
                    in_=xnb[(NTT - 1) % 2].ap().rearrange("p (c f) -> p c f", f=128)), 16)
            block.sync(p1_sync)
            sCON.n += 32

            DG1 = {}
            DG2 = {}

            def p1_vec(v):
                def g(ins):
                    pDg.inc(ins)
                    v.wait_ge(pDg.h, pDg.n)
                    return ins
                for t in range(-1, NTT):
                    if t + 1 < NTT:
                        tn = t + 1
                        p = tn % 2
                        v.wait_ge(pXp[p].h, 16 * (tn // 2 + 1))
                        g(v.reduce_sum(sx1[:], xt[p][:], axis=AX.X))
                        v.wait_ge(pSq1.h, tn + 1)
                        g(v.tensor_scalar_mul(m1p[p][:], sx1[:], 1.0 / D))
                        g(v.tensor_scalar_mul(t1v[:], ssq1p[p][:], 1.0 / D))
                        g(v.tensor_mul(msq[:], m1p[p][:], m1p[p][:]))
                        g(v.tensor_sub(v1v[:], t1v[:], msq[:]))
                        g(v.tensor_scalar_add(vpe[:], v1v[:], EPS))
                        pD1.inc(v.reciprocal(r1p[p][:], vpe[:]))
                        v.wait_ge(pD1.h, pD1.n)
                        g(v.tensor_mul(msq[:], v1v[:], r1p[p][:]))
                        g(v.tensor_scalar_add(vpe[:], msq[:], EPS))
                        pD4.inc(v.reciprocal(r2p[p][:], vpe[:]))
                    if t >= 0:
                        p = t % 2
                        v.wait_ge(pS1.h, t + 1)
                        g(v.tensor_scalar(nm1p[p][:], m1p[p][:], s1p[p][:], -1.0,
                                          ALU.mult, ALU.mult))
                        DG1[t] = pDg.n
                        v.wait_ge(pD3.h, t + 1)
                        g(v.tensor_scalar_mul(mu2p[p][:], sy2p[p][:], 1.0 / D))
                        v.wait_ge(pS2.h, t + 1)
                        g(v.tensor_scalar(nm2p[p][:], mu2p[p][:], s2p[p][:], -1.0,
                                          ALU.mult, ALU.mult))
                        DG2[t] = pDg.n
            block.vector(p1_vec)
            assert pD4.n == NTT and pD1.n == NTT

            def p1_act(s):
                for t in range(-1, NTT):
                    if t + 1 < NTT:
                        tn = t + 1
                        p = tn % 2
                        s.wait_ge(pXp[p].h, 16 * (tn // 2 + 1))
                        pSq1.inc(s.activation(sqscr[:], xt[p][:], AF.Square,
                                              accum_out=ssq1p[p][:]))
                    if t >= 0:
                        p = t % 2
                        s.wait_ge(pD1.h, t + 1)
                        pS1.inc(s.activation(s1p[p][:], r1p[p][:], AF.Sqrt))
                        s.wait_ge(pDg.h, DG1[t])
                        pD3.inc(s.activation(yb[:], xt[p][:], AF.Identity,
                                             bias=nm1p[p][:], scale=s1p[p][:],
                                             accum_out=sy2p[p][:]))
                        s.wait_ge(pD4.h, t + 1)
                        pS2.inc(s.activation(s2p[p][:], r2p[p][:], AF.Sqrt))
                        s.wait_ge(pDg.h, DG2[t])
                        if t >= 2:
                            s.wait_ge(pG.h, t - 1)
                        pD6.inc(s.activation(ttb[p][:], yb[:], AF.Identity,
                                             bias=nm2p[p][:], scale=s2p[p][:]))
            block.scalar(p1_act)
            assert pSq1.n == NTT and pS2.n == NTT and pD6.n == NTT

            def p1_gp(g):
                g.wait_ge(sCON.h, 65)
                for t in range(NTT):
                    p = t % 2
                    g.wait_ge(pD6.h, t + 1)
                    if t >= 2:
                        g.wait_ge(pWp[t % 2].h, 16 * (t // 2))
                    g.tensor_mul(g1b[:], ttb[p][:], gam[:])
                    pG.inc(g.tensor_add(xnb[p][:], g1b[:], bet[:]))
            block.gpsimd(p1_gp)
            assert pG.n == NTT

            barrier()

        with ExitStack() as c2:
            def s2b(name, shape, dt):
                return c2.enter_context(nc.sbuf_tensor(name, list(shape), dt))
            wq_sb = s2b("wq_sb", [128, NFC * D], BF16)
            xnqt = [s2b("xnqt0", [128, NFC * 512], BF16),
                    s2b("xnqt1", [128, NFC * 512], BF16)]
            qev = [s2b("qev0", [128, 512], BF16), s2b("qev1", [128, 512], BF16)]
            q_ps = [c2.enter_context(nc.psum_tensor("q_ps%d" % i, [128, 512], F32))
                    for i in range(3)]

            def p15a_gp(g):
                for fc in range(NFC):
                    qWL.inc(g.dma_start(
                        out=wq_sb[:, fc * D:(fc + 1) * D],
                        in_=wq_d[fc * 128:(fc + 1) * 128, :]), 16)
            block.gpsimd(p15a_gp)
            assert qWL.n == 256

            def p15a_sc(s):
                s.wait_ge(sCON.h, 33)
                s.wait_ge(pWp[0].h, pWp[0].n)
                s.wait_ge(pWp[1].h, pWp[1].n)
                for qw in range(NQC + 1):
                    if qw < NQC:
                        if qw >= 2:
                            s.wait_ge(qMM.h, 16 * (qw - 1))
                        for fc in range(NFC):
                            qX.inc(s.dma_start_transpose(
                                xnqt[qw % 2][:, fc * 512:(fc + 1) * 512],
                                xnq_c[fc, qw * 512:(qw + 1) * 512, :]), 16)
                    if qw >= 1:
                        for h in range(H):
                            idx = (qw - 1) * 16 + h
                            s.wait_ge(qMM.h, idx + 1)
                            if idx >= 2:
                                s.wait_ge(qWRp[idx % 2].h, 16 * (idx // 2))
                            qEV.inc(s.activation(qev[h % 2][:], q_ps[idx % 3][:],
                                                 AF.Identity, bias=bqs[:, h:h + 1],
                                                 scale=SCL))
            block.scalar(p15a_sc)
            assert qX.n == 16 * 64 and qEV.n == 64

            def p15a_pe(t):
                t.wait_ge(qWL.h, 256)
                for qw in range(NQC):
                    t.wait_ge(qX.h, 16 * 16 * (qw + 1))
                    for h in range(H):
                        idx = qw * 16 + h
                        if idx >= 2:
                            t.wait_ge(qEV.h, idx - 1)
                        for fc in range(NFC):
                            ins = t.matmul(
                                q_ps[(qw * 16 + h) % 3][:],
                                wq_sb[:, fc * D + h * 128: fc * D + (h + 1) * 128],
                                xnqt[qw % 2][:, fc * 512:(fc + 1) * 512],
                                start=(fc == 0), stop=(fc == NFC - 1))
                        qMM.inc(ins)
            block.tensor(p15a_pe)
            assert qMM.n == 64

            def p15a_sy(sy):
                for qw in range(NQC):
                    for h in range(H):
                        idx = qw * 16 + h
                        sy.wait_ge(qEV.h, idx + 1)
                        qWRp[idx % 2].inc(sy.dma_start(
                            out=q_c[h][:, qw * 512:(qw + 1) * 512],
                            in_=qev[h % 2][:]), 16)
            block.sync(p15a_sy)
            assert qWRp[0].n + qWRp[1].n == 16 * 64

            barrier()

        with ExitStack() as c3:
            def s3b(name, shape, dt):
                return c3.enter_context(nc.sbuf_tensor(name, list(shape), dt))
            wk_sb = s3b("wk_sb", [128, NFC * D], BF16)
            wv_sb = s3b("wv_sb", [128, NFC * D], BF16)
            xnt = [s3b("xnt0", [128, NFC * 512], BF16),
                   s3b("xnt1", [128, NFC * 512], BF16)]
            kev = [s3b("kev0", [128, 512], BF16), s3b("kev1", [128, 512], BF16)]
            vev = [s3b("vev0", [128, 512], BF16), s3b("vev1", [128, 512], BF16)]
            k_ps = [c3.enter_context(nc.psum_tensor("k_ps%d" % i, [128, 512], F32))
                    for i in range(3)]
            v_ps = [c3.enter_context(nc.psum_tensor("v_ps%d" % i, [128, 512], F32))
                    for i in range(3)]

            def p15b_gp(g):
                for fc in range(NFC):
                    qWL.inc(g.dma_start(
                        out=wk_sb[:, fc * D:(fc + 1) * D],
                        in_=wk_d[fc * 128:(fc + 1) * 128, :]), 16)
                for fc in range(NFC):
                    qWL.inc(g.dma_start(
                        out=wv_sb[:, fc * D:(fc + 1) * D],
                        in_=wv_d[fc * 128:(fc + 1) * 128, :]), 16)
            block.gpsimd(p15b_gp)
            assert qWL.n == 256 * 3

            def p15b_sc(s):
                for w in range(9):
                    if w < 8:
                        if w >= 2:
                            s.wait_ge(kKmm.h, 16 * (w - 1))
                            s.wait_ge(kVmm.h, 16 * (w - 1))
                        for fc in range(NFC):
                            kX.inc(s.dma_start_transpose(
                                xnt[w % 2][:, fc * 512:(fc + 1) * 512],
                                xn_c[fc, w * 512:(w + 1) * 512, :]), 16)
                    if w >= 1:
                        wp = w - 1
                        for h in range(H):
                            idx = wp * 16 + h
                            s.wait_ge(kKmm.h, idx + 1)
                            if idx >= 2:
                                s.wait_ge(kWRp[idx % 2].h, 16 * (idx // 2))
                            kKev.inc(s.activation(kev[h % 2][:], k_ps[idx % 3][:],
                                                  AF.Identity,
                                                  bias=bias_sb[:, 16 + h:17 + h]))
                        for u in range(16):
                            idx = wp * 16 + u
                            s.wait_ge(kVmm.h, idx + 1)
                            if idx >= 2:
                                s.wait_ge(vWRp[idx % 2].h, 16 * (idx // 2))
                            kVev.inc(s.activation(vev[u % 2][:], v_ps[idx % 3][:],
                                                  AF.Identity))
            block.scalar(p15b_sc)
            assert kX.n == 16 * 16 * 8 and kKev.n == 128 and kVev.n == 128

            def p15b_pe(t):
                t.wait_ge(qWL.h, 256 * 3)
                for w in range(8):
                    t.wait_ge(kX.h, 16 * 16 * (w + 1))
                    for h in range(H):
                        idx = w * 16 + h
                        if idx >= 2:
                            t.wait_ge(kKev.h, idx - 1)
                        for fc in range(NFC):
                            ins = t.matmul(
                                k_ps[(w * 16 + h) % 3][:],
                                wk_sb[:, fc * D + h * 128: fc * D + (h + 1) * 128],
                                xnt[w % 2][:, fc * 512:(fc + 1) * 512],
                                start=(fc == 0), stop=(fc == NFC - 1))
                        kKmm.inc(ins)
                    for hg in range(4):
                        for sub in range(4):
                            u = hg * 4 + sub
                            idx = w * 16 + u
                            if idx >= 2:
                                t.wait_ge(kVev.h, idx - 1)
                            for fc in range(NFC):
                                ins = t.matmul(
                                    v_ps[(w * 16 + u) % 3][:],
                                    xnt[w % 2][:, fc * 512 + sub * 128: fc * 512 + (sub + 1) * 128],
                                    wv_sb[:, fc * D + hg * 512: fc * D + (hg + 1) * 512],
                                    start=(fc == 0), stop=(fc == NFC - 1))
                            kVmm.inc(ins)
            block.tensor(p15b_pe)
            assert kKmm.n == 128 and kVmm.n == 128

            def p15b_sy(sy):
                for w in range(8):
                    for h in range(H):
                        idx = w * 16 + h
                        sy.wait_ge(kKev.h, idx + 1)
                        kWRp[idx % 2].inc(sy.dma_start(
                            out=k_c[h][:, w * 512:(w + 1) * 512],
                            in_=kev[h % 2][:]), 16)
                    for hg in range(4):
                        for sub in range(4):
                            u = hg * 4 + sub
                            idx = w * 16 + u
                            sy.wait_ge(kVev.h, idx + 1)
                            vWRp[idx % 2].inc(sy.dma_start(
                                out=v_c[hg, w, sub],
                                in_=vev[u % 2][:]), 16)
            block.sync(p15b_sy)
            assert kWRp[0].n + kWRp[1].n == 16 * 128
            assert vWRp[0].n + vWRp[1].n == 16 * 128

            barrier()



        def EX(qc):
            return 8 * (qc + 1)

        TOT_KT = H * sum(EX(qc) for qc in range(NQC))

        with ExitStack() as c4a:
            avcat = c4a.enter_context(nc.sbuf_tensor("avcat", [128, H * NOWN], BF16))
            c4 = c4a.enter_context(ExitStack())
            def s4b(name, shape, dt):
                return c4.enter_context(nc.sbuf_tensor(name, list(shape), dt))
            mask_sb = s4b("mask_sb", [128, 8 * 512], BF16)
            k_fm = [s4b("k_fm%d" % i, [128, S], BF16) for i in range(4)]
            v_sb = [s4b("v_sb%d" % i, [128, S], BF16) for i in range(4)]
            q_fm = [s4b("q_fm%d" % i, [128, NOWN], BF16) for i in range(4)]
            at = [s4b("at%d" % i, [128, 1024], BF16) for i in range(3)]
            atm = [s4b("atm%d" % i, [128, 1024], BF16) for i in range(2)]
            s0m2 = [s4b("s0m2_%d" % i, [128, 1], F32) for i in range(2)]
            e02 = [s4b("e02_%d" % i, [128, 1], BF16) for i in range(2)]
            mu_bf2 = [s4b("mu_bf2_%d" % i, [1, 1], BF16) for i in range(2)]
            negmu2 = [s4b("negmu2_%d" % i, [128, 1], F32) for i in range(2)]
            linv2 = [s4b("linv2_%d" % i, [1, 512], F32) for i in range(2)]
            lbc2 = [s4b("lbc2_%d" % i, [128, 512], F32) for i in range(2)]
            st_ps = [c4.enter_context(nc.psum_tensor("st_ps%d" % i, [128, 1024], F32))
                     for i in range(2)]
            av_ps2 = [c4.enter_context(nc.psum_tensor("av_ps%d" % i, [128, 512], F32))
                      for i in range(2)]
            lp_ps = [c4.enter_context(nc.psum_tensor("lp_ps%d" % i, [128, 512], F32))
                     for i in range(2)]

            cbuf = [s4b("cb0", [128, DFF], BF16), s4b("cb1", [128, DFF], BF16)]

            def p2_cast(g):
                ci = 0
                for wsrc, wdst in ((wg_d, wgb_c), (wu_d, wub_c)):
                    for fc in range(NFC):
                        if ci >= 2:
                            g.wait_ge(cWR.h, 16 * (ci - 1))
                        cLD.inc(g.dma_start(out=cbuf[ci % 2][:],
                                            in_=wsrc[fc * 128:(fc + 1) * 128, :]), 16)
                        g.wait_ge(cLD.h, 16 * (ci + 1))
                        cWR.inc(g.dma_start(
                            out=wdst[:, :, fc * 128:(fc + 1) * 128].rearrange(
                                "d p f -> p d f"),
                            in_=cbuf[ci % 2].ap().rearrange("p (d f) -> p d f", f=128)), 16)
                        ci += 1
                for dc in range(NDC):
                    if ci >= 2:
                        g.wait_ge(cWR.h, 16 * (ci - 1))
                    cLD.inc(g.dma_start(out=cbuf[ci % 2][:, 0:D],
                                        in_=wd_d[dc * 128:(dc + 1) * 128, :]), 16)
                    g.wait_ge(cLD.h, 16 * (ci + 1))
                    cWR.inc(g.dma_start(
                        out=wdb_c[dc].rearrange("q p f -> p q f"),
                        in_=cbuf[ci % 2][:, 0:D].rearrange("p (q f) -> p q f", f=512)), 16)
                    ci += 1
            block.gpsimd(p2_cast)
            CWR_ALL = cWR.n
            assert CWR_ALL == 16 * (2 * NFC + NDC)

            def p2_sy(sy):
                sy.dma_start(out=mask_sb.ap().rearrange("p (d j) -> p d j", j=512),
                             in_=mq.rearrange("d p j -> p d j")).then_inc(sCON.h, 16)
                for P in range(H // 2):
                    if P == 0:
                        for sc_ in (qWRp[0], qWRp[1], kWRp[0], kWRp[1], vWRp[0], vWRp[1]):
                            sy.wait_ge(sc_.h, sc_.n)
                    if P >= 2:
                        sy.wait_ge(aLB.h, 8 * (P - 1))
                    for h in (2 * P, 2 * P + 1):
                        ln4 = h % 4
                        hg, hh = h // 4, h % 4
                        aLDp[P % 2].inc(sy.dma_start(out=k_fm[ln4][:], in_=k_c[h]), 16)
                        aLDp[P % 2].inc(sy.dma_start(
                            out=v_sb[ln4][:],
                            in_=v_c[hg, :, :, :, hh * 128:(hh + 1) * 128].rearrange(
                                "w s p d -> p (w s) d")), 16)
                        aLDp[P % 2].inc(sy.dma_start(out=q_fm[ln4][:], in_=q_c[h]), 16)
            block.sync(p2_sy)
            sCON.n += 16
            SCON_MASK = sCON.n
            assert aLDp[0].n + aLDp[1].n == 48 * H

            def p2_pe(t):
                nst = nexp = nmsk = nkt = nlb = nlinv = navd = 0
                nE0 = nMUV = nMU = 0
                for P in range(H // 2):
                    heads = [(2 * P + i, (2 * P + i) % 4, i) for i in range(2)]
                    t.wait_ge(aLDp[P % 2].h, 96 * (P // 2 + 1))
                    t.wait_ge(aEXP.h, nexp)
                    for h, ln4, ln in heads:
                        aS0.inc(t.matmul(st_ps[ln][:, 0:128], q_fm[ln4][:, 0:128],
                                         k_fm[ln4][:, 0:128], start=True, stop=True))
                    t.wait_ge(aLINV.h, nlinv)
                    for h, ln4, ln in heads:
                        nE0 += 1
                        t.wait_ge(aE0.h, nE0)
                        aMUP.inc(t.matmul(lp_ps[ln][0:1, 0:1], ones_col[:], e02[ln][:],
                                          start=True, stop=True))
                    for h, ln4, ln in heads:
                        nMUV += 1
                        t.wait_ge(aMUV.h, nMUV)
                        aNBP.inc(t.matmul(st_ps[ln][:, 0:1], ones_row[:], mu_bf2[ln][:],
                                          start=True, stop=True))
                    t.wait_ge(aMU.h, 2 * (P + 1))
                    for qc in range(NQC):
                        ex = EX(qc)
                        pq = ex // 2
                        if navd:
                            t.wait_ge(aAVD.h, navd)

                        def emit_st(j, ln, ln4, qc=qc):
                            ps = st_ps[ln]
                            for half in range(2):
                                kt = 2 * j + half
                                ins = t.matmul(
                                    ps[:, half * 512:(half + 1) * 512],
                                    k_fm[ln4][:, kt * 128:(kt + 1) * 128],
                                    q_fm[ln4][:, qc * 512:(qc + 1) * 512],
                                    start=True, stop=True)
                            aST.inc(ins)

                        for h, ln4, ln in heads:
                            emit_st(0, ln, ln4)
                        for j in range(pq):
                            for h, ln4, ln in heads:
                                nexp += 1
                                d0 = 2 * j - (ex - 8)
                                if d0 >= 0:
                                    nmsk += 1
                                    t.wait_ge(aMSK.h, nmsk)
                                    src_t = atm[nmsk % 2]
                                else:
                                    t.wait_ge(aEXP.h, nexp)
                                    src_t = at[nexp % 3]
                                for half in range(2):
                                    kt = 2 * j + half
                                    t.matmul(av_ps2[ln][:],
                                             v_sb[ln4][:, kt * 128:(kt + 1) * 128],
                                             src_t[:, half * 512:(half + 1) * 512],
                                             start=(kt == 0), stop=(kt == ex - 1))
                                    ins = t.matmul(
                                        lp_ps[ln][0:1, :], ones_col[:],
                                        src_t[:, half * 512:(half + 1) * 512],
                                        start=(kt == 0), stop=(kt == ex - 1))
                                    aKT.inc(ins)
                                if j + 1 < pq:
                                    emit_st(j + 1, ln, ln4)
                        nkt += 2 * ex
                        for h, ln4, ln in heads:
                            nlinv += 1
                            t.wait_ge(aLINV.h, nlinv)
                            aLB.inc(t.matmul(lp_ps[ln][:, :], ones_row_f[:],
                                             linv2[ln][:], start=True, stop=True))
                        nlb += 2
                        navd += 2
                assert nkt == TOT_KT and nexp == TOT_KT // 2
            block.tensor(p2_pe)
            assert aST.n == TOT_KT // 2 and aKT.n == TOT_KT
            assert aLB.n == 2 * H * NQC // 2

            def p2_act(s):
                nst = nexp = nkt_c = nlb = 0
                for P in range(H // 2):
                    heads = [(2 * P + i, (2 * P + i) % 4, i) for i in range(2)]
                    for h, ln4, ln in heads:
                        s.wait_ge(aRM.h, 2 * P + ln + 1)
                        aE0.inc(s.activation(e02[ln][:], s0m2[ln][:], AF.Exp))
                    for h, ln4, ln in heads:
                        s.wait_ge(aMUP.h, 2 * P + ln + 1)
                        aMUV.inc(s.activation(mu_bf2[ln][:], lp_ps[ln][0:1, 0:1], AF.Ln))
                    for h, ln4, ln in heads:
                        s.wait_ge(aNBP.h, 2 * P + ln + 1)
                        aMU.inc(s.activation(negmu2[ln][:], st_ps[ln][:, 0:1],
                                             AF.Identity, scale=-1.0))
                    s.wait_ge(aMU.h, 2 * (P + 1))
                    for qc in range(NQC):
                        ex = EX(qc)
                        pq = ex // 2
                        for j in range(pq):
                            for h, ln4, ln in heads:
                                nst += 1
                                nexp += 1
                                s.wait_ge(aST.h, nst)
                                if nexp >= 4:
                                    s.wait_ge(aKT.h, 2 * (nexp - 3))
                                aEXP.inc(s.activation(at[nexp % 3][:], st_ps[ln][:],
                                                      AF.Exp, bias=negmu2[ln][:]))
                        for h, ln4, ln in heads:
                            nlb += 1
                            s.wait_ge(aLB.h, nlb)
                            aLBC.inc(s.copy(lbc2[ln][:], lp_ps[ln][:, :]))
            block.scalar(p2_act)
            assert aEXP.n == TOT_KT // 2

            def p2_vec(v):
                v.wait_ge(sCON.h, SCON_MASK)
                nexp = nmsk = nkt = nlinv = nlbc = navd = 0
                for P in range(H // 2):
                    heads = [(2 * P + i, (2 * P + i) % 4, i) for i in range(2)]
                    for h, ln4, ln in heads:
                        v.wait_ge(aS0.h, 2 * P + ln + 1)
                        aRM.inc(v.reduce_max(s0m2[ln][:], st_ps[ln][:, 0:128],
                                             axis=AX.X))
                    for qc in range(NQC):
                        ex = EX(qc)
                        pq = ex // 2
                        for j in range(pq):
                            for h, ln4, ln in heads:
                                nexp += 1
                                d0 = 2 * j - (ex - 8)
                                if d0 >= 0:
                                    v.wait_ge(aEXP.h, nexp)
                                    if nmsk >= 2:
                                        v.wait_ge(aKT.h, 2 * nexp - 4)
                                    nmsk += 1
                                    aMSK.inc(v.tensor_mul(
                                        atm[nmsk % 2][:], at[nexp % 3][:],
                                        mask_sb[:, d0 * 512:(d0 + 2) * 512]))
                        nkt += 2 * ex
                        for h, ln4, ln in heads:
                            v.wait_ge(aKT.h, nkt - 2 if ln == 0 else nkt)
                            nlinv += 1
                            aLINV.inc(v.reciprocal(linv2[ln][:], lp_ps[ln][0:1, :]))
                        for h, ln4, ln in heads:
                            nlbc += 1
                            v.wait_ge(aLBC.h, nlbc)
                            pDg.inc(v.tensor_mul(avn[:], av_ps2[ln][:], lbc2[ln][:]))
                            v.wait_ge(pDg.h, pDg.n)
                            h_abs = 2 * P + ln
                            navd += 1
                            aAVD.inc(v.tensor_scalar(
                                avcat[:, h_abs * NOWN + qc * 512: h_abs * NOWN + (qc + 1) * 512],
                                avn[:], bias_sb[:, 32 + h_abs:33 + h_abs], None,
                                ALU.add))
            block.vector(p2_vec)

            barrier()
            c4.close()

            with ExitStack() as c5:
                def s5b(name, shape, dt):
                    return c5.enter_context(nc.sbuf_tensor(name, list(shape), dt))
                wo_sb = s5b("wo_sb", [128, NFC * D], BF16)
                bo_bc = s5b("bo_bc", [128, D], F32)
                xq_t = [s5b("xq_t0", [128, D], F32), s5b("xq_t1", [128, D], F32)]
                r1t = [s5b("r1t0", [128, D], F32), s5b("r1t1", [128, D], F32)]
                xn2t = [s5b("xn2t0", [128, D], BF16), s5b("xn2t1", [128, D], BF16)]
                w3sq = s5b("w3sq", [128, D], F32)
                wo_ps = [c5.enter_context(nc.psum_tensor("wo_ps%d" % i, [128, 512], F32))
                         for i in range(4)]

                def p3_gp(g):
                    wLD.inc(g.dma_start(out=bo_bc[:],
                                        in_=bo_d[None, :].broadcast_to([128, D])), 16)
                    for hc in range(NFC):
                        wLD.inc(g.dma_start(
                            out=wo_sb[:, hc * D:(hc + 1) * D],
                            in_=wo_d[hc * 128:(hc + 1) * 128, :]), 16)
                block.gpsimd(p3_gp)
                assert wLD.n == 16 * 17

                def p3_sy(sy):
                    for tt in range(NT_OWN):
                        if tt >= 2:
                            sy.wait_ge(wEVA.h, 4 * (tt - 1))
                        wXQp[tt % 2].inc(sy.dma_start(
                            out=xq_t[tt % 2][:],
                            in_=xq[tt * 128:(tt + 1) * 128, :]), 16)
                        if tt >= 1:
                            sy.wait_ge(wAPL.h, tt)
                            wWRp[(tt - 1) % 2].inc(sy.dma_start(
                                out=r1_c[(tt - 1) * 128:tt * 128, :],
                                in_=r1t[(tt - 1) % 2][:]), 16)
                            wWRp[(tt - 1) % 2].inc(sy.dma_start(
                                out=xn2_c[:, (tt - 1) * 128:tt * 128, :].rearrange(
                                    "c t f -> t c f"),
                                in_=xn2t[(tt - 1) % 2].ap().rearrange(
                                    "p (c f) -> p c f", f=128)), 16)
                    sy.wait_ge(wAPL.h, NT_OWN)
                    wWRp[(NT_OWN - 1) % 2].inc(sy.dma_start(
                        out=r1_c[(NT_OWN - 1) * 128:NT_OWN * 128, :],
                        in_=r1t[(NT_OWN - 1) % 2][:]), 16)
                    wWRp[(NT_OWN - 1) % 2].inc(sy.dma_start(
                        out=xn2_c[:, (NT_OWN - 1) * 128:NT_OWN * 128, :].rearrange(
                            "c t f -> t c f"),
                        in_=xn2t[(NT_OWN - 1) % 2].ap().rearrange(
                            "p (c f) -> p c f", f=128)), 16)
                block.sync(p3_sy)
                assert wWRp[0].n + wWRp[1].n == 32 * NT_OWN

                def p3_pe(t):
                    t.wait_ge(wLD.h, 16 * 17)
                    t.wait_ge(aAVD.h, H * NQC)
                    for tt in range(NT_OWN):
                        for c4i in range(4):
                            if tt >= 1:
                                t.wait_ge(wEVA.h, 4 * (tt - 1) + c4i + 1)
                            for hc in range(NFC):
                                ins = t.matmul(
                                    wo_ps[c4i][:],
                                    avcat[:, hc * NOWN + tt * 128: hc * NOWN + (tt + 1) * 128],
                                    wo_sb[:, hc * D + c4i * 512: hc * D + (c4i + 1) * 512],
                                    start=(hc == 0), stop=(hc == NFC - 1))
                            wMM.inc(ins)
                block.tensor(p3_pe)
                assert wMM.n == 4 * NT_OWN

                def p3_act(s):
                    for tt in range(NT_OWN):
                        p = tt % 2
                        s.wait_ge(wEVA.h, 4 * (tt + 1))
                        wSq1.inc(s.activation(w3sq[:], r1t[p][:], AF.Square,
                                              accum_out=ssq1[:]))
                        s.wait_ge(wD1.h, tt + 1)
                        wS1.inc(s.activation(s1v[:], r1c[:], AF.Sqrt))
                block.scalar(p3_act)
                assert wSq1.n == NT_OWN and wS1.n == NT_OWN

                def p3_vec(v):
                    def g(ins):
                        wDg.inc(ins)
                        v.wait_ge(wDg.h, wDg.n)
                        return ins
                    for tt in range(NT_OWN):
                        p = tt % 2
                        v.wait_ge(wXQp[p].h, 16 * (tt // 2 + 1))
                        if tt >= 2:
                            v.wait_ge(wWRp[tt % 2].h, 32 * (tt // 2))
                        for c4i in range(4):
                            v.wait_ge(wMM.h, 4 * tt + c4i + 1)
                            sl = slice(c4i * 512, (c4i + 1) * 512)
                            g(v.tensor_add(tmp512[:], wo_ps[c4i][:], xq_t[p][:, sl]))
                            wEVA.inc(v.tensor_add(r1t[p][:, sl], tmp512[:],
                                                  bo_bc[:, sl]))
                        v.wait_ge(wEVA.h, 4 * (tt + 1))
                        g(v.reduce_sum(sx1[:], r1t[p][:], axis=AX.X))
                        g(v.tensor_scalar_mul(m1v[:], sx1[:], 1.0 / D))
                        v.wait_ge(wSq1.h, tt + 1)
                        g(v.tensor_scalar_mul(t1v[:], ssq1[:], 1.0 / D))
                        g(v.tensor_mul(msq[:], m1v[:], m1v[:]))
                        g(v.tensor_sub(v1v[:], t1v[:], msq[:]))
                        g(v.tensor_scalar_add(vpe[:], v1v[:], EPS))
                        wD1.inc(v.reciprocal(r1c[:], vpe[:]))
                        v.wait_ge(wS1.h, tt + 1)
                        g(v.tensor_mul(nm1[:], m1v[:], s1v[:]))
                        g(v.tensor_scalar_mul(nm1n[:], nm1[:], -1.0))
                        wAPL.inc(v.tensor_scalar(xn2t[p][:], r1t[p][:], s1v[:],
                                                 nm1n[:], ALU.mult, ALU.add))
                block.vector(p3_vec)
                assert wAPL.n == NT_OWN and wEVA.n == 4 * NT_OWN

                barrier()


        with ExitStack() as c6:
            def s6b(name, shape, dt):
                return c6.enter_context(nc.sbuf_tensor(name, list(shape), dt))
            xnt2 = s6b("xnt2", [128, NFC * FFN_W], BF16)
            hid = s6b("hid", [128, NDC * FFN_W], BF16)
            wgu = [s6b("wg0", [128, NFC * 128], BF16), s6b("wg1", [128, NFC * 128], BF16),
                   s6b("wu0", [128, NFC * 128], BF16), s6b("wu1", [128, NFC * 128], BF16)]
            sbg4 = [s6b("sbg4_0", [8, 512], BF16), s6b("sbg4_1", [8, 512], BF16)]
            sbu4 = [s6b("sbu4_0", [8, 512], BF16), s6b("sbu4_1", [8, 512], BF16)]
            bcg4 = [s6b("bcg4_0", [64, 512], BF16), s6b("bcg4_1", [64, 512], BF16)]
            bcu4 = [s6b("bcu4_0", [64, 512], BF16), s6b("bcu4_1", [64, 512], BF16)]
            wd_t = [s6b("wd0", [128, 512], BF16), s6b("wd1", [128, 512], BF16),
                    s6b("wd2", [128, 512], BF16)]
            gs_sb = s6b("gs_sb", [128, FFN_W], BF16)
            wr_sb = s6b("wr_sb", [128, NFC * E8], BF16)
            sag_sb = s6b("sag_sb", [128, NFC * RK], BF16)
            sau_sb = s6b("sau_sb", [128, NFC * RK], BF16)
            acg_sb = s6b("acg_sb", [128, NFC * ER], BF16)
            acu_sb = s6b("acu_sb", [128, NFC * ER], BF16)
            sad_sb = s6b("sad_sb", [128, NDC * RK], BF16)
            ad_sb = s6b("ad_sb", [128, NDC * ER], BF16)
            sbd_sb = s6b("sbd_sb", [8, D], BF16)
            bcd_sb = s6b("bcd_sb", [64, D], BF16)
            ident_sb = s6b("ident_sb", [128, 128], F32)
            rself_sb = s6b("rself_sb", [8, 64], F32)
            rsel_sb = s6b("rsel_sb", [8, 64], BF16)
            sg_sb = s6b("sg_sb", [128, E8], F32)
            bi_sb = s6b("bi_sb", [128, E8], F32)
            mk1 = s6b("mk1", [128, E8], F32)
            b2t = s6b("b2t", [128, E8], F32)
            mk2 = s6b("mk2", [128, E8], F32)
            mkt = s6b("mkt", [128, E8], F32)
            wsc = s6b("wsc", [128, E8], F32)
            m1e = s6b("m1e", [128, 1], F32)
            m2e = s6b("m2e", [128, 1], F32)
            sse = s6b("sse", [128, 1], F32)
            sie = s6b("sie", [128, 1], F32)
            w_tok = [s6b("w_tok0", [128, E8], F32), s6b("w_tok1", [128, E8], F32)]
            wt_sb = s6b("wt_sb", [8, FFN_W], F32)
            wbc_sb = s6b("wbc_sb", [64, FFN_W], F32)
            xawg_sb = s6b("xawg_sb", [64, FFN_W], BF16)
            xawu_sb = s6b("xawu_sb", [64, FFN_W], BF16)
            sxg_sb = s6b("sxg_sb", [8, FFN_W], BF16)
            sxu_sb = s6b("sxu_sb", [8, FFN_W], BF16)
            sxd_sb = s6b("sxd_sb", [8, FFN_W], BF16)
            hxw_sb = s6b("hxw_sb", [64, FFN_W], BF16)
            r1r = [s6b("r1r0", [128, 512], F32), s6b("r1r1", [128, 512], F32)]
            o_sb = [s6b("o_sb0", [128, 512], F32), s6b("o_sb1", [128, 512], F32)]

            def p4_const(g):
                g.dma_start(out=ident_sb[:], in_=ident_c.ap()).then_inc(fLD.h, 16)
                g.dma_start(out=rself_sb[:], in_=rsel_c.ap()).then_inc(fLD.h, 16)
                g.dma_start(out=wr_sb.ap().rearrange("p (c e) -> p c e", e=E8),
                            in_=wr_d.rearrange("(c p) e -> p c e", p=128)).then_inc(fLD.h, 16)
                g.dma_start(out=sag_sb.ap().rearrange("p (c r) -> p c r", r=RK),
                            in_=sag_d.rearrange("(c p) r -> p c r", p=128)).then_inc(fLD.h, 16)
                g.dma_start(out=sau_sb.ap().rearrange("p (c r) -> p c r", r=RK),
                            in_=sau_d.rearrange("(c p) r -> p c r", p=128)).then_inc(fLD.h, 16)
                for e in range(E8):
                    g.dma_start(
                        out=acg_sb.ap().rearrange("p (c e r) -> p c e r", e=E8, r=RK)[:, :, e, :],
                        in_=ag_d[e].rearrange("(c p) r -> p c r", p=128)).then_inc(fLD.h, 16)
                    g.dma_start(
                        out=acu_sb.ap().rearrange("p (c e r) -> p c e r", e=E8, r=RK)[:, :, e, :],
                        in_=au_d[e].rearrange("(c p) r -> p c r", p=128)).then_inc(fLD.h, 16)
                    g.dma_start(
                        out=ad_sb.ap().rearrange("p (c e r) -> p c e r", e=E8, r=RK)[:, :, e, :],
                        in_=ad_d[e].rearrange("(c p) r -> p c r", p=128)).then_inc(fLD.h, 16)
                g.dma_start(out=sad_sb.ap().rearrange("p (c r) -> p c r", r=RK),
                            in_=sad_d.rearrange("(c p) r -> p c r", p=128)).then_inc(fLD.h, 16)
                g.dma_start(out=sbd_sb[:], in_=sbd_d).then_inc(fLD.h, 16)
                g.dma_start(out=bcd_sb[:],
                            in_=bd_d.rearrange("e r d -> (e r) d")).then_inc(fLD.h, 16)
            block.gpsimd(p4_const)
            fLD.n += 16 * 32

            def p4_vec0(v):
                v.wait_ge(fLD.h, 32)
                fLD.inc(v.tensor_copy(rsel_sb[:], rself_sb[:]))
            block.vector(p4_vec0)
            FLD_ALL = fLD.n

            NI = FFN_W // 128

            for W in range(NWIN):
                with ExitStack() as cp:
                    rl_ps = cp.enter_context(nc.psum_tensor("rl_ps_w%d" % W, [128, 512], F32))
                    wt_ps = cp.enter_context(nc.psum_tensor("wt_ps_w%d" % W, [8, 512], F32))
                    wb_ps = cp.enter_context(nc.psum_tensor("wb_ps_w%d" % W, [64, FFN_W], F32))
                    xa_ps = cp.enter_context(nc.psum_tensor("xa_ps_w%d" % W, [64, FFN_W], F32))
                    sx_ps = cp.enter_context(nc.psum_tensor("sx_ps_w%d" % W, [8, FFN_W], F32))

                    fX2_0 = fX2.n
                    fRL0, fSG0, fTOP0, fWT0, fWTE0 = fRL.n, fSG.n, fTOP.n, fWT.n, fWTE.n
                    fWB0, fWBE0, fXA0, fXAW0, fSX0, fSXE0 = (
                        fWB.n, fWBE.n, fXA.n, fXAW.n, fSX.n, fSXE.n)

                    def p4a_sc(s, W=W, fX2_0=fX2_0):
                        if W == 0:
                            s.wait_ge(wWRp[0].h, wWRp[0].n)
                            s.wait_ge(wWRp[1].h, wWRp[1].n)
                        else:
                            s.wait_ge(fGMM.h, 86 * W)
                            s.wait_ge(fUMM.h, 86 * W)
                        for fc in range(NFC):
                            fX2.inc(s.dma_start_transpose(
                                xnt2[:, fc * FFN_W:(fc + 1) * FFN_W],
                                xn2_c[fc, W * FFN_W:(W + 1) * FFN_W, :]), 16)
                    block.scalar(p4a_sc)

                    def p4b_pe(t, W=W, fX2_0=fX2_0, fSG0=fSG0, fTOP0=fTOP0,
                              fWTE0=fWTE0):
                        t.wait_ge(fLD.h, FLD_ALL)
                        if W >= 1:
                            t.wait_ge(fOEV.h, 32 * W)
                        t.wait_ge(fX2.h, fX2_0 + 16 * 16)
                        for i in range(NI):
                            if i >= 1:
                                t.wait_ge(fSG.h, fSG0 + i)
                            for fc in range(NFC):
                                ins = t.matmul(
                                    rl_ps[:, 0:E8],
                                    xnt2[:, fc * FFN_W + i * 128: fc * FFN_W + (i + 1) * 128],
                                    wr_sb[:, fc * E8:(fc + 1) * E8],
                                    start=(fc == 0), stop=(fc == NFC - 1))
                            fRL.inc(ins)
                            t.wait_ge(fTOP.h, fTOP0 + i + 1)
                            if i >= 1:
                                t.wait_ge(fWTE.h, fWTE0 + i)
                            fWT.inc(t.transpose(wt_ps[:, 0:128], w_tok[i % 2][:],
                                                ident_sb[:]))
                    block.tensor(p4b_pe)

                    def p4b_act(s, fRL0=fRL0, fWT0=fWT0, fTOP0=fTOP0):
                        for i in range(NI):
                            if i >= 1:
                                s.wait_ge(fTOP.h, fTOP0 + i)
                            s.wait_ge(fRL.h, fRL0 + i + 1)
                            fSG.inc(s.activation(sg_sb[:], rl_ps[:, 0:E8], AF.Sigmoid))
                            s.wait_ge(fWT.h, fWT0 + i + 1)
                            fWTE.inc(s.copy(wt_sb[:, i * 128:(i + 1) * 128],
                                            wt_ps[:, 0:128]))
                    block.scalar(p4b_act)

                    def p4b_vec(v, fSG0=fSG0, fWT0=fWT0):
                        v.wait_ge(sCON.h, 32)
                        for i in range(NI):
                            v.wait_ge(fSG.h, fSG0 + i + 1)
                            if i >= 2:
                                v.wait_ge(fWT.h, fWT0 + i - 1)
                            def g(ins):
                                fDg.inc(ins)
                                v.wait_ge(fDg.h, fDg.n)
                                return ins
                            g(v.tensor_add(bi_sb[:], sg_sb[:], ebias_sb[:]))
                            g(v.reduce_max(m1e[:], bi_sb[:], axis=AX.X))
                            g(v.tensor_scalar(mk1[:], bi_sb[:], m1e[:], None, ALU.is_ge))
                            g(v.scalar_tensor_tensor(b2t[:], mk1[:], -1e30, bi_sb[:],
                                                     ALU.mult, ALU.add))
                            g(v.reduce_max(m2e[:], b2t[:], axis=AX.X))
                            g(v.tensor_scalar(mk2[:], b2t[:], m2e[:], None, ALU.is_ge))
                            g(v.tensor_add(mkt[:], mk1[:], mk2[:]))
                            g(v.tensor_mul(wsc[:], mkt[:], sg_sb[:]))
                            g(v.reduce_sum(sse[:], wsc[:], axis=AX.X))
                            g(v.tensor_scalar_add(sse[:], sse[:], 1e-8))
                            g(v.reciprocal(sie[:], sse[:]))
                            fTOP.inc(v.tensor_scalar(w_tok[i % 2][:], wsc[:], sie[:],
                                                     None, ALU.mult))
                    block.vector(p4b_vec)

                    def p4c_pe(t, fWTE0=fWTE0, fXAW0=fXAW0, fSXE0=fSXE0):
                        t.wait_ge(fWTE.h, fWTE0 + NI)
                        for half in range(2):
                            fWB.inc(t.matmul(wb_ps[:, half * 512:(half + 1) * 512],
                                             rself_sb[:],
                                             wt_sb[:, half * 512:(half + 1) * 512],
                                             start=True, stop=True))
                        for gi, asb in enumerate((acg_sb, acu_sb)):
                            if gi == 1:
                                t.wait_ge(fXAW.h, fXAW0 + 2)
                            for half in range(2):
                                for fc in range(NFC):
                                    ins = t.matmul(
                                        xa_ps[:, half * 512:(half + 1) * 512],
                                        asb[:, fc * ER:(fc + 1) * ER],
                                        xnt2[:, fc * FFN_W + half * 512: fc * FFN_W + half * 512 + 512],
                                        start=(fc == 0), stop=(fc == NFC - 1))
                                fXA.inc(ins)
                        for gi, ssb in enumerate((sag_sb, sau_sb)):
                            if gi == 1:
                                t.wait_ge(fSXE.h, fSXE0 + 2)
                            for half in range(2):
                                for fc in range(NFC):
                                    ins = t.matmul(
                                        sx_ps[:, half * 512:(half + 1) * 512],
                                        ssb[:, fc * RK:(fc + 1) * RK],
                                        xnt2[:, fc * FFN_W + half * 512: fc * FFN_W + half * 512 + 512],
                                        start=(fc == 0), stop=(fc == NFC - 1))
                                fSX.inc(ins)
                    block.tensor(p4c_pe)

                    def p4c_act(s, fWB0=fWB0, fSX0=fSX0):
                        s.wait_ge(fWB.h, fWB0 + 2)
                        fWBE.inc(s.copy(wbc_sb[:], wb_ps[:]))
                        for gi, dst in enumerate((sxg_sb, sxu_sb)):
                            for half in range(2):
                                s.wait_ge(fSX.h, fSX0 + gi * 2 + half + 1)
                                fSXE.inc(s.activation(
                                    dst[:, half * 512:(half + 1) * 512],
                                    sx_ps[:, half * 512:(half + 1) * 512],
                                    AF.Identity, scale=1.0 / RK))
                    block.scalar(p4c_act)

                    def p4c_vec(v, fWBE0=fWBE0, fXA0=fXA0):
                        v.wait_ge(fWBE.h, fWBE0 + 1)
                        for gi, dst in enumerate((xawg_sb, xawu_sb)):
                            for half in range(2):
                                v.wait_ge(fXA.h, fXA0 + gi * 2 + half + 1)
                                fXAW.inc(v.tensor_mul(
                                    dst[:, half * 512:(half + 1) * 512],
                                    xa_ps[:, half * 512:(half + 1) * 512],
                                    wbc_sb[:, half * 512:(half + 1) * 512]))
                    block.vector(p4c_vec)

                with ExitStack() as cq:
                    g_ps = cq.enter_context(nc.psum_tensor("g_ps_w%d" % W, [128, FFN_W], F32))
                    u_ps = cq.enter_context(nc.psum_tensor("u_ps_w%d" % W, [128, FFN_W], F32))
                    sxd_ps = cq.enter_context(nc.psum_tensor("sxd_ps_w%d" % W, [8, FFN_W], F32))
                    hxd_ps = cq.enter_context(nc.psum_tensor("hxd_ps_w%d" % W, [64, FFN_W], F32))

                    fGMM0, fUMM0, fSIL0, fUCP0, fHID0 = (
                        fGMM.n, fUMM.n, fSIL.n, fUCP.n, fHID.n)
                    fWGUp0 = [fWGUp[0].n, fWGUp[1].n]
                    fSBp0 = [fSBp[0].n, fSBp[1].n]

                    def p4d_gp(g, fGMM0=fGMM0, fUMM0=fUMM0, W=W):
                        for dc in range(NDC):
                            if dc >= 2:
                                tgt = fGMM0 + 2 * (dc - 1)
                            else:
                                tgt = fGMM0 if W > 0 else 0
                            if tgt > 0:
                                g.wait_ge(fGMM.h, tgt)
                                g.wait_ge(fUMM.h, tgt)
                            if W == 0 and dc == 0:
                                g.wait_ge(cWR.h, CWR_ALL)
                            if dc % 4 == 0:
                                dc4 = dc // 4
                                f0 = dc4 * 512
                                f1 = min(f0 + 512, DFF)
                                spb = fSBp[dc4 % 2]
                                spb.inc(g.dma_start(out=sbg4[dc4 % 2][:, 0:f1 - f0],
                                                    in_=sbg_d[:, f0:f1]), 16)
                                spb.inc(g.dma_start(out=sbu4[dc4 % 2][:, 0:f1 - f0],
                                                    in_=sbu_d[:, f0:f1]), 16)
                                spb.inc(g.dma_start(
                                    out=bcg4[dc4 % 2][:, 0:f1 - f0],
                                    in_=bg_d[:, :, f0:f1].rearrange("e r f -> (e r) f")), 16)
                                spb.inc(g.dma_start(
                                    out=bcu4[dc4 % 2][:, 0:f1 - f0],
                                    in_=bu_d[:, :, f0:f1].rearrange("e r f -> (e r) f")), 16)
                            sp = fWGUp[dc % 2]
                            sp.inc(g.dma_start(out=wgu[dc % 2][:], in_=wgb_c[dc]), 16)
                            sp.inc(g.dma_start(out=wgu[2 + dc % 2][:], in_=wub_c[dc]), 16)
                    block.gpsimd(p4d_gp)

                    def p4d_pe(t, W=W, fWGUp0=fWGUp0, fSBp0=fSBp0, fSIL0=fSIL0,
                              fUCP0=fUCP0, fHID0=fHID0, fXAW0=fXAW0, fSXE0=fSXE0):
                        t.wait_ge(fXAW.h, fXAW0 + 4)
                        t.wait_ge(fSXE.h, fSXE0 + 4)
                        for dc in range(NDC):
                            t.wait_ge(fWGUp[dc % 2].h, fWGUp0[dc % 2] + 32 * (dc // 2 + 1))
                            if dc % 4 == 0:
                                dc4 = dc // 4
                                t.wait_ge(fSBp[dc4 % 2].h,
                                          fSBp0[dc4 % 2] + 64 * (dc4 // 2 + 1))
                            for half in range(2):
                                if dc >= 1:
                                    t.wait_ge(fSIL.h, fSIL0 + 2 * (dc - 1) + half + 1)
                                sl = slice(half * 512, (half + 1) * 512)
                                for fc in range(NFC):
                                    t.matmul(g_ps[:, sl],
                                             wgu[dc % 2][:, fc * 128:(fc + 1) * 128],
                                             xnt2[:, fc * FFN_W + half * 512: fc * FFN_W + half * 512 + 512],
                                             start=(fc == 0), stop=False)
                                t.matmul(g_ps[:, sl],
                                         sbg4[(dc // 4) % 2][:, (dc % 4) * 128:(dc % 4 + 1) * 128],
                                         sxg_sb[:, sl], start=False, stop=False)
                                fGMM.inc(t.matmul(
                                    g_ps[:, sl],
                                    bcg4[(dc // 4) % 2][:, (dc % 4) * 128:(dc % 4 + 1) * 128],
                                    xawg_sb[:, sl], start=False, stop=True))
                            for half in range(2):
                                if dc >= 1:
                                    t.wait_ge((fUCP if half == 0 else fHID).h,
                                              (fUCP0 if half == 0 else fHID0) + dc)
                                sl = slice(half * 512, (half + 1) * 512)
                                for fc in range(NFC):
                                    t.matmul(u_ps[:, sl],
                                             wgu[2 + dc % 2][:, fc * 128:(fc + 1) * 128],
                                             xnt2[:, fc * FFN_W + half * 512: fc * FFN_W + half * 512 + 512],
                                             start=(fc == 0), stop=False)
                                t.matmul(u_ps[:, sl],
                                         sbu4[(dc // 4) % 2][:, (dc % 4) * 128:(dc % 4 + 1) * 128],
                                         sxu_sb[:, sl], start=False, stop=False)
                                fUMM.inc(t.matmul(
                                    u_ps[:, sl],
                                    bcu4[(dc // 4) % 2][:, (dc % 4) * 128:(dc % 4 + 1) * 128],
                                    xawu_sb[:, sl], start=False, stop=True))
                            for dcx in ([dc - 1] if dc >= 1 else []) + ([NDC - 1] if dc == NDC - 1 else []):
                                t.wait_ge(fHID.h, fHID0 + dcx + 1)
                                for half in range(2):
                                    sl = slice(half * 512, (half + 1) * 512)
                                    t.matmul(sxd_ps[:, sl],
                                             sad_sb[:, dcx * RK:(dcx + 1) * RK],
                                             hid[:, dcx * FFN_W + half * 512: dcx * FFN_W + half * 512 + 512],
                                             start=(dcx == 0), stop=(dcx == NDC - 1))
                                    ins = t.matmul(hxd_ps[:, sl],
                                                   ad_sb[:, dcx * ER:(dcx + 1) * ER],
                                                   hid[:, dcx * FFN_W + half * 512: dcx * FFN_W + half * 512 + 512],
                                                   start=(dcx == 0), stop=(dcx == NDC - 1))
                                if dcx == NDC - 1:
                                    fHX.inc(ins)
                    block.tensor(p4d_pe)

                    def p4d_act(s, W=W, fGMM0=fGMM0, fHID0=fHID0, fUCP0=fUCP0):
                        for dc in range(NDC):
                            for half in range(2):
                                s.wait_ge(fGMM.h, fGMM0 + 2 * dc + half + 1)
                                if dc >= 1:
                                    s.wait_ge((fUCP if half == 0 else fHID).h,
                                              (fUCP0 if half == 0 else fHID0) + dc)
                                sl = slice(half * 512, (half + 1) * 512)
                                fSIL.inc(s.activation(gs_sb[:, sl], g_ps[:, sl],
                                                      AF.Silu))
                        s.wait_ge(fHX.h, W + 1)
                        fSXD.inc(s.activation(sxd_sb[:], sxd_ps[:], AF.Identity,
                                              scale=1.0 / RK))
                    block.scalar(p4d_act)

                    def p4d_vec(v, W=W, fUMM0=fUMM0, fSIL0=fSIL0, fHID0=fHID0,
                                fWBE0=fWBE0, fUCP0=fUCP0):
                        for dc in range(NDC):
                            v.wait_ge(fUMM.h, fUMM0 + 2 * dc + 2)
                            v.wait_ge(fSIL.h, fSIL0 + 2 * dc + 2)
                            fUCP.inc(v.tensor_mul(
                                hid[:, dc * FFN_W:dc * FFN_W + 512],
                                gs_sb[:, 0:512], u_ps[:, 0:512]))
                            fHID.inc(v.tensor_mul(
                                hid[:, dc * FFN_W + 512:(dc + 1) * FFN_W],
                                gs_sb[:, 512:1024], u_ps[:, 512:1024]))
                        v.wait_ge(fHX.h, W + 1)
                        v.wait_ge(fWBE.h, fWBE0 + 1)
                        fHXW.inc(v.tensor_mul(hxw_sb[:], hxd_ps[:], wbc_sb[:]))
                    block.vector(p4d_vec)

                    if dbg:
                        def p4d_dump(sy, W=W, fHID0=fHID0):
                            for dc in range(NDC):
                                sy.wait_ge(fHID.h, fHID0 + dc + 1)
                                sy.dma_start(out=hid_c[W, dc],
                                             in_=hid[:, dc * FFN_W:(dc + 1) * FFN_W]
                                             ).then_inc(cLD.h, 16)
                        block.sync(p4d_dump)

                with ExitStack() as cr:
                    dn_ps = [cr.enter_context(nc.psum_tensor("dn_ps%d_w%d" % (i, W),
                                                             [128, 512], F32))
                             for i in range(8)]
                    fWDU0, fDN0, fOEV0 = (fWDU.n, fDN.n, fOEV.n)
                    fWDp0 = [fWDp[0].n, fWDp[1].n, fWDp[2].n]
                    fR1p0 = [fR1p[0].n, fR1p[1].n]
                    fOUTp0 = [fOUTp[0].n, fOUTp[1].n]

                    def p4e_wd(s, fWDU0=fWDU0, W=W):
                        if W == 0:
                            s.wait_ge(cWR.h, CWR_ALL)
                        for d4 in range(4):
                            for dc in range(NDC):
                                didx = d4 * NDC + dc
                                tgt = fWDU0 + didx - 2
                                if tgt > 0:
                                    s.wait_ge(fWDU.h, tgt)
                                fWDp[didx % 3].inc(s.dma_start(
                                    out=wd_t[didx % 3][:],
                                    in_=wdb_c[dc, d4]), 16)
                    block.scalar(p4e_wd)

                    def p4e_pe(t, W=W, fWDp0=fWDp0, fOEV0=fOEV0):
                        t.wait_ge(fSXD.h, W + 1)
                        t.wait_ge(fHXW.h, W + 1)
                        for d4 in range(4):
                            if d4 >= 1:
                                t.wait_ge(fOEV.h, fOEV0 + 8 * d4)
                            for dc in range(NDC):
                                didx = d4 * NDC + dc
                                t.wait_ge(fWDp[didx % 3].h,
                                          fWDp0[didx % 3] + 16 * (didx // 3 + 1))
                                for tt in range(8):
                                    ins = t.matmul(
                                        dn_ps[tt][:],
                                        hid[:, dc * FFN_W + tt * 128: dc * FFN_W + (tt + 1) * 128],
                                        wd_t[didx % 3][:],
                                        start=(dc == 0), stop=False,
                                        skip_group_check=True)
                                fWDU.inc(ins)
                            for tt in range(8):
                                t.matmul(dn_ps[tt][:], sxd_sb[:, tt * 128:(tt + 1) * 128],
                                         sbd_sb[:, d4 * 512:(d4 + 1) * 512],
                                         start=False, stop=False, skip_group_check=True)
                                fDN.inc(t.matmul(
                                    dn_ps[tt][:], hxw_sb[:, tt * 128:(tt + 1) * 128],
                                    bcd_sb[:, d4 * 512:(d4 + 1) * 512],
                                    start=False, stop=True, skip_group_check=True))
                    block.tensor(p4e_pe)

                    def p4e_sy(sy, W=W, fOEV0=fOEV0):
                        sy.wait_ge(wWRp[0].h, wWRp[0].n)
                        sy.wait_ge(wWRp[1].h, wWRp[1].n)
                        for d4 in range(4):
                            for tt in range(8):
                                i = d4 * 8 + tt
                                if i >= 2:
                                    sy.wait_ge(fOEV.h, fOEV0 + i - 1)
                                fR1p[i % 2].inc(sy.dma_start(
                                    out=r1r[i % 2][:],
                                    in_=r1_c[W * FFN_W + tt * 128: W * FFN_W + (tt + 1) * 128,
                                             d4 * 512:(d4 + 1) * 512]), 16)
                                sy.wait_ge(fOEV.h, fOEV0 + i + 1)
                                fOUTp[i % 2].inc(sy.dma_start(
                                    out=out_d[W * FFN_W + tt * 128: W * FFN_W + (tt + 1) * 128,
                                              d4 * 512:(d4 + 1) * 512],
                                    in_=o_sb[i % 2][:]), 16)
                    block.sync(p4e_sy)

                    def p4e_vec(v, W=W, fDN0=fDN0, fR1p0=fR1p0, fOEV0=fOEV0,
                                fOUTp0=fOUTp0):
                        for d4 in range(4):
                            for tt in range(8):
                                i = d4 * 8 + tt
                                v.wait_ge(fDN.h, fDN0 + d4 * 8 + tt + 1)
                                v.wait_ge(fR1p[i % 2].h, fR1p0[i % 2] + 16 * (i // 2 + 1))
                                if i >= 2:
                                    v.wait_ge(fOUTp[i % 2].h, fOUTp0[i % 2] + 16 * (i // 2))
                                fOEV.inc(v.tensor_add(o_sb[i % 2][:], dn_ps[tt][:],
                                                      r1r[i % 2][:]))
                    block.vector(p4e_vec)

            def p_final(sy):
                sy.wait_ge(fOUTp[0].h, fOUTp[0].n)
                sy.wait_ge(fOUTp[1].h, fOUTp[1].n)
            block.sync(p_final)
            assert fOUTp[0].n + fOUTp[1].n == 16 * 64

    return nc


def _make_masks(r):
    m = np.zeros((8, 128, 512), np.float32)
    tri = np.triu(np.ones((128, 128), np.float32))
    for d in range(8):
        for mcol in range(4):
            delta = d - 2 * mcol - r
            blk = m[d][:, mcol * 128:(mcol + 1) * 128]
            if delta < 0:
                blk[:] = 1.0
            elif delta == 0:
                blk[:] = tri
    return m.astype(ml_dtypes.bfloat16)


def kernel(**inputs):
    if "nc" not in _CACHE:
        _CACHE["nc"] = build_nc()
    nc = _CACHE["nc"]

    X = np.ascontiguousarray(inputs["L0_attn_X"], dtype=np.float32)

    def f32(k):
        return np.ascontiguousarray(inputs[k], dtype=np.float32)

    weights = {
        "ln_g": f32("L0_attn_ln_g"), "ln_b": f32("L0_attn_ln_b"),
        "wq": f32("L0_attn_Wq"), "wk": f32("L0_attn_Wk"),
        "wv": f32("L0_attn_Wv"), "wo": f32("L0_attn_Wo"),
        "bo": f32("L0_attn_bo"),
        "w_router": f32("L0_ffn_W_router"),
        "w_gate": f32("L0_ffn_W_gate"), "w_up": f32("L0_ffn_W_up"),
        "w_down": f32("L0_ffn_W_down"),
        "sa_gate": f32("L0_ffn_sA_gate"), "sb_gate": f32("L0_ffn_sB_gate"),
        "sa_up": f32("L0_ffn_sA_up"), "sb_up": f32("L0_ffn_sB_up"),
        "sa_down": f32("L0_ffn_sA_down"), "sb_down": f32("L0_ffn_sB_down"),
        "a_gate": f32("L0_ffn_A_gate"), "b_gate": f32("L0_ffn_B_gate"),
        "a_up": f32("L0_ffn_A_up"), "b_up": f32("L0_ffn_B_up"),
        "a_down": f32("L0_ffn_A_down"), "b_down": f32("L0_ffn_B_down"),
    }
    bias_pp = np.zeros((128, 64), np.float32)
    bias_pp[:, 0:16] = f32("L0_attn_bq").reshape(16, 128).T
    bias_pp[:, 16:32] = f32("L0_attn_bk").reshape(16, 128).T
    bias_pp[:, 32:48] = f32("L0_attn_bv").reshape(16, 128).T
    ebias = np.ascontiguousarray(
        np.broadcast_to(f32("L0_ffn_expert_bias"), (128, 8)))
    masks = [_make_masks(0), _make_masks(1)]

    in_maps = []
    for c in range(8):
        b, r = c // 2, c % 2
        m = dict(weights)
        m["xb"] = X[b]
        m["xq"] = np.ascontiguousarray(
            X[b].reshape(32, 128, 2048)[r::2].reshape(2048, 2048))
        m["mq"] = masks[r]
        m["bias_pp"] = bias_pp
        m["ebias_bc"] = ebias
        in_maps.append(m)

    res = run_bass_kernel_spmd(nc, in_maps, core_ids=list(range(8)))
    out = np.empty((4, 4096, 2048), np.float32)
    for c in range(8):
        b, r = c // 2, c % 2
        out[b].reshape(32, 128, 2048)[r::2] = \
            res.results[c]["out"].reshape(16, 128, 2048)
    return out


if __name__ == "__main__":
    nc = build_nc()
    print("build ok; instructions:", nc.n_instructions())


# version 16
# speedup vs baseline: 1.0176x; 1.0176x over previous
import math
import numpy as np
import ml_dtypes
from contextlib import ExitStack

import concourse.bass as bass
import concourse.mybir as mybir
from concourse.bass_utils import run_bass_kernel_spmd

F32 = mybir.dt.float32
BF16 = mybir.dt.bfloat16
AF = mybir.ActivationFunctionType
ALU = mybir.AluOpType
AX = mybir.AxisListType

D = 2048
NFC = 16
H = 16
DH = 128
S = 4096
NT_ALL = 32
NOWN = 2048
NT_OWN = 16
NQC = 4
DFF = 5504
NDC = 43
E8 = 8
RK = 8
ER = 64
SCL = 1.0 / math.sqrt(128.0)
EPS = 1e-5
FFN_W = 1024
NWIN = 2

_CACHE = {}


class SC:
    def __init__(self, nc, ctx, name):
        self.h = ctx.enter_context(nc.semaphore(name))
        self.n = 0

    def inc(self, instr, k=1):
        instr.then_inc(self.h, k)
        self.n += k
        return instr


def build_nc(dbg=False):
    nc = bass.Bass("TRN2", target_bir_lowering=False)

    def din(name, shape, dt=F32):
        return nc.dram_tensor(name, list(shape), dt, kind="ExternalInput").ap()

    xb = din("xb", [S, D])
    xq = din("xq", [NOWN, D])
    mq = din("mq", [8, 128, 512], BF16)
    bias_pp = din("bias_pp", [128, 64])
    ebias_bc = din("ebias_bc", [128, E8])
    ln_g = din("ln_g", [D])
    ln_b = din("ln_b", [D])
    wq_d = din("wq", [D, D])
    wk_d = din("wk", [D, D])
    wv_d = din("wv", [D, D])
    wo_d = din("wo", [D, D])
    bo_d = din("bo", [D])
    wr_d = din("w_router", [D, E8])
    wg_d = din("w_gate", [D, DFF])
    wu_d = din("w_up", [D, DFF])
    wd_d = din("w_down", [DFF, D])
    sag_d = din("sa_gate", [D, RK])
    sbg_d = din("sb_gate", [RK, DFF])
    sau_d = din("sa_up", [D, RK])
    sbu_d = din("sb_up", [RK, DFF])
    sad_d = din("sa_down", [DFF, RK])
    sbd_d = din("sb_down", [RK, D])
    ag_d = din("a_gate", [E8, D, RK])
    bg_d = din("b_gate", [E8, RK, DFF])
    au_d = din("a_up", [E8, D, RK])
    bu_d = din("b_up", [E8, RK, DFF])
    ad_d = din("a_down", [E8, DFF, RK])
    bd_d = din("b_down", [E8, RK, D])

    out_d = nc.dram_tensor("out", [NOWN, D], F32, kind="ExternalOutput").ap()

    skind = "ExternalOutput" if dbg else "Internal"
    xn_c = nc.dram_tensor("xn_c", [NFC, S, 128], BF16, kind=skind).ap()
    xnq_c = nc.dram_tensor("xnq_c", [NFC, NOWN, 128], BF16, kind=skind).ap()
    q_c = nc.dram_tensor("q_c", [H, 128, NOWN], BF16, kind=skind).ap()
    k_c = nc.dram_tensor("k_c", [H, 128, S], BF16, kind=skind).ap()
    v_c = nc.dram_tensor("v_c", [4, 8, 4, 128, 512], BF16, kind=skind).ap()
    r1_c = nc.dram_tensor("r1_c", [NOWN, D], F32, kind=skind).ap()
    hid_c = nc.dram_tensor("hid_c", [NWIN, NDC, 128, FFN_W], BF16, kind=skind).ap() if dbg else None
    wgb_c = nc.dram_tensor("wgb_c", [NDC, 128, NFC * 128], BF16, kind=skind).ap()
    wub_c = nc.dram_tensor("wub_c", [NDC, 128, NFC * 128], BF16, kind=skind).ap()
    wdb_c = nc.dram_tensor("wdb_c", [NDC, 4, 128, 512], BF16, kind=skind).ap()
    xn2_c = nc.dram_tensor("xn2_c", [NFC, NOWN, 128], BF16, kind=skind).ap()

    ident_c = nc.inline_tensor(np.eye(128, dtype=np.float32), "identf")
    rsel_np = np.zeros((8, 64), np.float32)
    for e in range(8):
        rsel_np[e, e * 8:(e + 1) * 8] = 1.0 / RK
    rsel_c = nc.inline_tensor(rsel_np, "rself")

    ctx = ExitStack()
    with ctx:
        def sb(name, shape, dt):
            return ctx.enter_context(nc.sbuf_tensor(name, list(shape), dt))

        ones_col = sb("ones_col", [128, 1], BF16)
        ones_row = sb("ones_row", [1, 128], BF16)
        ones_row_f = sb("ones_row_f", [1, 128], F32)
        bias_sb = sb("bias_sb", [128, 64], F32)
        bqs = sb("bqs", [128, 16], F32)
        ebias_sb = sb("ebias_sb", [128, E8], F32)
        negmu = sb("negmu", [128, 1], F32)
        mu_bf = sb("mu_bf", [1, 1], BF16)
        s0m = sb("s0m", [128, 1], F32)
        e0 = sb("e0", [128, 1], BF16)
        linv = sb("linv", [1, 512], F32)
        lbc = sb("lbc", [128, 512], F32)
        avn = sb("avn", [128, 512], F32)
        tmp512 = sb("tmp512", [128, 512], F32)
        sx1 = sb("sx1", [128, 1], F32)
        ssq1 = sb("ssq1", [128, 1], F32)
        sy2 = sb("sy2", [128, 1], F32)
        ssq2 = sb("ssq2", [128, 1], F32)
        m1v = sb("m1v", [128, 1], F32)
        t1v = sb("t1v", [128, 1], F32)
        msq = sb("msq", [128, 1], F32)
        v1v = sb("v1v", [128, 1], F32)
        vpe = sb("vpe", [128, 1], F32)
        r1c = sb("r1c", [128, 1], F32)
        r2c = sb("r2c", [128, 1], F32)
        s1v = sb("s1v", [128, 1], F32)
        s2v = sb("s2v", [128, 1], F32)
        nm1 = sb("nm1", [128, 1], F32)
        nm1n = sb("nm1n", [128, 1], F32)
        nm2 = sb("nm2", [128, 1], F32)
        nm2n = sb("nm2n", [128, 1], F32)
        mu2v = sb("mu2v", [128, 1], F32)
        ssq1p = [sb("ssq1p%d" % i, [128, 1], F32) for i in range(2)]
        sy2p = [sb("sy2p%d" % i, [128, 1], F32) for i in range(2)]
        m1p = [sb("m1p%d" % i, [128, 1], F32) for i in range(2)]
        r1p = [sb("r1p%d" % i, [128, 1], F32) for i in range(2)]
        r2p = [sb("r2p%d" % i, [128, 1], F32) for i in range(2)]
        s1p = [sb("s1p%d" % i, [128, 1], F32) for i in range(2)]
        s2p = [sb("s2p%d" % i, [128, 1], F32) for i in range(2)]
        nm1p = [sb("nm1p%d" % i, [128, 1], F32) for i in range(2)]
        nm2p = [sb("nm2p%d" % i, [128, 1], F32) for i in range(2)]
        mu2p = [sb("mu2p%d" % i, [128, 1], F32) for i in range(2)]

        def S_(name):
            return SC(nc, ctx, name)

        sCON = S_("sCON")
        sBAR = S_("sBAR")
        pSq1 = S_("pSq1")
        pS1 = S_("pS1")
        pSq2 = S_("pSq2")
        pS2 = S_("pS2")
        pD1 = S_("pD1")
        pD3 = S_("pD3")
        pD4 = S_("pD4")
        pDg = S_("pDg")
        pD6 = S_("pD6")
        pG = S_("pG")
        qWL = S_("qWL")
        qX = S_("qX")
        qMM = S_("qMM")
        qEV = S_("qEV")
        kX = S_("kX")
        kKmm = S_("kKmm")
        kVmm = S_("kVmm")
        kKev = S_("kKev")
        kVev = S_("kVev")
        aHD = S_("aHD")
        aS0 = S_("aS0")
        aRM = S_("aRM")
        aE0 = S_("aE0")
        aMUP = S_("aMUP")
        aMUV = S_("aMUV")
        aNBP = S_("aNBP")
        aMU = S_("aMU")
        aST = S_("aST")
        aEXP = S_("aEXP")
        aMSK = S_("aMSK")
        aKT = S_("aKT")
        aLINV = S_("aLINV")
        aLB = S_("aLB")
        aLBC = S_("aLBC")
        aAVD = S_("aAVD")
        wLD = S_("wLD")
        wMM = S_("wMM")
        wEVA = S_("wEVA")
        wSq1 = S_("wSq1")
        wS1 = S_("wS1")
        wD1 = S_("wD1")
        wDg = S_("wDg")
        wAPL = S_("wAPL")
        fLD = S_("fLD")
        fX2 = S_("fX2")
        fRL = S_("fRL")
        fSG = S_("fSG")
        fTOP = S_("fTOP")
        fWT = S_("fWT")
        fWTE = S_("fWTE")
        fWB = S_("fWB")
        fWBE = S_("fWBE")
        fXA = S_("fXA")
        fXAW = S_("fXAW")
        fSX = S_("fSX")
        fSXE = S_("fSXE")
        fGMM = S_("fGMM")
        fUMM = S_("fUMM")
        fSIL = S_("fSIL")
        fUCP = S_("fUCP")
        fHID = S_("fHID")
        fHX = S_("fHX")
        fSXD = S_("fSXD")
        fHXW = S_("fHXW")
        fWDU = S_("fWDU")
        fDN = S_("fDN")
        fOEV = S_("fOEV")
        fDg = S_("fDg")
        cLD = S_("cLD")
        cWR = S_("cWR")
        pXp = [S_("pXp0"), S_("pXp1")]
        pWp = [S_("pWp0"), S_("pWp1")]
        qWRp = [S_("qWRp0"), S_("qWRp1")]
        kWRp = [S_("kWRp0"), S_("kWRp1")]
        vWRp = [S_("vWRp0"), S_("vWRp1")]
        aLDp = [S_("aLDp0"), S_("aLDp1")]
        wXQp = [S_("wXQp0"), S_("wXQp1")]
        wWRp = [S_("wWRp0"), S_("wWRp1")]
        fWGUp = [S_("fWGUp0"), S_("fWGUp1")]
        fWDp = [S_("fWDp0"), S_("fWDp1"), S_("fWDp2")]
        fR1p = [S_("fR1p0"), S_("fR1p1")]
        fOUTp = [S_("fOUTp0"), S_("fOUTp1")]
        fSBp = [S_("fSBp0"), S_("fSBp1")]

        block = ctx.enter_context(nc.Block())
        nbar = [0]

        def barrier():
            nbar[0] += 1
            tgt = 5 * nbar[0]

            def f(eng):
                eng.drain()
                eng.sem_inc(sBAR.h, 1)
                eng.wait_ge(sBAR.h, tgt)
            block.sync(f)
            block.gpsimd(f)
            block.tensor(f)
            block.vector(f)
            block.scalar(f)
            sBAR.n = tgt

        def p0_sync(sy):
            sy.dma_start(out=bias_sb[:], in_=bias_pp).then_inc(sCON.h, 16)
            sy.dma_start(out=ebias_sb[:], in_=ebias_bc).then_inc(sCON.h, 16)
        block.sync(p0_sync)
        sCON.n += 32

        def p0_vec(v):
            v.memset(ones_col[:], 1.0)
            v.memset(ones_row[:], 1.0)
            v.memset(ones_row_f[:], 1.0)
            v.wait_ge(sCON.h, 32)
            v.tensor_scalar_mul(bqs[:], bias_sb[:, 0:16], SCL).then_inc(sCON.h, 1)
        block.vector(p0_vec)
        sCON.n += 1

        NTT = NT_ALL + NT_OWN

        with ExitStack() as c1:
            def s1b(name, shape, dt):
                return c1.enter_context(nc.sbuf_tensor(name, list(shape), dt))
            xt = [s1b("p1_x0", [128, D], F32), s1b("p1_x1", [128, D], F32)]
            yb = s1b("p1_y", [128, D], F32)
            sqscr = s1b("p1_sq", [128, D], F32)
            ttb = [s1b("p1_t0", [128, D], F32), s1b("p1_t1", [128, D], F32)]
            g1b = s1b("p1_g1", [128, D], F32)
            xnb = [s1b("p1_xn0", [128, D], BF16), s1b("p1_xn1", [128, D], BF16)]
            gam = s1b("p1_gam", [128, D], F32)
            bet = s1b("p1_bet", [128, D], F32)

            def src_ap(t):
                if t < NT_ALL:
                    return xb[t * 128:(t + 1) * 128, :]
                tt = t - NT_ALL
                return xq[tt * 128:(tt + 1) * 128, :]

            def dst_ap(t):
                if t < NT_ALL:
                    return xn_c[:, t * 128:(t + 1) * 128, :].rearrange("c t f -> t c f")
                tt = t - NT_ALL
                return xnq_c[:, tt * 128:(tt + 1) * 128, :].rearrange("c t f -> t c f")

            def p1_sync(sy):
                sy.dma_start(out=gam[:], in_=ln_g[None, :].broadcast_to([128, D])).then_inc(sCON.h, 16)
                sy.dma_start(out=bet[:], in_=ln_b[None, :].broadcast_to([128, D])).then_inc(sCON.h, 16)
                for t in range(NTT):
                    if t >= 2:
                        sy.wait_ge(pD3.h, t - 1)
                    pXp[t % 2].inc(sy.dma_start(out=xt[t % 2][:], in_=src_ap(t)), 16)
                    if t >= 1:
                        sy.wait_ge(pG.h, t)
                        pWp[(t - 1) % 2].inc(sy.dma_start(
                            out=dst_ap(t - 1),
                            in_=xnb[(t - 1) % 2].ap().rearrange("p (c f) -> p c f", f=128)), 16)
                sy.wait_ge(pG.h, NTT)
                pWp[(NTT - 1) % 2].inc(sy.dma_start(
                    out=dst_ap(NTT - 1),
                    in_=xnb[(NTT - 1) % 2].ap().rearrange("p (c f) -> p c f", f=128)), 16)
            block.sync(p1_sync)
            sCON.n += 32

            DG1 = {}
            DG2 = {}

            def p1_vec(v):
                def g(ins):
                    pDg.inc(ins)
                    v.wait_ge(pDg.h, pDg.n)
                    return ins
                for t in range(-1, NTT):
                    if t >= 0:
                        p = t % 2
                        v.wait_ge(pS1.h, t + 1)
                        g(v.tensor_scalar(nm1p[p][:], m1p[p][:], s1p[p][:], -1.0,
                                          ALU.mult, ALU.mult))
                        DG1[t] = pDg.n
                    if t + 1 < NTT:
                        tn = t + 1
                        p = tn % 2
                        v.wait_ge(pXp[p].h, 16 * (tn // 2 + 1))
                        g(v.reduce_sum(sx1[:], xt[p][:], axis=AX.X))
                        v.wait_ge(pSq1.h, tn + 1)
                        g(v.tensor_scalar_mul(m1p[p][:], sx1[:], 1.0 / D))
                        g(v.tensor_scalar_mul(t1v[:], ssq1p[p][:], 1.0 / D))
                        g(v.tensor_mul(msq[:], m1p[p][:], m1p[p][:]))
                        g(v.tensor_sub(v1v[:], t1v[:], msq[:]))
                        g(v.tensor_scalar_add(vpe[:], v1v[:], EPS))
                        pD1.inc(v.reciprocal(r1p[p][:], vpe[:]))
                        v.wait_ge(pD1.h, pD1.n)
                        g(v.tensor_mul(msq[:], v1v[:], r1p[p][:]))
                        g(v.tensor_scalar_add(vpe[:], msq[:], EPS))
                        pD4.inc(v.reciprocal(r2p[p][:], vpe[:]))
                    if t >= 0:
                        p = t % 2
                        v.wait_ge(pD3.h, t + 1)
                        g(v.tensor_scalar_mul(mu2p[p][:], sy2p[p][:], 1.0 / D))
                        v.wait_ge(pS2.h, t + 1)
                        g(v.tensor_scalar(nm2p[p][:], mu2p[p][:], s2p[p][:], -1.0,
                                          ALU.mult, ALU.mult))
                        DG2[t] = pDg.n
            block.vector(p1_vec)
            assert pD4.n == NTT and pD1.n == NTT

            def p1_act(s):
                for t in range(-1, NTT):
                    if t + 1 < NTT:
                        tn = t + 1
                        p = tn % 2
                        s.wait_ge(pXp[p].h, 16 * (tn // 2 + 1))
                        pSq1.inc(s.activation(sqscr[:], xt[p][:], AF.Square,
                                              accum_out=ssq1p[p][:]))
                    if t >= 0:
                        p = t % 2
                        s.wait_ge(pD1.h, t + 1)
                        pS1.inc(s.activation(s1p[p][:], r1p[p][:], AF.Sqrt))
                        s.wait_ge(pDg.h, DG1[t])
                        pD3.inc(s.activation(yb[:], xt[p][:], AF.Identity,
                                             bias=nm1p[p][:], scale=s1p[p][:],
                                             accum_out=sy2p[p][:]))
                        s.wait_ge(pD4.h, t + 1)
                        pS2.inc(s.activation(s2p[p][:], r2p[p][:], AF.Sqrt))
                        s.wait_ge(pDg.h, DG2[t])
                        if t >= 2:
                            s.wait_ge(pG.h, t - 1)
                        pD6.inc(s.activation(ttb[p][:], yb[:], AF.Identity,
                                             bias=nm2p[p][:], scale=s2p[p][:]))
            block.scalar(p1_act)
            assert pSq1.n == NTT and pS2.n == NTT and pD6.n == NTT

            def p1_gp(g):
                g.wait_ge(sCON.h, 65)
                for t in range(NTT):
                    p = t % 2
                    g.wait_ge(pD6.h, t + 1)
                    if t >= 2:
                        g.wait_ge(pWp[t % 2].h, 16 * (t // 2))
                    g.tensor_mul(g1b[:], ttb[p][:], gam[:])
                    pG.inc(g.tensor_add(xnb[p][:], g1b[:], bet[:]))
            block.gpsimd(p1_gp)
            assert pG.n == NTT

            barrier()

        with ExitStack() as c2:
            def s2b(name, shape, dt):
                return c2.enter_context(nc.sbuf_tensor(name, list(shape), dt))
            wq_sb = s2b("wq_sb", [128, NFC * D], BF16)
            xnqt = [s2b("xnqt0", [128, NFC * 512], BF16),
                    s2b("xnqt1", [128, NFC * 512], BF16)]
            qev = [s2b("qev0", [128, 512], BF16), s2b("qev1", [128, 512], BF16)]
            q_ps = [c2.enter_context(nc.psum_tensor("q_ps%d" % i, [128, 512], F32))
                    for i in range(3)]

            def p15a_gp(g):
                for fc in range(NFC):
                    qWL.inc(g.dma_start(
                        out=wq_sb[:, fc * D:(fc + 1) * D],
                        in_=wq_d[fc * 128:(fc + 1) * 128, :]), 16)
            block.gpsimd(p15a_gp)
            assert qWL.n == 256

            def p15a_sc(s):
                s.wait_ge(sCON.h, 33)
                s.wait_ge(pWp[0].h, pWp[0].n)
                s.wait_ge(pWp[1].h, pWp[1].n)
                for qw in range(NQC + 1):
                    if qw < NQC:
                        if qw >= 2:
                            s.wait_ge(qMM.h, 16 * (qw - 1))
                        for fc in range(NFC):
                            qX.inc(s.dma_start_transpose(
                                xnqt[qw % 2][:, fc * 512:(fc + 1) * 512],
                                xnq_c[fc, qw * 512:(qw + 1) * 512, :]), 16)
                    if qw >= 1:
                        for h in range(H):
                            idx = (qw - 1) * 16 + h
                            s.wait_ge(qMM.h, idx + 1)
                            if idx >= 2:
                                s.wait_ge(qWRp[idx % 2].h, 16 * (idx // 2))
                            qEV.inc(s.activation(qev[h % 2][:], q_ps[idx % 3][:],
                                                 AF.Identity, bias=bqs[:, h:h + 1],
                                                 scale=SCL))
            block.scalar(p15a_sc)
            assert qX.n == 16 * 64 and qEV.n == 64

            def p15a_pe(t):
                t.wait_ge(qWL.h, 256)
                for qw in range(NQC):
                    t.wait_ge(qX.h, 16 * 16 * (qw + 1))
                    for h in range(H):
                        idx = qw * 16 + h
                        if idx >= 2:
                            t.wait_ge(qEV.h, idx - 1)
                        for fc in range(NFC):
                            ins = t.matmul(
                                q_ps[(qw * 16 + h) % 3][:],
                                wq_sb[:, fc * D + h * 128: fc * D + (h + 1) * 128],
                                xnqt[qw % 2][:, fc * 512:(fc + 1) * 512],
                                start=(fc == 0), stop=(fc == NFC - 1))
                        qMM.inc(ins)
            block.tensor(p15a_pe)
            assert qMM.n == 64

            def p15a_sy(sy):
                for qw in range(NQC):
                    for h in range(H):
                        idx = qw * 16 + h
                        sy.wait_ge(qEV.h, idx + 1)
                        qWRp[idx % 2].inc(sy.dma_start(
                            out=q_c[h][:, qw * 512:(qw + 1) * 512],
                            in_=qev[h % 2][:]), 16)
            block.sync(p15a_sy)
            assert qWRp[0].n + qWRp[1].n == 16 * 64

            barrier()

        with ExitStack() as c3:
            def s3b(name, shape, dt):
                return c3.enter_context(nc.sbuf_tensor(name, list(shape), dt))
            wk_sb = s3b("wk_sb", [128, NFC * D], BF16)
            wv_sb = s3b("wv_sb", [128, NFC * D], BF16)
            xnt = [s3b("xnt0", [128, NFC * 512], BF16),
                   s3b("xnt1", [128, NFC * 512], BF16)]
            kev = [s3b("kev0", [128, 512], BF16), s3b("kev1", [128, 512], BF16)]
            vev = [s3b("vev0", [128, 512], BF16), s3b("vev1", [128, 512], BF16)]
            k_ps = [c3.enter_context(nc.psum_tensor("k_ps%d" % i, [128, 512], F32))
                    for i in range(3)]
            v_ps = [c3.enter_context(nc.psum_tensor("v_ps%d" % i, [128, 512], F32))
                    for i in range(3)]

            def p15b_gp(g):
                for fc in range(NFC):
                    qWL.inc(g.dma_start(
                        out=wk_sb[:, fc * D:(fc + 1) * D],
                        in_=wk_d[fc * 128:(fc + 1) * 128, :]), 16)
                for fc in range(NFC):
                    qWL.inc(g.dma_start(
                        out=wv_sb[:, fc * D:(fc + 1) * D],
                        in_=wv_d[fc * 128:(fc + 1) * 128, :]), 16)
            block.gpsimd(p15b_gp)
            assert qWL.n == 256 * 3

            def p15b_sc(s):
                for w in range(9):
                    if w < 8:
                        if w >= 2:
                            s.wait_ge(kKmm.h, 16 * (w - 1))
                            s.wait_ge(kVmm.h, 16 * (w - 1))
                        for fc in range(NFC):
                            kX.inc(s.dma_start_transpose(
                                xnt[w % 2][:, fc * 512:(fc + 1) * 512],
                                xn_c[fc, w * 512:(w + 1) * 512, :]), 16)
                    if w >= 1:
                        wp = w - 1
                        for h in range(H):
                            idx = wp * 16 + h
                            s.wait_ge(kKmm.h, idx + 1)
                            if idx >= 2:
                                s.wait_ge(kWRp[idx % 2].h, 16 * (idx // 2))
                            kKev.inc(s.activation(kev[h % 2][:], k_ps[idx % 3][:],
                                                  AF.Identity,
                                                  bias=bias_sb[:, 16 + h:17 + h]))
                        for u in range(16):
                            idx = wp * 16 + u
                            s.wait_ge(kVmm.h, idx + 1)
                            if idx >= 2:
                                s.wait_ge(vWRp[idx % 2].h, 16 * (idx // 2))
                            kVev.inc(s.activation(vev[u % 2][:], v_ps[idx % 3][:],
                                                  AF.Identity))
            block.scalar(p15b_sc)
            assert kX.n == 16 * 16 * 8 and kKev.n == 128 and kVev.n == 128

            def p15b_pe(t):
                t.wait_ge(qWL.h, 256 * 3)
                for w in range(8):
                    t.wait_ge(kX.h, 16 * 16 * (w + 1))
                    for h in range(H):
                        idx = w * 16 + h
                        if idx >= 2:
                            t.wait_ge(kKev.h, idx - 1)
                        for fc in range(NFC):
                            ins = t.matmul(
                                k_ps[(w * 16 + h) % 3][:],
                                wk_sb[:, fc * D + h * 128: fc * D + (h + 1) * 128],
                                xnt[w % 2][:, fc * 512:(fc + 1) * 512],
                                start=(fc == 0), stop=(fc == NFC - 1))
                        kKmm.inc(ins)
                    for hg in range(4):
                        for sub in range(4):
                            u = hg * 4 + sub
                            idx = w * 16 + u
                            if idx >= 2:
                                t.wait_ge(kVev.h, idx - 1)
                            for fc in range(NFC):
                                ins = t.matmul(
                                    v_ps[(w * 16 + u) % 3][:],
                                    xnt[w % 2][:, fc * 512 + sub * 128: fc * 512 + (sub + 1) * 128],
                                    wv_sb[:, fc * D + hg * 512: fc * D + (hg + 1) * 512],
                                    start=(fc == 0), stop=(fc == NFC - 1))
                            kVmm.inc(ins)
            block.tensor(p15b_pe)
            assert kKmm.n == 128 and kVmm.n == 128

            def p15b_sy(sy):
                for w in range(8):
                    for h in range(H):
                        idx = w * 16 + h
                        sy.wait_ge(kKev.h, idx + 1)
                        kWRp[idx % 2].inc(sy.dma_start(
                            out=k_c[h][:, w * 512:(w + 1) * 512],
                            in_=kev[h % 2][:]), 16)
                    for hg in range(4):
                        for sub in range(4):
                            u = hg * 4 + sub
                            idx = w * 16 + u
                            sy.wait_ge(kVev.h, idx + 1)
                            vWRp[idx % 2].inc(sy.dma_start(
                                out=v_c[hg, w, sub],
                                in_=vev[u % 2][:]), 16)
            block.sync(p15b_sy)
            assert kWRp[0].n + kWRp[1].n == 16 * 128
            assert vWRp[0].n + vWRp[1].n == 16 * 128

            barrier()



        def EX(qc):
            return 8 * (qc + 1)

        TOT_KT = H * sum(EX(qc) for qc in range(NQC))

        with ExitStack() as c4a:
            avcat = c4a.enter_context(nc.sbuf_tensor("avcat", [128, H * NOWN], BF16))
            c4 = c4a.enter_context(ExitStack())
            def s4b(name, shape, dt):
                return c4.enter_context(nc.sbuf_tensor(name, list(shape), dt))
            mask_sb = s4b("mask_sb", [128, 8 * 512], BF16)
            k_fm = [s4b("k_fm%d" % i, [128, S], BF16) for i in range(4)]
            v_sb = [s4b("v_sb%d" % i, [128, S], BF16) for i in range(4)]
            q_fm = [s4b("q_fm%d" % i, [128, NOWN], BF16) for i in range(4)]
            at = [s4b("at%d" % i, [128, 1024], BF16) for i in range(3)]
            atm = [s4b("atm%d" % i, [128, 1024], BF16) for i in range(2)]
            s0m2 = [s4b("s0m2_%d" % i, [128, 1], F32) for i in range(2)]
            e02 = [s4b("e02_%d" % i, [128, 1], BF16) for i in range(2)]
            mu_bf2 = [s4b("mu_bf2_%d" % i, [1, 1], BF16) for i in range(2)]
            negmu2 = [s4b("negmu2_%d" % i, [128, 1], F32) for i in range(2)]
            linv2 = [s4b("linv2_%d" % i, [1, 512], F32) for i in range(2)]
            lbc2 = [s4b("lbc2_%d" % i, [128, 512], F32) for i in range(2)]
            st_ps = [c4.enter_context(nc.psum_tensor("st_ps%d" % i, [128, 1024], F32))
                     for i in range(2)]
            av_ps2 = [c4.enter_context(nc.psum_tensor("av_ps%d" % i, [128, 512], F32))
                      for i in range(2)]
            lp_ps = [c4.enter_context(nc.psum_tensor("lp_ps%d" % i, [128, 512], F32))
                     for i in range(2)]

            cbuf = [s4b("cb0", [128, DFF], BF16), s4b("cb1", [128, DFF], BF16)]

            def p2_cast(g):
                ci = 0
                for wsrc, wdst in ((wg_d, wgb_c), (wu_d, wub_c)):
                    for fc in range(NFC):
                        if ci >= 2:
                            g.wait_ge(cWR.h, 16 * (ci - 1))
                        cLD.inc(g.dma_start(out=cbuf[ci % 2][:],
                                            in_=wsrc[fc * 128:(fc + 1) * 128, :]), 16)
                        g.wait_ge(cLD.h, 16 * (ci + 1))
                        cWR.inc(g.dma_start(
                            out=wdst[:, :, fc * 128:(fc + 1) * 128].rearrange(
                                "d p f -> p d f"),
                            in_=cbuf[ci % 2].ap().rearrange("p (d f) -> p d f", f=128)), 16)
                        ci += 1
                for dc in range(NDC):
                    if ci >= 2:
                        g.wait_ge(cWR.h, 16 * (ci - 1))
                    cLD.inc(g.dma_start(out=cbuf[ci % 2][:, 0:D],
                                        in_=wd_d[dc * 128:(dc + 1) * 128, :]), 16)
                    g.wait_ge(cLD.h, 16 * (ci + 1))
                    cWR.inc(g.dma_start(
                        out=wdb_c[dc].rearrange("q p f -> p q f"),
                        in_=cbuf[ci % 2][:, 0:D].rearrange("p (q f) -> p q f", f=512)), 16)
                    ci += 1
            block.gpsimd(p2_cast)
            CWR_ALL = cWR.n
            assert CWR_ALL == 16 * (2 * NFC + NDC)

            def p2_sy(sy):
                sy.dma_start(out=mask_sb.ap().rearrange("p (d j) -> p d j", j=512),
                             in_=mq.rearrange("d p j -> p d j")).then_inc(sCON.h, 16)
                for P in range(H // 2):
                    if P == 0:
                        for sc_ in (qWRp[0], qWRp[1], kWRp[0], kWRp[1], vWRp[0], vWRp[1]):
                            sy.wait_ge(sc_.h, sc_.n)
                    if P >= 2:
                        sy.wait_ge(aLB.h, 8 * (P - 1))
                    for h in (2 * P, 2 * P + 1):
                        ln4 = h % 4
                        hg, hh = h // 4, h % 4
                        aLDp[P % 2].inc(sy.dma_start(out=k_fm[ln4][:], in_=k_c[h]), 16)
                        aLDp[P % 2].inc(sy.dma_start(
                            out=v_sb[ln4][:],
                            in_=v_c[hg, :, :, :, hh * 128:(hh + 1) * 128].rearrange(
                                "w s p d -> p (w s) d")), 16)
                        aLDp[P % 2].inc(sy.dma_start(out=q_fm[ln4][:], in_=q_c[h]), 16)
            block.sync(p2_sy)
            sCON.n += 16
            SCON_MASK = sCON.n
            assert aLDp[0].n + aLDp[1].n == 48 * H

            def p2_pe(t):
                nst = nexp = nmsk = nkt = nlb = nlinv = navd = 0
                nE0 = nMUV = nMU = 0
                for P in range(H // 2):
                    heads = [(2 * P + i, (2 * P + i) % 4, i) for i in range(2)]
                    t.wait_ge(aLDp[P % 2].h, 96 * (P // 2 + 1))
                    t.wait_ge(aEXP.h, nexp)
                    for h, ln4, ln in heads:
                        aS0.inc(t.matmul(st_ps[ln][:, 0:128], q_fm[ln4][:, 0:128],
                                         k_fm[ln4][:, 0:128], start=True, stop=True))
                    t.wait_ge(aLINV.h, nlinv)
                    for h, ln4, ln in heads:
                        nE0 += 1
                        t.wait_ge(aE0.h, nE0)
                        aMUP.inc(t.matmul(lp_ps[ln][0:1, 0:1], ones_col[:], e02[ln][:],
                                          start=True, stop=True))
                    for h, ln4, ln in heads:
                        nMUV += 1
                        t.wait_ge(aMUV.h, nMUV)
                        aNBP.inc(t.matmul(st_ps[ln][:, 0:1], ones_row[:], mu_bf2[ln][:],
                                          start=True, stop=True))
                    t.wait_ge(aMU.h, 2 * (P + 1))
                    for qc in range(NQC):
                        ex = EX(qc)
                        pq = ex // 2
                        if navd:
                            t.wait_ge(aAVD.h, navd)

                        def emit_st(j, ln, ln4, qc=qc):
                            ps = st_ps[ln]
                            for half in range(2):
                                kt = 2 * j + half
                                ins = t.matmul(
                                    ps[:, half * 512:(half + 1) * 512],
                                    k_fm[ln4][:, kt * 128:(kt + 1) * 128],
                                    q_fm[ln4][:, qc * 512:(qc + 1) * 512],
                                    start=True, stop=True)
                            aST.inc(ins)

                        for h, ln4, ln in heads:
                            emit_st(0, ln, ln4)
                        for j in range(pq):
                            for h, ln4, ln in heads:
                                nexp += 1
                                d0 = 2 * j - (ex - 8)
                                if d0 >= 0:
                                    nmsk += 1
                                    t.wait_ge(aMSK.h, nmsk)
                                    src_t = atm[nmsk % 2]
                                else:
                                    t.wait_ge(aEXP.h, nexp)
                                    src_t = at[nexp % 3]
                                for half in range(2):
                                    kt = 2 * j + half
                                    t.matmul(av_ps2[ln][:],
                                             v_sb[ln4][:, kt * 128:(kt + 1) * 128],
                                             src_t[:, half * 512:(half + 1) * 512],
                                             start=(kt == 0), stop=(kt == ex - 1))
                                    ins = t.matmul(
                                        lp_ps[ln][0:1, :], ones_col[:],
                                        src_t[:, half * 512:(half + 1) * 512],
                                        start=(kt == 0), stop=(kt == ex - 1))
                                    aKT.inc(ins)
                                if j + 1 < pq:
                                    emit_st(j + 1, ln, ln4)
                        nkt += 2 * ex
                        for h, ln4, ln in heads:
                            nlinv += 1
                            t.wait_ge(aLINV.h, nlinv)
                            aLB.inc(t.matmul(lp_ps[ln][:, :], ones_row_f[:],
                                             linv2[ln][:], start=True, stop=True))
                        nlb += 2
                        navd += 2
                assert nkt == TOT_KT and nexp == TOT_KT // 2
            block.tensor(p2_pe)
            assert aST.n == TOT_KT // 2 and aKT.n == TOT_KT
            assert aLB.n == 2 * H * NQC // 2

            def p2_act(s):
                nst = nexp = nkt_c = nlb = 0
                for P in range(H // 2):
                    heads = [(2 * P + i, (2 * P + i) % 4, i) for i in range(2)]
                    for h, ln4, ln in heads:
                        s.wait_ge(aRM.h, 2 * P + ln + 1)
                        aE0.inc(s.activation(e02[ln][:], s0m2[ln][:], AF.Exp))
                    for h, ln4, ln in heads:
                        s.wait_ge(aMUP.h, 2 * P + ln + 1)
                        aMUV.inc(s.activation(mu_bf2[ln][:], lp_ps[ln][0:1, 0:1], AF.Ln))
                    for h, ln4, ln in heads:
                        s.wait_ge(aNBP.h, 2 * P + ln + 1)
                        aMU.inc(s.activation(negmu2[ln][:], st_ps[ln][:, 0:1],
                                             AF.Identity, scale=-1.0))
                    s.wait_ge(aMU.h, 2 * (P + 1))
                    for qc in range(NQC):
                        ex = EX(qc)
                        pq = ex // 2
                        for j in range(pq):
                            for h, ln4, ln in heads:
                                nst += 1
                                nexp += 1
                                s.wait_ge(aST.h, nst)
                                if nexp >= 4:
                                    s.wait_ge(aKT.h, 2 * (nexp - 3))
                                aEXP.inc(s.activation(at[nexp % 3][:], st_ps[ln][:],
                                                      AF.Exp, bias=negmu2[ln][:]))
                        for h, ln4, ln in heads:
                            nlb += 1
                            s.wait_ge(aLB.h, nlb)
                            aLBC.inc(s.copy(lbc2[ln][:], lp_ps[ln][:, :]))
            block.scalar(p2_act)
            assert aEXP.n == TOT_KT // 2

            def p2_vec(v):
                v.wait_ge(sCON.h, SCON_MASK)
                nexp = nmsk = nkt = nlinv = nlbc = navd = 0
                for P in range(H // 2):
                    heads = [(2 * P + i, (2 * P + i) % 4, i) for i in range(2)]
                    for h, ln4, ln in heads:
                        v.wait_ge(aS0.h, 2 * P + ln + 1)
                        aRM.inc(v.reduce_max(s0m2[ln][:], st_ps[ln][:, 0:128],
                                             axis=AX.X))
                    for qc in range(NQC):
                        ex = EX(qc)
                        pq = ex // 2
                        for j in range(pq):
                            for h, ln4, ln in heads:
                                nexp += 1
                                d0 = 2 * j - (ex - 8)
                                if d0 >= 0:
                                    v.wait_ge(aEXP.h, nexp)
                                    if nmsk >= 2:
                                        v.wait_ge(aKT.h, 2 * nexp - 4)
                                    nmsk += 1
                                    aMSK.inc(v.tensor_mul(
                                        atm[nmsk % 2][:], at[nexp % 3][:],
                                        mask_sb[:, d0 * 512:(d0 + 2) * 512]))
                        nkt += 2 * ex
                        for h, ln4, ln in heads:
                            v.wait_ge(aKT.h, nkt - 2 if ln == 0 else nkt)
                            nlinv += 1
                            aLINV.inc(v.reciprocal(linv2[ln][:], lp_ps[ln][0:1, :]))
                        for h, ln4, ln in heads:
                            nlbc += 1
                            v.wait_ge(aLBC.h, nlbc)
                            pDg.inc(v.tensor_mul(avn[:], av_ps2[ln][:], lbc2[ln][:]))
                            v.wait_ge(pDg.h, pDg.n)
                            h_abs = 2 * P + ln
                            navd += 1
                            aAVD.inc(v.tensor_scalar(
                                avcat[:, h_abs * NOWN + qc * 512: h_abs * NOWN + (qc + 1) * 512],
                                avn[:], bias_sb[:, 32 + h_abs:33 + h_abs], None,
                                ALU.add))
            block.vector(p2_vec)

            barrier()
            c4.close()

            with ExitStack() as c5:
                def s5b(name, shape, dt):
                    return c5.enter_context(nc.sbuf_tensor(name, list(shape), dt))
                wo_sb = s5b("wo_sb", [128, NFC * D], BF16)
                bo_bc = s5b("bo_bc", [128, D], F32)
                xq_t = [s5b("xq_t0", [128, D], F32), s5b("xq_t1", [128, D], F32)]
                r1t = [s5b("r1t0", [128, D], F32), s5b("r1t1", [128, D], F32)]
                xn2t = [s5b("xn2t0", [128, D], BF16), s5b("xn2t1", [128, D], BF16)]
                w3sq = s5b("w3sq", [128, D], F32)
                wo_ps = [c5.enter_context(nc.psum_tensor("wo_ps%d" % i, [128, 512], F32))
                         for i in range(4)]

                def p3_gp(g):
                    wLD.inc(g.dma_start(out=bo_bc[:],
                                        in_=bo_d[None, :].broadcast_to([128, D])), 16)
                    for hc in range(NFC):
                        wLD.inc(g.dma_start(
                            out=wo_sb[:, hc * D:(hc + 1) * D],
                            in_=wo_d[hc * 128:(hc + 1) * 128, :]), 16)
                block.gpsimd(p3_gp)
                assert wLD.n == 16 * 17

                def p3_sy(sy):
                    for tt in range(NT_OWN):
                        if tt >= 2:
                            sy.wait_ge(wEVA.h, 4 * (tt - 1))
                        wXQp[tt % 2].inc(sy.dma_start(
                            out=xq_t[tt % 2][:],
                            in_=xq[tt * 128:(tt + 1) * 128, :]), 16)
                        if tt >= 1:
                            sy.wait_ge(wAPL.h, tt)
                            wWRp[(tt - 1) % 2].inc(sy.dma_start(
                                out=r1_c[(tt - 1) * 128:tt * 128, :],
                                in_=r1t[(tt - 1) % 2][:]), 16)
                            wWRp[(tt - 1) % 2].inc(sy.dma_start(
                                out=xn2_c[:, (tt - 1) * 128:tt * 128, :].rearrange(
                                    "c t f -> t c f"),
                                in_=xn2t[(tt - 1) % 2].ap().rearrange(
                                    "p (c f) -> p c f", f=128)), 16)
                    sy.wait_ge(wAPL.h, NT_OWN)
                    wWRp[(NT_OWN - 1) % 2].inc(sy.dma_start(
                        out=r1_c[(NT_OWN - 1) * 128:NT_OWN * 128, :],
                        in_=r1t[(NT_OWN - 1) % 2][:]), 16)
                    wWRp[(NT_OWN - 1) % 2].inc(sy.dma_start(
                        out=xn2_c[:, (NT_OWN - 1) * 128:NT_OWN * 128, :].rearrange(
                            "c t f -> t c f"),
                        in_=xn2t[(NT_OWN - 1) % 2].ap().rearrange(
                            "p (c f) -> p c f", f=128)), 16)
                block.sync(p3_sy)
                assert wWRp[0].n + wWRp[1].n == 32 * NT_OWN

                def p3_pe(t):
                    t.wait_ge(wLD.h, 16 * 17)
                    t.wait_ge(aAVD.h, H * NQC)
                    for tt in range(NT_OWN):
                        for c4i in range(4):
                            if tt >= 1:
                                t.wait_ge(wEVA.h, 4 * (tt - 1) + c4i + 1)
                            for hc in range(NFC):
                                ins = t.matmul(
                                    wo_ps[c4i][:],
                                    avcat[:, hc * NOWN + tt * 128: hc * NOWN + (tt + 1) * 128],
                                    wo_sb[:, hc * D + c4i * 512: hc * D + (c4i + 1) * 512],
                                    start=(hc == 0), stop=(hc == NFC - 1))
                            wMM.inc(ins)
                block.tensor(p3_pe)
                assert wMM.n == 4 * NT_OWN

                def p3_act(s):
                    for tt in range(NT_OWN):
                        p = tt % 2
                        s.wait_ge(wEVA.h, 4 * (tt + 1))
                        wSq1.inc(s.activation(w3sq[:], r1t[p][:], AF.Square,
                                              accum_out=ssq1[:]))
                        s.wait_ge(wD1.h, tt + 1)
                        wS1.inc(s.activation(s1v[:], r1c[:], AF.Sqrt))
                block.scalar(p3_act)
                assert wSq1.n == NT_OWN and wS1.n == NT_OWN

                def p3_vec(v):
                    def g(ins):
                        wDg.inc(ins)
                        v.wait_ge(wDg.h, wDg.n)
                        return ins
                    for tt in range(NT_OWN):
                        p = tt % 2
                        v.wait_ge(wXQp[p].h, 16 * (tt // 2 + 1))
                        if tt >= 2:
                            v.wait_ge(wWRp[tt % 2].h, 32 * (tt // 2))
                        for c4i in range(4):
                            v.wait_ge(wMM.h, 4 * tt + c4i + 1)
                            sl = slice(c4i * 512, (c4i + 1) * 512)
                            g(v.tensor_add(tmp512[:], wo_ps[c4i][:], xq_t[p][:, sl]))
                            wEVA.inc(v.tensor_add(r1t[p][:, sl], tmp512[:],
                                                  bo_bc[:, sl]))
                        v.wait_ge(wEVA.h, 4 * (tt + 1))
                        g(v.reduce_sum(sx1[:], r1t[p][:], axis=AX.X))
                        g(v.tensor_scalar_mul(m1v[:], sx1[:], 1.0 / D))
                        v.wait_ge(wSq1.h, tt + 1)
                        g(v.tensor_scalar_mul(t1v[:], ssq1[:], 1.0 / D))
                        g(v.tensor_mul(msq[:], m1v[:], m1v[:]))
                        g(v.tensor_sub(v1v[:], t1v[:], msq[:]))
                        g(v.tensor_scalar_add(vpe[:], v1v[:], EPS))
                        wD1.inc(v.reciprocal(r1c[:], vpe[:]))
                        v.wait_ge(wS1.h, tt + 1)
                        g(v.tensor_mul(nm1[:], m1v[:], s1v[:]))
                        g(v.tensor_scalar_mul(nm1n[:], nm1[:], -1.0))
                        wAPL.inc(v.tensor_scalar(xn2t[p][:], r1t[p][:], s1v[:],
                                                 nm1n[:], ALU.mult, ALU.add))
                block.vector(p3_vec)
                assert wAPL.n == NT_OWN and wEVA.n == 4 * NT_OWN

                barrier()


        with ExitStack() as c6:
            def s6b(name, shape, dt):
                return c6.enter_context(nc.sbuf_tensor(name, list(shape), dt))
            xnt2 = s6b("xnt2", [128, NFC * FFN_W], BF16)
            hid = s6b("hid", [128, NDC * FFN_W], BF16)
            wgu = [s6b("wg0", [128, NFC * 128], BF16), s6b("wg1", [128, NFC * 128], BF16),
                   s6b("wu0", [128, NFC * 128], BF16), s6b("wu1", [128, NFC * 128], BF16)]
            sbg4 = [s6b("sbg4_0", [8, 512], BF16), s6b("sbg4_1", [8, 512], BF16)]
            sbu4 = [s6b("sbu4_0", [8, 512], BF16), s6b("sbu4_1", [8, 512], BF16)]
            bcg4 = [s6b("bcg4_0", [64, 512], BF16), s6b("bcg4_1", [64, 512], BF16)]
            bcu4 = [s6b("bcu4_0", [64, 512], BF16), s6b("bcu4_1", [64, 512], BF16)]
            wd_t = [s6b("wd0", [128, 512], BF16), s6b("wd1", [128, 512], BF16),
                    s6b("wd2", [128, 512], BF16)]
            gs_sb = s6b("gs_sb", [128, FFN_W], BF16)
            wr_sb = s6b("wr_sb", [128, NFC * E8], BF16)
            sag_sb = s6b("sag_sb", [128, NFC * RK], BF16)
            sau_sb = s6b("sau_sb", [128, NFC * RK], BF16)
            acg_sb = s6b("acg_sb", [128, NFC * ER], BF16)
            acu_sb = s6b("acu_sb", [128, NFC * ER], BF16)
            sad_sb = s6b("sad_sb", [128, NDC * RK], BF16)
            ad_sb = s6b("ad_sb", [128, NDC * ER], BF16)
            sbd_sb = s6b("sbd_sb", [8, D], BF16)
            bcd_sb = s6b("bcd_sb", [64, D], BF16)
            ident_sb = s6b("ident_sb", [128, 128], F32)
            rself_sb = s6b("rself_sb", [8, 64], F32)
            rsel_sb = s6b("rsel_sb", [8, 64], BF16)
            sg_sb = s6b("sg_sb", [128, E8], F32)
            bi_sb = s6b("bi_sb", [128, E8], F32)
            mk1 = s6b("mk1", [128, E8], F32)
            b2t = s6b("b2t", [128, E8], F32)
            mk2 = s6b("mk2", [128, E8], F32)
            mkt = s6b("mkt", [128, E8], F32)
            wsc = s6b("wsc", [128, E8], F32)
            m1e = s6b("m1e", [128, 1], F32)
            m2e = s6b("m2e", [128, 1], F32)
            sse = s6b("sse", [128, 1], F32)
            sie = s6b("sie", [128, 1], F32)
            w_tok = [s6b("w_tok0", [128, E8], F32), s6b("w_tok1", [128, E8], F32)]
            wt_sb = s6b("wt_sb", [8, FFN_W], F32)
            wbc_sb = s6b("wbc_sb", [64, FFN_W], F32)
            xawg_sb = s6b("xawg_sb", [64, FFN_W], BF16)
            xawu_sb = s6b("xawu_sb", [64, FFN_W], BF16)
            sxg_sb = s6b("sxg_sb", [8, FFN_W], BF16)
            sxu_sb = s6b("sxu_sb", [8, FFN_W], BF16)
            sxd_sb = s6b("sxd_sb", [8, FFN_W], BF16)
            hxw_sb = s6b("hxw_sb", [64, FFN_W], BF16)
            r1r = [s6b("r1r0", [128, 512], F32), s6b("r1r1", [128, 512], F32)]
            o_sb = [s6b("o_sb0", [128, 512], F32), s6b("o_sb1", [128, 512], F32)]

            def p4_const(g):
                g.dma_start(out=ident_sb[:], in_=ident_c.ap()).then_inc(fLD.h, 16)
                g.dma_start(out=rself_sb[:], in_=rsel_c.ap()).then_inc(fLD.h, 16)
                g.dma_start(out=wr_sb.ap().rearrange("p (c e) -> p c e", e=E8),
                            in_=wr_d.rearrange("(c p) e -> p c e", p=128)).then_inc(fLD.h, 16)
                g.dma_start(out=sag_sb.ap().rearrange("p (c r) -> p c r", r=RK),
                            in_=sag_d.rearrange("(c p) r -> p c r", p=128)).then_inc(fLD.h, 16)
                g.dma_start(out=sau_sb.ap().rearrange("p (c r) -> p c r", r=RK),
                            in_=sau_d.rearrange("(c p) r -> p c r", p=128)).then_inc(fLD.h, 16)
                for e in range(E8):
                    g.dma_start(
                        out=acg_sb.ap().rearrange("p (c e r) -> p c e r", e=E8, r=RK)[:, :, e, :],
                        in_=ag_d[e].rearrange("(c p) r -> p c r", p=128)).then_inc(fLD.h, 16)
                    g.dma_start(
                        out=acu_sb.ap().rearrange("p (c e r) -> p c e r", e=E8, r=RK)[:, :, e, :],
                        in_=au_d[e].rearrange("(c p) r -> p c r", p=128)).then_inc(fLD.h, 16)
                    g.dma_start(
                        out=ad_sb.ap().rearrange("p (c e r) -> p c e r", e=E8, r=RK)[:, :, e, :],
                        in_=ad_d[e].rearrange("(c p) r -> p c r", p=128)).then_inc(fLD.h, 16)
                g.dma_start(out=sad_sb.ap().rearrange("p (c r) -> p c r", r=RK),
                            in_=sad_d.rearrange("(c p) r -> p c r", p=128)).then_inc(fLD.h, 16)
                g.dma_start(out=sbd_sb[:], in_=sbd_d).then_inc(fLD.h, 16)
                g.dma_start(out=bcd_sb[:],
                            in_=bd_d.rearrange("e r d -> (e r) d")).then_inc(fLD.h, 16)
            block.gpsimd(p4_const)
            fLD.n += 16 * 32

            def p4_vec0(v):
                v.wait_ge(fLD.h, 32)
                fLD.inc(v.tensor_copy(rsel_sb[:], rself_sb[:]))
            block.vector(p4_vec0)
            FLD_ALL = fLD.n

            NI = FFN_W // 128

            for W in range(NWIN):
                with ExitStack() as cp:
                    rl_ps = cp.enter_context(nc.psum_tensor("rl_ps_w%d" % W, [128, 512], F32))
                    wt_ps = cp.enter_context(nc.psum_tensor("wt_ps_w%d" % W, [8, 512], F32))
                    wb_ps = cp.enter_context(nc.psum_tensor("wb_ps_w%d" % W, [64, FFN_W], F32))
                    xa_ps = cp.enter_context(nc.psum_tensor("xa_ps_w%d" % W, [64, FFN_W], F32))
                    sx_ps = cp.enter_context(nc.psum_tensor("sx_ps_w%d" % W, [8, FFN_W], F32))

                    fX2_0 = fX2.n
                    fRL0, fSG0, fTOP0, fWT0, fWTE0 = fRL.n, fSG.n, fTOP.n, fWT.n, fWTE.n
                    fWB0, fWBE0, fXA0, fXAW0, fSX0, fSXE0 = (
                        fWB.n, fWBE.n, fXA.n, fXAW.n, fSX.n, fSXE.n)

                    def p4a_sc(s, W=W, fX2_0=fX2_0):
                        if W == 0:
                            s.wait_ge(wWRp[0].h, wWRp[0].n)
                            s.wait_ge(wWRp[1].h, wWRp[1].n)
                        else:
                            s.wait_ge(fGMM.h, 86 * W)
                            s.wait_ge(fUMM.h, 86 * W)
                        for fc in range(NFC):
                            fX2.inc(s.dma_start_transpose(
                                xnt2[:, fc * FFN_W:(fc + 1) * FFN_W],
                                xn2_c[fc, W * FFN_W:(W + 1) * FFN_W, :]), 16)
                    block.scalar(p4a_sc)

                    def p4b_pe(t, W=W, fX2_0=fX2_0, fSG0=fSG0, fTOP0=fTOP0,
                              fWTE0=fWTE0):
                        t.wait_ge(fLD.h, FLD_ALL)
                        if W >= 1:
                            t.wait_ge(fOEV.h, 32 * W)
                        t.wait_ge(fX2.h, fX2_0 + 16 * 16)
                        for i in range(NI):
                            if i >= 1:
                                t.wait_ge(fSG.h, fSG0 + i)
                            for fc in range(NFC):
                                ins = t.matmul(
                                    rl_ps[:, 0:E8],
                                    xnt2[:, fc * FFN_W + i * 128: fc * FFN_W + (i + 1) * 128],
                                    wr_sb[:, fc * E8:(fc + 1) * E8],
                                    start=(fc == 0), stop=(fc == NFC - 1))
                            fRL.inc(ins)
                            t.wait_ge(fTOP.h, fTOP0 + i + 1)
                            if i >= 1:
                                t.wait_ge(fWTE.h, fWTE0 + i)
                            fWT.inc(t.transpose(wt_ps[:, 0:128], w_tok[i % 2][:],
                                                ident_sb[:]))
                    block.tensor(p4b_pe)

                    def p4b_act(s, fRL0=fRL0, fWT0=fWT0, fTOP0=fTOP0):
                        for i in range(NI):
                            if i >= 1:
                                s.wait_ge(fTOP.h, fTOP0 + i)
                            s.wait_ge(fRL.h, fRL0 + i + 1)
                            fSG.inc(s.activation(sg_sb[:], rl_ps[:, 0:E8], AF.Sigmoid))
                            s.wait_ge(fWT.h, fWT0 + i + 1)
                            fWTE.inc(s.copy(wt_sb[:, i * 128:(i + 1) * 128],
                                            wt_ps[:, 0:128]))
                    block.scalar(p4b_act)

                    def p4b_vec(v, fSG0=fSG0, fWT0=fWT0):
                        v.wait_ge(sCON.h, 32)
                        for i in range(NI):
                            v.wait_ge(fSG.h, fSG0 + i + 1)
                            if i >= 2:
                                v.wait_ge(fWT.h, fWT0 + i - 1)
                            def g(ins):
                                fDg.inc(ins)
                                v.wait_ge(fDg.h, fDg.n)
                                return ins
                            g(v.tensor_add(bi_sb[:], sg_sb[:], ebias_sb[:]))
                            g(v.reduce_max(m1e[:], bi_sb[:], axis=AX.X))
                            g(v.tensor_scalar(mk1[:], bi_sb[:], m1e[:], None, ALU.is_ge))
                            g(v.scalar_tensor_tensor(b2t[:], mk1[:], -1e30, bi_sb[:],
                                                     ALU.mult, ALU.add))
                            g(v.reduce_max(m2e[:], b2t[:], axis=AX.X))
                            g(v.tensor_scalar(mk2[:], b2t[:], m2e[:], None, ALU.is_ge))
                            g(v.tensor_add(mkt[:], mk1[:], mk2[:]))
                            g(v.tensor_mul(wsc[:], mkt[:], sg_sb[:]))
                            g(v.reduce_sum(sse[:], wsc[:], axis=AX.X))
                            g(v.tensor_scalar_add(sse[:], sse[:], 1e-8))
                            g(v.reciprocal(sie[:], sse[:]))
                            fTOP.inc(v.tensor_scalar(w_tok[i % 2][:], wsc[:], sie[:],
                                                     None, ALU.mult))
                    block.vector(p4b_vec)

                    def p4c_pe(t, fWTE0=fWTE0, fXAW0=fXAW0, fSXE0=fSXE0):
                        t.wait_ge(fWTE.h, fWTE0 + NI)
                        for half in range(2):
                            fWB.inc(t.matmul(wb_ps[:, half * 512:(half + 1) * 512],
                                             rself_sb[:],
                                             wt_sb[:, half * 512:(half + 1) * 512],
                                             start=True, stop=True))
                        for gi, asb in enumerate((acg_sb, acu_sb)):
                            if gi == 1:
                                t.wait_ge(fXAW.h, fXAW0 + 2)
                            for half in range(2):
                                for fc in range(NFC):
                                    ins = t.matmul(
                                        xa_ps[:, half * 512:(half + 1) * 512],
                                        asb[:, fc * ER:(fc + 1) * ER],
                                        xnt2[:, fc * FFN_W + half * 512: fc * FFN_W + half * 512 + 512],
                                        start=(fc == 0), stop=(fc == NFC - 1))
                                fXA.inc(ins)
                        for gi, ssb in enumerate((sag_sb, sau_sb)):
                            if gi == 1:
                                t.wait_ge(fSXE.h, fSXE0 + 2)
                            for half in range(2):
                                for fc in range(NFC):
                                    ins = t.matmul(
                                        sx_ps[:, half * 512:(half + 1) * 512],
                                        ssb[:, fc * RK:(fc + 1) * RK],
                                        xnt2[:, fc * FFN_W + half * 512: fc * FFN_W + half * 512 + 512],
                                        start=(fc == 0), stop=(fc == NFC - 1))
                                fSX.inc(ins)
                    block.tensor(p4c_pe)

                    def p4c_act(s, fWB0=fWB0, fSX0=fSX0):
                        s.wait_ge(fWB.h, fWB0 + 2)
                        fWBE.inc(s.copy(wbc_sb[:], wb_ps[:]))
                        for gi, dst in enumerate((sxg_sb, sxu_sb)):
                            for half in range(2):
                                s.wait_ge(fSX.h, fSX0 + gi * 2 + half + 1)
                                fSXE.inc(s.activation(
                                    dst[:, half * 512:(half + 1) * 512],
                                    sx_ps[:, half * 512:(half + 1) * 512],
                                    AF.Identity, scale=1.0 / RK))
                    block.scalar(p4c_act)

                    def p4c_vec(v, fWBE0=fWBE0, fXA0=fXA0):
                        v.wait_ge(fWBE.h, fWBE0 + 1)
                        for gi, dst in enumerate((xawg_sb, xawu_sb)):
                            for half in range(2):
                                v.wait_ge(fXA.h, fXA0 + gi * 2 + half + 1)
                                fXAW.inc(v.tensor_mul(
                                    dst[:, half * 512:(half + 1) * 512],
                                    xa_ps[:, half * 512:(half + 1) * 512],
                                    wbc_sb[:, half * 512:(half + 1) * 512]))
                    block.vector(p4c_vec)

                with ExitStack() as cq:
                    g_ps = cq.enter_context(nc.psum_tensor("g_ps_w%d" % W, [128, FFN_W], F32))
                    u_ps = cq.enter_context(nc.psum_tensor("u_ps_w%d" % W, [128, FFN_W], F32))
                    sxd_ps = cq.enter_context(nc.psum_tensor("sxd_ps_w%d" % W, [8, FFN_W], F32))
                    hxd_ps = cq.enter_context(nc.psum_tensor("hxd_ps_w%d" % W, [64, FFN_W], F32))

                    fGMM0, fUMM0, fSIL0, fUCP0, fHID0 = (
                        fGMM.n, fUMM.n, fSIL.n, fUCP.n, fHID.n)
                    fWGUp0 = [fWGUp[0].n, fWGUp[1].n]
                    fSBp0 = [fSBp[0].n, fSBp[1].n]

                    def p4d_gp(g, fGMM0=fGMM0, fUMM0=fUMM0, W=W):
                        for dc in range(NDC):
                            if dc >= 2:
                                tgt = fGMM0 + 2 * (dc - 1)
                            else:
                                tgt = fGMM0 if W > 0 else 0
                            if tgt > 0:
                                g.wait_ge(fGMM.h, tgt)
                                g.wait_ge(fUMM.h, tgt)
                            if W == 0 and dc == 0:
                                g.wait_ge(cWR.h, CWR_ALL)
                            if dc % 4 == 0:
                                dc4 = dc // 4
                                f0 = dc4 * 512
                                f1 = min(f0 + 512, DFF)
                                spb = fSBp[dc4 % 2]
                                spb.inc(g.dma_start(out=sbg4[dc4 % 2][:, 0:f1 - f0],
                                                    in_=sbg_d[:, f0:f1]), 16)
                                spb.inc(g.dma_start(out=sbu4[dc4 % 2][:, 0:f1 - f0],
                                                    in_=sbu_d[:, f0:f1]), 16)
                                spb.inc(g.dma_start(
                                    out=bcg4[dc4 % 2][:, 0:f1 - f0],
                                    in_=bg_d[:, :, f0:f1].rearrange("e r f -> (e r) f")), 16)
                                spb.inc(g.dma_start(
                                    out=bcu4[dc4 % 2][:, 0:f1 - f0],
                                    in_=bu_d[:, :, f0:f1].rearrange("e r f -> (e r) f")), 16)
                            sp = fWGUp[dc % 2]
                            sp.inc(g.dma_start(out=wgu[dc % 2][:], in_=wgb_c[dc]), 16)
                            sp.inc(g.dma_start(out=wgu[2 + dc % 2][:], in_=wub_c[dc]), 16)
                    block.gpsimd(p4d_gp)

                    def p4d_pe(t, W=W, fWGUp0=fWGUp0, fSBp0=fSBp0, fSIL0=fSIL0,
                              fUCP0=fUCP0, fHID0=fHID0, fXAW0=fXAW0, fSXE0=fSXE0):
                        t.wait_ge(fXAW.h, fXAW0 + 4)
                        t.wait_ge(fSXE.h, fSXE0 + 4)
                        for dc in range(NDC):
                            t.wait_ge(fWGUp[dc % 2].h, fWGUp0[dc % 2] + 32 * (dc // 2 + 1))
                            if dc % 4 == 0:
                                dc4 = dc // 4
                                t.wait_ge(fSBp[dc4 % 2].h,
                                          fSBp0[dc4 % 2] + 64 * (dc4 // 2 + 1))
                            for half in range(2):
                                if dc >= 1:
                                    t.wait_ge(fSIL.h, fSIL0 + 2 * (dc - 1) + half + 1)
                                sl = slice(half * 512, (half + 1) * 512)
                                for fc in range(NFC):
                                    t.matmul(g_ps[:, sl],
                                             wgu[dc % 2][:, fc * 128:(fc + 1) * 128],
                                             xnt2[:, fc * FFN_W + half * 512: fc * FFN_W + half * 512 + 512],
                                             start=(fc == 0), stop=False)
                                t.matmul(g_ps[:, sl],
                                         sbg4[(dc // 4) % 2][:, (dc % 4) * 128:(dc % 4 + 1) * 128],
                                         sxg_sb[:, sl], start=False, stop=False)
                                fGMM.inc(t.matmul(
                                    g_ps[:, sl],
                                    bcg4[(dc // 4) % 2][:, (dc % 4) * 128:(dc % 4 + 1) * 128],
                                    xawg_sb[:, sl], start=False, stop=True))
                            for half in range(2):
                                if dc >= 1:
                                    t.wait_ge((fUCP if half == 0 else fHID).h,
                                              (fUCP0 if half == 0 else fHID0) + dc)
                                sl = slice(half * 512, (half + 1) * 512)
                                for fc in range(NFC):
                                    t.matmul(u_ps[:, sl],
                                             wgu[2 + dc % 2][:, fc * 128:(fc + 1) * 128],
                                             xnt2[:, fc * FFN_W + half * 512: fc * FFN_W + half * 512 + 512],
                                             start=(fc == 0), stop=False)
                                t.matmul(u_ps[:, sl],
                                         sbu4[(dc // 4) % 2][:, (dc % 4) * 128:(dc % 4 + 1) * 128],
                                         sxu_sb[:, sl], start=False, stop=False)
                                fUMM.inc(t.matmul(
                                    u_ps[:, sl],
                                    bcu4[(dc // 4) % 2][:, (dc % 4) * 128:(dc % 4 + 1) * 128],
                                    xawu_sb[:, sl], start=False, stop=True))
                            for dcx in ([dc - 1] if dc >= 1 else []) + ([NDC - 1] if dc == NDC - 1 else []):
                                t.wait_ge(fHID.h, fHID0 + dcx + 1)
                                for half in range(2):
                                    sl = slice(half * 512, (half + 1) * 512)
                                    t.matmul(sxd_ps[:, sl],
                                             sad_sb[:, dcx * RK:(dcx + 1) * RK],
                                             hid[:, dcx * FFN_W + half * 512: dcx * FFN_W + half * 512 + 512],
                                             start=(dcx == 0), stop=(dcx == NDC - 1))
                                    ins = t.matmul(hxd_ps[:, sl],
                                                   ad_sb[:, dcx * ER:(dcx + 1) * ER],
                                                   hid[:, dcx * FFN_W + half * 512: dcx * FFN_W + half * 512 + 512],
                                                   start=(dcx == 0), stop=(dcx == NDC - 1))
                                if dcx == NDC - 1:
                                    fHX.inc(ins)
                    block.tensor(p4d_pe)

                    def p4d_act(s, W=W, fGMM0=fGMM0, fHID0=fHID0, fUCP0=fUCP0):
                        for dc in range(NDC):
                            for half in range(2):
                                s.wait_ge(fGMM.h, fGMM0 + 2 * dc + half + 1)
                                if dc >= 1:
                                    s.wait_ge((fUCP if half == 0 else fHID).h,
                                              (fUCP0 if half == 0 else fHID0) + dc)
                                sl = slice(half * 512, (half + 1) * 512)
                                fSIL.inc(s.activation(gs_sb[:, sl], g_ps[:, sl],
                                                      AF.Silu))
                        s.wait_ge(fHX.h, W + 1)
                        fSXD.inc(s.activation(sxd_sb[:], sxd_ps[:], AF.Identity,
                                              scale=1.0 / RK))
                    block.scalar(p4d_act)

                    def p4d_vec(v, W=W, fUMM0=fUMM0, fSIL0=fSIL0, fHID0=fHID0,
                                fWBE0=fWBE0, fUCP0=fUCP0):
                        for dc in range(NDC):
                            v.wait_ge(fUMM.h, fUMM0 + 2 * dc + 2)
                            v.wait_ge(fSIL.h, fSIL0 + 2 * dc + 2)
                            fUCP.inc(v.tensor_mul(
                                hid[:, dc * FFN_W:dc * FFN_W + 512],
                                gs_sb[:, 0:512], u_ps[:, 0:512]))
                            fHID.inc(v.tensor_mul(
                                hid[:, dc * FFN_W + 512:(dc + 1) * FFN_W],
                                gs_sb[:, 512:1024], u_ps[:, 512:1024]))
                        v.wait_ge(fHX.h, W + 1)
                        v.wait_ge(fWBE.h, fWBE0 + 1)
                        fHXW.inc(v.tensor_mul(hxw_sb[:], hxd_ps[:], wbc_sb[:]))
                    block.vector(p4d_vec)

                    if dbg:
                        def p4d_dump(sy, W=W, fHID0=fHID0):
                            for dc in range(NDC):
                                sy.wait_ge(fHID.h, fHID0 + dc + 1)
                                sy.dma_start(out=hid_c[W, dc],
                                             in_=hid[:, dc * FFN_W:(dc + 1) * FFN_W]
                                             ).then_inc(cLD.h, 16)
                        block.sync(p4d_dump)

                with ExitStack() as cr:
                    dn_ps = [cr.enter_context(nc.psum_tensor("dn_ps%d_w%d" % (i, W),
                                                             [128, 512], F32))
                             for i in range(8)]
                    fWDU0, fDN0, fOEV0 = (fWDU.n, fDN.n, fOEV.n)
                    fWDp0 = [fWDp[0].n, fWDp[1].n, fWDp[2].n]
                    fR1p0 = [fR1p[0].n, fR1p[1].n]
                    fOUTp0 = [fOUTp[0].n, fOUTp[1].n]

                    def p4e_wd(s, fWDU0=fWDU0, W=W):
                        if W == 0:
                            s.wait_ge(cWR.h, CWR_ALL)
                        for d4 in range(4):
                            for dc in range(NDC):
                                didx = d4 * NDC + dc
                                tgt = fWDU0 + didx - 2
                                if tgt > 0:
                                    s.wait_ge(fWDU.h, tgt)
                                fWDp[didx % 3].inc(s.dma_start(
                                    out=wd_t[didx % 3][:],
                                    in_=wdb_c[dc, d4]), 16)
                    block.scalar(p4e_wd)

                    def p4e_pe(t, W=W, fWDp0=fWDp0, fOEV0=fOEV0):
                        t.wait_ge(fSXD.h, W + 1)
                        t.wait_ge(fHXW.h, W + 1)
                        for d4 in range(4):
                            if d4 >= 1:
                                t.wait_ge(fOEV.h, fOEV0 + 8 * d4)
                            for dc in range(NDC):
                                didx = d4 * NDC + dc
                                t.wait_ge(fWDp[didx % 3].h,
                                          fWDp0[didx % 3] + 16 * (didx // 3 + 1))
                                for tt in range(8):
                                    ins = t.matmul(
                                        dn_ps[tt][:],
                                        hid[:, dc * FFN_W + tt * 128: dc * FFN_W + (tt + 1) * 128],
                                        wd_t[didx % 3][:],
                                        start=(dc == 0), stop=False,
                                        skip_group_check=True)
                                fWDU.inc(ins)
                            for tt in range(8):
                                t.matmul(dn_ps[tt][:], sxd_sb[:, tt * 128:(tt + 1) * 128],
                                         sbd_sb[:, d4 * 512:(d4 + 1) * 512],
                                         start=False, stop=False, skip_group_check=True)
                                fDN.inc(t.matmul(
                                    dn_ps[tt][:], hxw_sb[:, tt * 128:(tt + 1) * 128],
                                    bcd_sb[:, d4 * 512:(d4 + 1) * 512],
                                    start=False, stop=True, skip_group_check=True))
                    block.tensor(p4e_pe)

                    def p4e_sy(sy, W=W, fOEV0=fOEV0):
                        sy.wait_ge(wWRp[0].h, wWRp[0].n)
                        sy.wait_ge(wWRp[1].h, wWRp[1].n)
                        for d4 in range(4):
                            for tt in range(8):
                                i = d4 * 8 + tt
                                if i >= 2:
                                    sy.wait_ge(fOEV.h, fOEV0 + i - 1)
                                fR1p[i % 2].inc(sy.dma_start(
                                    out=r1r[i % 2][:],
                                    in_=r1_c[W * FFN_W + tt * 128: W * FFN_W + (tt + 1) * 128,
                                             d4 * 512:(d4 + 1) * 512]), 16)
                                sy.wait_ge(fOEV.h, fOEV0 + i + 1)
                                fOUTp[i % 2].inc(sy.dma_start(
                                    out=out_d[W * FFN_W + tt * 128: W * FFN_W + (tt + 1) * 128,
                                              d4 * 512:(d4 + 1) * 512],
                                    in_=o_sb[i % 2][:]), 16)
                    block.sync(p4e_sy)

                    def p4e_vec(v, W=W, fDN0=fDN0, fR1p0=fR1p0, fOEV0=fOEV0,
                                fOUTp0=fOUTp0):
                        for d4 in range(4):
                            for tt in range(8):
                                i = d4 * 8 + tt
                                v.wait_ge(fDN.h, fDN0 + d4 * 8 + tt + 1)
                                v.wait_ge(fR1p[i % 2].h, fR1p0[i % 2] + 16 * (i // 2 + 1))
                                if i >= 2:
                                    v.wait_ge(fOUTp[i % 2].h, fOUTp0[i % 2] + 16 * (i // 2))
                                fOEV.inc(v.tensor_add(o_sb[i % 2][:], dn_ps[tt][:],
                                                      r1r[i % 2][:]))
                    block.vector(p4e_vec)

            def p_final(sy):
                sy.wait_ge(fOUTp[0].h, fOUTp[0].n)
                sy.wait_ge(fOUTp[1].h, fOUTp[1].n)
            block.sync(p_final)
            assert fOUTp[0].n + fOUTp[1].n == 16 * 64

    return nc


def _make_masks(r):
    m = np.zeros((8, 128, 512), np.float32)
    tri = np.triu(np.ones((128, 128), np.float32))
    for d in range(8):
        for mcol in range(4):
            delta = d - 2 * mcol - r
            blk = m[d][:, mcol * 128:(mcol + 1) * 128]
            if delta < 0:
                blk[:] = 1.0
            elif delta == 0:
                blk[:] = tri
    return m.astype(ml_dtypes.bfloat16)


def kernel(**inputs):
    if "nc" not in _CACHE:
        _CACHE["nc"] = build_nc()
    nc = _CACHE["nc"]

    X = np.ascontiguousarray(inputs["L0_attn_X"], dtype=np.float32)

    def f32(k):
        return np.ascontiguousarray(inputs[k], dtype=np.float32)

    weights = {
        "ln_g": f32("L0_attn_ln_g"), "ln_b": f32("L0_attn_ln_b"),
        "wq": f32("L0_attn_Wq"), "wk": f32("L0_attn_Wk"),
        "wv": f32("L0_attn_Wv"), "wo": f32("L0_attn_Wo"),
        "bo": f32("L0_attn_bo"),
        "w_router": f32("L0_ffn_W_router"),
        "w_gate": f32("L0_ffn_W_gate"), "w_up": f32("L0_ffn_W_up"),
        "w_down": f32("L0_ffn_W_down"),
        "sa_gate": f32("L0_ffn_sA_gate"), "sb_gate": f32("L0_ffn_sB_gate"),
        "sa_up": f32("L0_ffn_sA_up"), "sb_up": f32("L0_ffn_sB_up"),
        "sa_down": f32("L0_ffn_sA_down"), "sb_down": f32("L0_ffn_sB_down"),
        "a_gate": f32("L0_ffn_A_gate"), "b_gate": f32("L0_ffn_B_gate"),
        "a_up": f32("L0_ffn_A_up"), "b_up": f32("L0_ffn_B_up"),
        "a_down": f32("L0_ffn_A_down"), "b_down": f32("L0_ffn_B_down"),
    }
    bias_pp = np.zeros((128, 64), np.float32)
    bias_pp[:, 0:16] = f32("L0_attn_bq").reshape(16, 128).T
    bias_pp[:, 16:32] = f32("L0_attn_bk").reshape(16, 128).T
    bias_pp[:, 32:48] = f32("L0_attn_bv").reshape(16, 128).T
    ebias = np.ascontiguousarray(
        np.broadcast_to(f32("L0_ffn_expert_bias"), (128, 8)))
    masks = [_make_masks(0), _make_masks(1)]

    in_maps = []
    for c in range(8):
        b, r = c // 2, c % 2
        m = dict(weights)
        m["xb"] = X[b]
        m["xq"] = np.ascontiguousarray(
            X[b].reshape(32, 128, 2048)[r::2].reshape(2048, 2048))
        m["mq"] = masks[r]
        m["bias_pp"] = bias_pp
        m["ebias_bc"] = ebias
        in_maps.append(m)

    res = run_bass_kernel_spmd(nc, in_maps, core_ids=list(range(8)))
    out = np.empty((4, 4096, 2048), np.float32)
    for c in range(8):
        b, r = c // 2, c % 2
        out[b].reshape(32, 128, 2048)[r::2] = \
            res.results[c]["out"].reshape(16, 128, 2048)
    return out


if __name__ == "__main__":
    nc = build_nc()
    print("build ok; instructions:", nc.n_instructions())


# version 23
# speedup vs baseline: 1.0784x; 1.0598x over previous
import math
import numpy as np
import ml_dtypes
from contextlib import ExitStack

import concourse.bass as bass
import concourse.mybir as mybir
from concourse.bass_utils import run_bass_kernel_spmd

F32 = mybir.dt.float32
BF16 = mybir.dt.bfloat16
AF = mybir.ActivationFunctionType
ALU = mybir.AluOpType
AX = mybir.AxisListType

D = 2048
NFC = 16
H = 16
DH = 128
S = 4096
NT_ALL = 32
NOWN = 2048
NT_OWN = 16
NQC = 4
DFF = 5504
NDC = 43
E8 = 8
RK = 8
ER = 64
SCL = 1.0 / math.sqrt(128.0)
EPS = 1e-5
FFN_W = 1024
NWIN = 2

_CACHE = {}


class SC:
    def __init__(self, nc, ctx, name):
        self.h = ctx.enter_context(nc.semaphore(name))
        self.n = 0

    def inc(self, instr, k=1):
        instr.then_inc(self.h, k)
        self.n += k
        return instr


def build_nc(dbg=False):
    nc = bass.Bass("TRN2", target_bir_lowering=False)

    def din(name, shape, dt=F32):
        return nc.dram_tensor(name, list(shape), dt, kind="ExternalInput").ap()

    xb = din("xb", [S, D])
    xq = din("xq", [NOWN, D])
    mq = din("mq", [8, 128, 512], BF16)
    bias_pp = din("bias_pp", [128, 64])
    ebias_bc = din("ebias_bc", [128, E8])
    ln_g = din("ln_g", [D])
    ln_b = din("ln_b", [D])
    wq_d = din("wq", [D, D])
    wk_d = din("wk", [D, D])
    wv_d = din("wv", [D, D])
    wo_d = din("wo", [D, D])
    bo_d = din("bo", [D])
    wr_d = din("w_router", [D, E8])
    wg_d = din("w_gate", [D, DFF])
    wu_d = din("w_up", [D, DFF])
    wd_d = din("w_down", [DFF, D])
    sag_d = din("sa_gate", [D, RK])
    sbg_d = din("sb_gate", [RK, DFF])
    sau_d = din("sa_up", [D, RK])
    sbu_d = din("sb_up", [RK, DFF])
    sad_d = din("sa_down", [DFF, RK])
    sbd_d = din("sb_down", [RK, D])
    ag_d = din("a_gate", [E8, D, RK])
    bg_d = din("b_gate", [E8, RK, DFF])
    au_d = din("a_up", [E8, D, RK])
    bu_d = din("b_up", [E8, RK, DFF])
    ad_d = din("a_down", [E8, DFF, RK])
    bd_d = din("b_down", [E8, RK, D])

    out_d = nc.dram_tensor("out", [NOWN, D], F32, kind="ExternalOutput").ap()

    skind = "ExternalOutput" if dbg else "Internal"
    xn_c = nc.dram_tensor("xn_c", [NFC, S, 128], BF16, kind=skind).ap()
    xnq_c = nc.dram_tensor("xnq_c", [NFC, NOWN, 128], BF16, kind=skind).ap()
    q_c = nc.dram_tensor("q_c", [H, 128, NOWN], BF16, kind=skind).ap()
    k_c = nc.dram_tensor("k_c", [H, 128, S], BF16, kind=skind).ap()
    v_c = nc.dram_tensor("v_c", [4, 8, 4, 128, 512], BF16, kind=skind).ap()
    r1_c = nc.dram_tensor("r1_c", [NOWN, D], F32, kind=skind).ap()
    hid_c = nc.dram_tensor("hid_c", [NWIN, NDC, 128, FFN_W], BF16, kind=skind).ap() if dbg else None
    wgb_c = nc.dram_tensor("wgb_c", [NDC, 128, NFC * 128], BF16, kind=skind).ap()
    wub_c = nc.dram_tensor("wub_c", [NDC, 128, NFC * 128], BF16, kind=skind).ap()
    wdb_c = nc.dram_tensor("wdb_c", [NDC, 4, 128, 512], BF16, kind=skind).ap()
    xn2_c = nc.dram_tensor("xn2_c", [NFC, NOWN, 128], BF16, kind=skind).ap()

    ident_c = nc.inline_tensor(np.eye(128, dtype=np.float32), "identf")
    rsel_np = np.zeros((9, 72), np.float32)
    for e in range(8):
        rsel_np[e, e * 8:(e + 1) * 8] = 1.0 / RK
    rsel_np[8, 64:72] = 1.0 / RK
    rsel_c = nc.inline_tensor(rsel_np, "rself")

    ctx = ExitStack()
    with ctx:
        def sb(name, shape, dt):
            return ctx.enter_context(nc.sbuf_tensor(name, list(shape), dt))

        ones_col = sb("ones_col", [128, 1], BF16)
        ones_row = sb("ones_row", [1, 128], BF16)
        ones_row_f = sb("ones_row_f", [1, 128], F32)
        bias_sb = sb("bias_sb", [128, 64], F32)
        bqs = sb("bqs", [128, 16], F32)
        ebias_sb = sb("ebias_sb", [128, E8], F32)
        negmu = sb("negmu", [128, 1], F32)
        mu_bf = sb("mu_bf", [1, 1], BF16)
        s0m = sb("s0m", [128, 1], F32)
        e0 = sb("e0", [128, 1], BF16)
        linv = sb("linv", [1, 512], F32)
        lbc = sb("lbc", [128, 512], F32)
        avn = sb("avn", [128, 512], F32)
        tmp512 = sb("tmp512", [128, 512], F32)
        sx1 = sb("sx1", [128, 1], F32)
        ssq1 = sb("ssq1", [128, 1], F32)
        sy2 = sb("sy2", [128, 1], F32)
        ssq2 = sb("ssq2", [128, 1], F32)
        m1v = sb("m1v", [128, 1], F32)
        t1v = sb("t1v", [128, 1], F32)
        msq = sb("msq", [128, 1], F32)
        v1v = sb("v1v", [128, 1], F32)
        vpe = sb("vpe", [128, 1], F32)
        r1c = sb("r1c", [128, 1], F32)
        r2c = sb("r2c", [128, 1], F32)
        s1v = sb("s1v", [128, 1], F32)
        s2v = sb("s2v", [128, 1], F32)
        nm1 = sb("nm1", [128, 1], F32)
        nm1n = sb("nm1n", [128, 1], F32)
        nm2 = sb("nm2", [128, 1], F32)
        nm2n = sb("nm2n", [128, 1], F32)
        mu2v = sb("mu2v", [128, 1], F32)
        ssq1p = [sb("ssq1p%d" % i, [128, 1], F32) for i in range(2)]
        sy2p = [sb("sy2p%d" % i, [128, 1], F32) for i in range(2)]
        m1p = [sb("m1p%d" % i, [128, 1], F32) for i in range(2)]
        r1p = [sb("r1p%d" % i, [128, 1], F32) for i in range(2)]
        r2p = [sb("r2p%d" % i, [128, 1], F32) for i in range(2)]
        s1p = [sb("s1p%d" % i, [128, 1], F32) for i in range(2)]
        s2p = [sb("s2p%d" % i, [128, 1], F32) for i in range(2)]
        nm1p = [sb("nm1p%d" % i, [128, 1], F32) for i in range(2)]
        nm2p = [sb("nm2p%d" % i, [128, 1], F32) for i in range(2)]
        mu2p = [sb("mu2p%d" % i, [128, 1], F32) for i in range(2)]

        def S_(name):
            return SC(nc, ctx, name)

        sCON = S_("sCON")
        sBAR = S_("sBAR")
        pSq1 = S_("pSq1")
        pS1 = S_("pS1")
        pSq2 = S_("pSq2")
        pS2 = S_("pS2")
        pD1 = S_("pD1")
        pD3 = S_("pD3")
        pD4 = S_("pD4")
        pDg = S_("pDg")
        pD6 = S_("pD6")
        pG = S_("pG")
        qWL = S_("qWL")
        qX = S_("qX")
        qMM = S_("qMM")
        qEV = S_("qEV")
        kX = S_("kX")
        kKmm = S_("kKmm")
        kVmm = S_("kVmm")
        kKev = S_("kKev")
        kVev = S_("kVev")
        aHD = S_("aHD")
        aS0 = S_("aS0")
        aRM = S_("aRM")
        aE0 = S_("aE0")
        aMUP = S_("aMUP")
        aMUV = S_("aMUV")
        aNBP = S_("aNBP")
        aMU = S_("aMU")
        aST = S_("aST")
        aEXP = S_("aEXP")
        aMSK = S_("aMSK")
        aKT = S_("aKT")
        aLINV = S_("aLINV")
        aLB = S_("aLB")
        aLBC = S_("aLBC")
        aAVD = S_("aAVD")
        wLD = S_("wLD")
        wMM = S_("wMM")
        wEVA = S_("wEVA")
        wSq1 = S_("wSq1")
        wS1 = S_("wS1")
        wD1 = S_("wD1")
        wDg = S_("wDg")
        wAPL = S_("wAPL")
        fLD = S_("fLD")
        fX2 = S_("fX2")
        fRL = S_("fRL")
        fSG = S_("fSG")
        fTOP = S_("fTOP")
        fWT = S_("fWT")
        fWTE = S_("fWTE")
        fWB = S_("fWB")
        fWBE = S_("fWBE")
        fXA = S_("fXA")
        fXAW = S_("fXAW")
        fSX = S_("fSX")
        fSXE = S_("fSXE")
        fGMM = S_("fGMM")
        fUMM = S_("fUMM")
        fSIL = S_("fSIL")
        fUCP = S_("fUCP")
        fHID = S_("fHID")
        fHX = S_("fHX")
        fSXD = S_("fSXD")
        fHXW = S_("fHXW")
        fWDU = S_("fWDU")
        fDN = S_("fDN")
        fOEV = S_("fOEV")
        fDg = S_("fDg")
        cLD = S_("cLD")
        cWR = S_("cWR")
        pXp = [S_("pXp0"), S_("pXp1")]
        pWp = [S_("pWp0"), S_("pWp1")]
        qWRp = [S_("qWRp0"), S_("qWRp1")]
        kWRp = [S_("kWRp0"), S_("kWRp1")]
        vWRp = [S_("vWRp0"), S_("vWRp1")]
        aLDp = [S_("aLDp0"), S_("aLDp1")]
        wXQp = [S_("wXQp0"), S_("wXQp1")]
        wWRp = [S_("wWRp0"), S_("wWRp1")]
        fWGUp = [S_("fWGUp0"), S_("fWGUp1")]
        fWDp = [S_("fWDp0"), S_("fWDp1"), S_("fWDp2")]
        fR1p = [S_("fR1p0"), S_("fR1p1")]
        fOUTp = [S_("fOUTp0"), S_("fOUTp1")]
        fSBp = [S_("fSBp0"), S_("fSBp1")]

        block = ctx.enter_context(nc.Block())
        nbar = [0]

        def barrier():
            nbar[0] += 1
            tgt = 5 * nbar[0]

            def f(eng):
                eng.drain()
                eng.sem_inc(sBAR.h, 1)
                eng.wait_ge(sBAR.h, tgt)
            block.sync(f)
            block.gpsimd(f)
            block.tensor(f)
            block.vector(f)
            block.scalar(f)
            sBAR.n = tgt

        def p0_sync(sy):
            sy.dma_start(out=bias_sb[:], in_=bias_pp).then_inc(sCON.h, 16)
            sy.dma_start(out=ebias_sb[:], in_=ebias_bc).then_inc(sCON.h, 16)
        block.sync(p0_sync)
        sCON.n += 32

        def p0_vec(v):
            v.memset(ones_col[:], 1.0)
            v.memset(ones_row[:], 1.0)
            v.memset(ones_row_f[:], 1.0)
            v.wait_ge(sCON.h, 32)
            v.tensor_scalar_mul(bqs[:], bias_sb[:, 0:16], SCL).then_inc(sCON.h, 1)
        block.vector(p0_vec)
        sCON.n += 1

        NTT = NT_ALL + NT_OWN

        with ExitStack() as c1:
            def s1b(name, shape, dt):
                return c1.enter_context(nc.sbuf_tensor(name, list(shape), dt))
            xt = [s1b("p1_x0", [128, D], F32), s1b("p1_x1", [128, D], F32)]
            yb = s1b("p1_y", [128, D], F32)
            sqscr = s1b("p1_sq", [128, D], F32)
            ttb = [s1b("p1_t0", [128, D], F32), s1b("p1_t1", [128, D], F32)]
            g1b = s1b("p1_g1", [128, D], F32)
            xnb = [s1b("p1_xn0", [128, D], BF16), s1b("p1_xn1", [128, D], BF16)]
            gam = s1b("p1_gam", [128, D], F32)
            bet = s1b("p1_bet", [128, D], F32)

            def src_ap(t):
                if t < NT_ALL:
                    return xb[t * 128:(t + 1) * 128, :]
                tt = t - NT_ALL
                return xq[tt * 128:(tt + 1) * 128, :]

            def dst_ap(t):
                if t < NT_ALL:
                    return xn_c[:, t * 128:(t + 1) * 128, :].rearrange("c t f -> t c f")
                tt = t - NT_ALL
                return xnq_c[:, tt * 128:(tt + 1) * 128, :].rearrange("c t f -> t c f")

            def p1_sync(sy):
                sy.dma_start(out=gam[:], in_=ln_g[None, :].broadcast_to([128, D])).then_inc(sCON.h, 16)
                sy.dma_start(out=bet[:], in_=ln_b[None, :].broadcast_to([128, D])).then_inc(sCON.h, 16)
                for t in range(NTT):
                    if t >= 2:
                        sy.wait_ge(pD3.h, t - 1)
                    pXp[t % 2].inc(sy.dma_start(out=xt[t % 2][:], in_=src_ap(t)), 16)
                    if t >= 1:
                        sy.wait_ge(pG.h, t)
                        pWp[(t - 1) % 2].inc(sy.dma_start(
                            out=dst_ap(t - 1),
                            in_=xnb[(t - 1) % 2].ap().rearrange("p (c f) -> p c f", f=128)), 16)
                sy.wait_ge(pG.h, NTT)
                pWp[(NTT - 1) % 2].inc(sy.dma_start(
                    out=dst_ap(NTT - 1),
                    in_=xnb[(NTT - 1) % 2].ap().rearrange("p (c f) -> p c f", f=128)), 16)
            block.sync(p1_sync)
            sCON.n += 32

            def p1_act(s):
                for t in range(NTT):
                    p = t % 2
                    s.wait_ge(pXp[p].h, 16 * (t // 2 + 1))
                    pSq1.inc(s.activation(sqscr[:], xt[p][:], AF.Square, accum_out=ssq1[:]))
                    s.wait_ge(pD1.h, t + 1)
                    pS1.inc(s.activation(s1v[:], r1c[:], AF.Sqrt))
                    s.wait_ge(pD3.h, t + 1)
                    pSq2.inc(s.activation(sqscr[:], yb[:], AF.Square, accum_out=ssq2[:]))
                    s.wait_ge(pD4.h, t + 1)
                    pS2.inc(s.activation(s2v[:], r2c[:], AF.Sqrt))
            block.scalar(p1_act)
            assert pSq1.n == NTT and pS2.n == NTT

            def p1_vec(v):
                def g(ins):
                    pDg.inc(ins)
                    v.wait_ge(pDg.h, pDg.n)
                    return ins
                for t in range(NTT):
                    p = t % 2
                    v.wait_ge(pXp[p].h, 16 * (t // 2 + 1))
                    g(v.reduce_sum(sx1[:], xt[p][:], axis=AX.X))
                    v.wait_ge(pSq1.h, t + 1)
                    g(v.tensor_scalar_mul(m1v[:], sx1[:], 1.0 / D))
                    g(v.tensor_scalar_mul(t1v[:], ssq1[:], 1.0 / D))
                    g(v.tensor_mul(msq[:], m1v[:], m1v[:]))
                    g(v.tensor_sub(v1v[:], t1v[:], msq[:]))
                    g(v.tensor_scalar_add(vpe[:], v1v[:], EPS))
                    pD1.inc(v.reciprocal(r1c[:], vpe[:]))
                    v.wait_ge(pS1.h, t + 1)
                    g(v.tensor_mul(nm1[:], m1v[:], s1v[:]))
                    g(v.tensor_scalar_mul(nm1n[:], nm1[:], -1.0))
                    pD3.inc(v.tensor_scalar(yb[:], xt[p][:], s1v[:], nm1n[:],
                                            ALU.mult, ALU.add, accum_out=sy2[:]))
                    v.wait_ge(pSq2.h, t + 1)
                    g(v.tensor_scalar_mul(mu2v[:], sy2[:], 1.0 / D))
                    g(v.tensor_scalar_mul(t1v[:], ssq2[:], 1.0 / D))
                    g(v.tensor_mul(msq[:], mu2v[:], mu2v[:]))
                    g(v.tensor_sub(v1v[:], t1v[:], msq[:]))
                    g(v.tensor_scalar_add(vpe[:], v1v[:], EPS))
                    pD4.inc(v.reciprocal(r2c[:], vpe[:]))
                    v.wait_ge(pS2.h, t + 1)
                    g(v.tensor_mul(nm2[:], mu2v[:], s2v[:]))
                    g(v.tensor_scalar_mul(nm2n[:], nm2[:], -1.0))
                    if t >= 2:
                        v.wait_ge(pG.h, t - 1)
                    pD6.inc(v.tensor_scalar(ttb[p][:], yb[:], s2v[:], nm2n[:],
                                            ALU.mult, ALU.add))
            block.vector(p1_vec)
            assert pD6.n == NTT

            def p1_gp(g):
                g.wait_ge(sCON.h, 65)
                for t in range(NTT):
                    p = t % 2
                    g.wait_ge(pD6.h, t + 1)
                    if t >= 2:
                        g.wait_ge(pWp[t % 2].h, 16 * (t // 2))
                    g.tensor_mul(g1b[:], ttb[p][:], gam[:])
                    pG.inc(g.tensor_add(xnb[p][:], g1b[:], bet[:]))
            block.gpsimd(p1_gp)
            assert pG.n == NTT

            barrier()

        with ExitStack() as c2:
            def s2b(name, shape, dt):
                return c2.enter_context(nc.sbuf_tensor(name, list(shape), dt))
            wq_sb = s2b("wq_sb", [128, NFC * D], BF16)
            xnqt = [s2b("xnqt%d" % i, [128, NFC * 512], BF16) for i in range(3)]
            qev = [s2b("qev0", [128, 512], BF16), s2b("qev1", [128, 512], BF16)]
            q_ps = [c2.enter_context(nc.psum_tensor("q_ps%d" % i, [128, 512], F32))
                    for i in range(3)]

            def p15a_gp(g):
                for fc in range(NFC):
                    qWL.inc(g.dma_start(
                        out=wq_sb[:, fc * D:(fc + 1) * D],
                        in_=wq_d[fc * 128:(fc + 1) * 128, :]), 16)
            block.gpsimd(p15a_gp)
            assert qWL.n == 256

            def p15a_sc(s):
                s.wait_ge(sCON.h, 33)
                s.wait_ge(pWp[0].h, pWp[0].n)
                s.wait_ge(pWp[1].h, pWp[1].n)
                for qw in range(NQC + 1):
                    if qw < NQC:
                        if qw >= 3:
                            s.wait_ge(qMM.h, 16 * (qw - 2))
                        for fc in range(NFC):
                            qX.inc(s.dma_start_transpose(
                                xnqt[qw % 3][:, fc * 512:(fc + 1) * 512],
                                xnq_c[fc, qw * 512:(qw + 1) * 512, :]), 16)
                    if qw >= 1:
                        for h in range(H):
                            idx = (qw - 1) * 16 + h
                            s.wait_ge(qMM.h, idx + 1)
                            if idx >= 2:
                                s.wait_ge(qWRp[idx % 2].h, 16 * (idx // 2))
                            qEV.inc(s.activation(qev[h % 2][:], q_ps[idx % 3][:],
                                                 AF.Identity, bias=bqs[:, h:h + 1],
                                                 scale=SCL))
            block.scalar(p15a_sc)
            assert qX.n == 16 * 64 and qEV.n == 64

            def p15a_pe(t):
                t.wait_ge(qWL.h, 256)
                for qw in range(NQC):
                    t.wait_ge(qX.h, 16 * 16 * (qw + 1))
                    for h in range(H):
                        idx = qw * 16 + h
                        if idx >= 2:
                            t.wait_ge(qEV.h, idx - 1)
                        for fc in range(NFC):
                            ins = t.matmul(
                                q_ps[(qw * 16 + h) % 3][:],
                                wq_sb[:, fc * D + h * 128: fc * D + (h + 1) * 128],
                                xnqt[qw % 3][:, fc * 512:(fc + 1) * 512],
                                start=(fc == 0), stop=(fc == NFC - 1))
                        qMM.inc(ins)
            block.tensor(p15a_pe)
            assert qMM.n == 64

            def p15a_sy(sy):
                for qw in range(NQC):
                    for h in range(H):
                        idx = qw * 16 + h
                        sy.wait_ge(qEV.h, idx + 1)
                        qWRp[idx % 2].inc(sy.dma_start(
                            out=q_c[h][:, qw * 512:(qw + 1) * 512],
                            in_=qev[h % 2][:]), 16)
            block.sync(p15a_sy)
            assert qWRp[0].n + qWRp[1].n == 16 * 64

            barrier()

        with ExitStack() as c3:
            def s3b(name, shape, dt):
                return c3.enter_context(nc.sbuf_tensor(name, list(shape), dt))
            wk_sb = s3b("wk_sb", [128, NFC * D], BF16)
            wv_sb = s3b("wv_sb", [128, NFC * D], BF16)
            xnt = [s3b("xntw%d" % i, [128, NFC * 512], BF16) for i in range(3)]
            kev = [s3b("kev0", [128, 512], BF16), s3b("kev1", [128, 512], BF16)]
            vev = [s3b("vev0", [128, 512], BF16), s3b("vev1", [128, 512], BF16)]
            k_ps = [c3.enter_context(nc.psum_tensor("k_ps%d" % i, [128, 512], F32))
                    for i in range(3)]
            v_ps = [c3.enter_context(nc.psum_tensor("v_ps%d" % i, [128, 512], F32))
                    for i in range(3)]

            def p15b_gp(g):
                for fc in range(NFC):
                    qWL.inc(g.dma_start(
                        out=wk_sb[:, fc * D:(fc + 1) * D],
                        in_=wk_d[fc * 128:(fc + 1) * 128, :]), 16)
                for fc in range(NFC):
                    qWL.inc(g.dma_start(
                        out=wv_sb[:, fc * D:(fc + 1) * D],
                        in_=wv_d[fc * 128:(fc + 1) * 128, :]), 16)
            block.gpsimd(p15b_gp)
            assert qWL.n == 256 * 3

            def p15b_sc(s):
                for w in range(9):
                    if w < 8:
                        if w >= 3:
                            s.wait_ge(kKmm.h, 16 * (w - 2))
                            s.wait_ge(kVmm.h, 16 * (w - 2))
                        for fc in range(NFC):
                            kX.inc(s.dma_start_transpose(
                                xnt[w % 3][:, fc * 512:(fc + 1) * 512],
                                xn_c[fc, w * 512:(w + 1) * 512, :]), 16)
                    if w >= 1:
                        wp = w - 1
                        for h in range(H):
                            idx = wp * 16 + h
                            s.wait_ge(kKmm.h, idx + 1)
                            if idx >= 2:
                                s.wait_ge(kWRp[idx % 2].h, 16 * (idx // 2))
                            kKev.inc(s.activation(kev[h % 2][:], k_ps[idx % 3][:],
                                                  AF.Identity,
                                                  bias=bias_sb[:, 16 + h:17 + h]))
                        for u in range(16):
                            idx = wp * 16 + u
                            s.wait_ge(kVmm.h, idx + 1)
                            if idx >= 2:
                                s.wait_ge(vWRp[idx % 2].h, 16 * (idx // 2))
                            kVev.inc(s.activation(vev[u % 2][:], v_ps[idx % 3][:],
                                                  AF.Identity))
            block.scalar(p15b_sc)
            assert kX.n == 16 * 16 * 8 and kKev.n == 128 and kVev.n == 128

            def p15b_pe(t):
                t.wait_ge(qWL.h, 256 * 3)
                for w in range(8):
                    t.wait_ge(kX.h, 16 * 16 * (w + 1))
                    for h in range(H):
                        idx = w * 16 + h
                        if idx >= 2:
                            t.wait_ge(kKev.h, idx - 1)
                        for fc in range(NFC):
                            ins = t.matmul(
                                k_ps[(w * 16 + h) % 3][:],
                                wk_sb[:, fc * D + h * 128: fc * D + (h + 1) * 128],
                                xnt[w % 3][:, fc * 512:(fc + 1) * 512],
                                start=(fc == 0), stop=(fc == NFC - 1))
                        kKmm.inc(ins)
                    for hg in range(4):
                        for sub in range(4):
                            u = hg * 4 + sub
                            idx = w * 16 + u
                            if idx >= 2:
                                t.wait_ge(kVev.h, idx - 1)
                            for fc in range(NFC):
                                ins = t.matmul(
                                    v_ps[(w * 16 + u) % 3][:],
                                    xnt[w % 3][:, fc * 512 + sub * 128: fc * 512 + (sub + 1) * 128],
                                    wv_sb[:, fc * D + hg * 512: fc * D + (hg + 1) * 512],
                                    start=(fc == 0), stop=(fc == NFC - 1))
                            kVmm.inc(ins)
            block.tensor(p15b_pe)
            assert kKmm.n == 128 and kVmm.n == 128

            def p15b_sy(sy):
                for w in range(8):
                    for h in range(H):
                        idx = w * 16 + h
                        sy.wait_ge(kKev.h, idx + 1)
                        kWRp[idx % 2].inc(sy.dma_start(
                            out=k_c[h][:, w * 512:(w + 1) * 512],
                            in_=kev[h % 2][:]), 16)
                    for hg in range(4):
                        for sub in range(4):
                            u = hg * 4 + sub
                            idx = w * 16 + u
                            sy.wait_ge(kVev.h, idx + 1)
                            vWRp[idx % 2].inc(sy.dma_start(
                                out=v_c[hg, w, sub],
                                in_=vev[u % 2][:]), 16)
            block.sync(p15b_sy)
            assert kWRp[0].n + kWRp[1].n == 16 * 128
            assert vWRp[0].n + vWRp[1].n == 16 * 128

            barrier()



        def EX(qc):
            return 8 * (qc + 1)

        TOT_KT = H * sum(EX(qc) for qc in range(NQC))

        with ExitStack() as c4a:
            avcat = c4a.enter_context(nc.sbuf_tensor("avcat", [128, H * NOWN], BF16))
            c4 = c4a.enter_context(ExitStack())
            def s4b(name, shape, dt):
                return c4.enter_context(nc.sbuf_tensor(name, list(shape), dt))
            mask_sb = s4b("mask_sb", [128, 8 * 512], BF16)
            k_fm = [s4b("k_fm%d" % i, [128, S], BF16) for i in range(4)]
            v_sb = [s4b("v_sb%d" % i, [128, S], BF16) for i in range(4)]
            q_fm = [s4b("q_fm%d" % i, [128, NOWN], BF16) for i in range(4)]
            at = [s4b("at%d" % i, [128, 1024], BF16) for i in range(3)]
            atm = [s4b("atm%d" % i, [128, 1024], BF16) for i in range(2)]
            s0m2 = [s4b("s0m2_%d" % i, [128, 1], F32) for i in range(2)]
            e02 = [s4b("e02_%d" % i, [128, 1], BF16) for i in range(2)]
            mu_bf2 = [s4b("mu_bf2_%d" % i, [1, 1], BF16) for i in range(2)]
            negmu2 = [s4b("negmu2_%d" % i, [128, 1], F32) for i in range(2)]
            linv2 = [s4b("linv2_%d" % i, [1, 512], F32) for i in range(2)]
            lbc2 = [s4b("lbc2_%d" % i, [128, 512], F32) for i in range(2)]
            st_ps = [c4.enter_context(nc.psum_tensor("st_ps%d" % i, [128, 1024], F32))
                     for i in range(2)]
            av_ps2 = [c4.enter_context(nc.psum_tensor("av_ps%d" % i, [128, 512], F32))
                      for i in range(2)]
            lp_ps = [c4.enter_context(nc.psum_tensor("lp_ps%d" % i, [128, 512], F32))
                     for i in range(2)]

            cbuf = [s4b("cb0", [128, DFF], BF16), s4b("cb1", [128, DFF], BF16)]

            def p2_cast(g):
                ci = 0
                for wsrc, wdst in ((wg_d, wgb_c), (wu_d, wub_c)):
                    for fc in range(NFC):
                        if ci >= 2:
                            g.wait_ge(cWR.h, 16 * (ci - 1))
                        cLD.inc(g.dma_start(out=cbuf[ci % 2][:],
                                            in_=wsrc[fc * 128:(fc + 1) * 128, :]), 16)
                        g.wait_ge(cLD.h, 16 * (ci + 1))
                        cWR.inc(g.dma_start(
                            out=wdst[:, :, fc * 128:(fc + 1) * 128].rearrange(
                                "d p f -> p d f"),
                            in_=cbuf[ci % 2].ap().rearrange("p (d f) -> p d f", f=128)), 16)
                        ci += 1
                for dc in range(NDC):
                    if ci >= 2:
                        g.wait_ge(cWR.h, 16 * (ci - 1))
                    cLD.inc(g.dma_start(out=cbuf[ci % 2][:, 0:D],
                                        in_=wd_d[dc * 128:(dc + 1) * 128, :]), 16)
                    g.wait_ge(cLD.h, 16 * (ci + 1))
                    cWR.inc(g.dma_start(
                        out=wdb_c[dc].rearrange("q p f -> p q f"),
                        in_=cbuf[ci % 2][:, 0:D].rearrange("p (q f) -> p q f", f=512)), 16)
                    ci += 1
            block.gpsimd(p2_cast)
            CWR_ALL = cWR.n
            assert CWR_ALL == 16 * (2 * NFC + NDC)

            def p2_sy(sy):
                sy.dma_start(out=mask_sb.ap().rearrange("p (d j) -> p d j", j=512),
                             in_=mq.rearrange("d p j -> p d j")).then_inc(sCON.h, 16)
                for P in range(H // 2):
                    if P == 0:
                        for sc_ in (qWRp[0], qWRp[1], kWRp[0], kWRp[1], vWRp[0], vWRp[1]):
                            sy.wait_ge(sc_.h, sc_.n)
                    if P >= 2:
                        sy.wait_ge(aLB.h, 8 * (P - 1))
                    for h in (2 * P, 2 * P + 1):
                        ln4 = h % 4
                        hg, hh = h // 4, h % 4
                        aLDp[P % 2].inc(sy.dma_start(out=k_fm[ln4][:], in_=k_c[h]), 16)
                        aLDp[P % 2].inc(sy.dma_start(
                            out=v_sb[ln4][:],
                            in_=v_c[hg, :, :, :, hh * 128:(hh + 1) * 128].rearrange(
                                "w s p d -> p (w s) d")), 16)
                        aLDp[P % 2].inc(sy.dma_start(out=q_fm[ln4][:], in_=q_c[h]), 16)
            block.sync(p2_sy)
            sCON.n += 16
            SCON_MASK = sCON.n
            assert aLDp[0].n + aLDp[1].n == 48 * H

            def p2_pe(t):
                nst = nexp = nmsk = nkt = nlb = nlinv = navd = 0
                nE0 = nMUV = nMU = 0
                for P in range(H // 2):
                    heads = [(2 * P + i, (2 * P + i) % 4, i) for i in range(2)]
                    t.wait_ge(aLDp[P % 2].h, 96 * (P // 2 + 1))
                    t.wait_ge(aEXP.h, nexp)
                    for h, ln4, ln in heads:
                        aS0.inc(t.matmul(st_ps[ln][:, 0:128], q_fm[ln4][:, 0:128],
                                         k_fm[ln4][:, 0:128], start=True, stop=True))
                    t.wait_ge(aLINV.h, nlinv)
                    for h, ln4, ln in heads:
                        nE0 += 1
                        t.wait_ge(aE0.h, nE0)
                        aMUP.inc(t.matmul(lp_ps[ln][0:1, 0:1], ones_col[:], e02[ln][:],
                                          start=True, stop=True))
                    for h, ln4, ln in heads:
                        nMUV += 1
                        t.wait_ge(aMUV.h, nMUV)
                        aNBP.inc(t.matmul(st_ps[ln][:, 0:1], ones_row[:], mu_bf2[ln][:],
                                          start=True, stop=True))
                    t.wait_ge(aMU.h, 2 * (P + 1))
                    for qc in range(NQC):
                        ex = EX(qc)
                        pq = ex // 2
                        if navd:
                            t.wait_ge(aAVD.h, navd)

                        def emit_st(j, ln, ln4, qc=qc):
                            ps = st_ps[ln]
                            for half in range(2):
                                kt = 2 * j + half
                                ins = t.matmul(
                                    ps[:, half * 512:(half + 1) * 512],
                                    k_fm[ln4][:, kt * 128:(kt + 1) * 128],
                                    q_fm[ln4][:, qc * 512:(qc + 1) * 512],
                                    start=True, stop=True)
                            aST.inc(ins)

                        for h, ln4, ln in heads:
                            emit_st(0, ln, ln4)
                        for j in range(pq):
                            for h, ln4, ln in heads:
                                nexp += 1
                                d0 = 2 * j - (ex - 8)
                                if d0 >= 0:
                                    nmsk += 1
                                    t.wait_ge(aMSK.h, nmsk)
                                    src_t = atm[nmsk % 2]
                                else:
                                    t.wait_ge(aEXP.h, nexp)
                                    src_t = at[nexp % 3]
                                for half in range(2):
                                    kt = 2 * j + half
                                    t.matmul(av_ps2[ln][:],
                                             v_sb[ln4][:, kt * 128:(kt + 1) * 128],
                                             src_t[:, half * 512:(half + 1) * 512],
                                             start=(kt == 0), stop=(kt == ex - 1))
                                    ins = t.matmul(
                                        lp_ps[ln][0:1, :], ones_col[:],
                                        src_t[:, half * 512:(half + 1) * 512],
                                        start=(kt == 0), stop=(kt == ex - 1))
                                    aKT.inc(ins)
                                if j + 1 < pq:
                                    emit_st(j + 1, ln, ln4)
                        nkt += 2 * ex
                        for h, ln4, ln in heads:
                            nlinv += 1
                            t.wait_ge(aLINV.h, nlinv)
                            aLB.inc(t.matmul(lp_ps[ln][:, :], ones_row_f[:],
                                             linv2[ln][:], start=True, stop=True))
                        nlb += 2
                        navd += 2
                assert nkt == TOT_KT and nexp == TOT_KT // 2
            block.tensor(p2_pe)
            assert aST.n == TOT_KT // 2 and aKT.n == TOT_KT
            assert aLB.n == 2 * H * NQC // 2

            def p2_act(s):
                nst = nexp = nkt_c = nlb = 0
                for P in range(H // 2):
                    heads = [(2 * P + i, (2 * P + i) % 4, i) for i in range(2)]
                    for h, ln4, ln in heads:
                        s.wait_ge(aRM.h, 2 * P + ln + 1)
                        aE0.inc(s.activation(e02[ln][:], s0m2[ln][:], AF.Exp))
                    for h, ln4, ln in heads:
                        s.wait_ge(aMUP.h, 2 * P + ln + 1)
                        aMUV.inc(s.activation(mu_bf2[ln][:], lp_ps[ln][0:1, 0:1], AF.Ln))
                    for h, ln4, ln in heads:
                        s.wait_ge(aNBP.h, 2 * P + ln + 1)
                        aMU.inc(s.activation(negmu2[ln][:], st_ps[ln][:, 0:1],
                                             AF.Identity, scale=-1.0))
                    s.wait_ge(aMU.h, 2 * (P + 1))
                    for qc in range(NQC):
                        ex = EX(qc)
                        pq = ex // 2
                        for j in range(pq):
                            for h, ln4, ln in heads:
                                nst += 1
                                nexp += 1
                                s.wait_ge(aST.h, nst)
                                if nexp >= 4:
                                    s.wait_ge(aKT.h, 2 * (nexp - 3))
                                aEXP.inc(s.activation(at[nexp % 3][:], st_ps[ln][:],
                                                      AF.Exp, bias=negmu2[ln][:]))
                        for h, ln4, ln in heads:
                            nlb += 1
                            s.wait_ge(aLB.h, nlb)
                            aLBC.inc(s.copy(lbc2[ln][:], lp_ps[ln][:, :]))
            block.scalar(p2_act)
            assert aEXP.n == TOT_KT // 2

            def p2_vec(v):
                v.wait_ge(sCON.h, SCON_MASK)
                nexp = nmsk = nkt = nlinv = nlbc = navd = 0
                for P in range(H // 2):
                    heads = [(2 * P + i, (2 * P + i) % 4, i) for i in range(2)]
                    for h, ln4, ln in heads:
                        v.wait_ge(aS0.h, 2 * P + ln + 1)
                        aRM.inc(v.reduce_max(s0m2[ln][:], st_ps[ln][:, 0:128],
                                             axis=AX.X))
                    for qc in range(NQC):
                        ex = EX(qc)
                        pq = ex // 2
                        for j in range(pq):
                            for h, ln4, ln in heads:
                                nexp += 1
                                d0 = 2 * j - (ex - 8)
                                if d0 >= 0:
                                    v.wait_ge(aEXP.h, nexp)
                                    if nmsk >= 2:
                                        v.wait_ge(aKT.h, 2 * nexp - 4)
                                    nmsk += 1
                                    aMSK.inc(v.tensor_mul(
                                        atm[nmsk % 2][:], at[nexp % 3][:],
                                        mask_sb[:, d0 * 512:(d0 + 2) * 512]))
                        nkt += 2 * ex
                        for h, ln4, ln in heads:
                            v.wait_ge(aKT.h, nkt - 2 if ln == 0 else nkt)
                            nlinv += 1
                            aLINV.inc(v.reciprocal(linv2[ln][:], lp_ps[ln][0:1, :]))
                        for h, ln4, ln in heads:
                            nlbc += 1
                            v.wait_ge(aLBC.h, nlbc)
                            pDg.inc(v.tensor_mul(avn[:], av_ps2[ln][:], lbc2[ln][:]))
                            v.wait_ge(pDg.h, pDg.n)
                            h_abs = 2 * P + ln
                            navd += 1
                            aAVD.inc(v.tensor_scalar(
                                avcat[:, h_abs * NOWN + qc * 512: h_abs * NOWN + (qc + 1) * 512],
                                avn[:], bias_sb[:, 32 + h_abs:33 + h_abs], None,
                                ALU.add))
            block.vector(p2_vec)

            barrier()
            c4.close()

            with ExitStack() as c5:
                def s5b(name, shape, dt):
                    return c5.enter_context(nc.sbuf_tensor(name, list(shape), dt))
                wo_sb = s5b("wo_sb", [128, NFC * D], BF16)
                bo_bc = s5b("bo_bc", [128, D], F32)
                xq_t = [s5b("xq_t0", [128, D], F32), s5b("xq_t1", [128, D], F32)]
                r1t = [s5b("r1t0", [128, D], F32), s5b("r1t1", [128, D], F32)]
                xn2t = [s5b("xn2t0", [128, D], BF16), s5b("xn2t1", [128, D], BF16)]
                w3sq = s5b("w3sq", [128, D], F32)
                wo_ps = [c5.enter_context(nc.psum_tensor("wo_ps%d" % i, [128, 512], F32))
                         for i in range(4)]

                def p3_gp(g):
                    wLD.inc(g.dma_start(out=bo_bc[:],
                                        in_=bo_d[None, :].broadcast_to([128, D])), 16)
                    for hc in range(NFC):
                        wLD.inc(g.dma_start(
                            out=wo_sb[:, hc * D:(hc + 1) * D],
                            in_=wo_d[hc * 128:(hc + 1) * 128, :]), 16)
                block.gpsimd(p3_gp)
                assert wLD.n == 16 * 17

                def p3_sy(sy):
                    for tt in range(NT_OWN):
                        if tt >= 2:
                            sy.wait_ge(wEVA.h, 4 * (tt - 1))
                        wXQp[tt % 2].inc(sy.dma_start(
                            out=xq_t[tt % 2][:],
                            in_=xq[tt * 128:(tt + 1) * 128, :]), 16)
                        if tt >= 1:
                            sy.wait_ge(wAPL.h, tt)
                            wWRp[(tt - 1) % 2].inc(sy.dma_start(
                                out=r1_c[(tt - 1) * 128:tt * 128, :],
                                in_=r1t[(tt - 1) % 2][:]), 16)
                            wWRp[(tt - 1) % 2].inc(sy.dma_start(
                                out=xn2_c[:, (tt - 1) * 128:tt * 128, :].rearrange(
                                    "c t f -> t c f"),
                                in_=xn2t[(tt - 1) % 2].ap().rearrange(
                                    "p (c f) -> p c f", f=128)), 16)
                    sy.wait_ge(wAPL.h, NT_OWN)
                    wWRp[(NT_OWN - 1) % 2].inc(sy.dma_start(
                        out=r1_c[(NT_OWN - 1) * 128:NT_OWN * 128, :],
                        in_=r1t[(NT_OWN - 1) % 2][:]), 16)
                    wWRp[(NT_OWN - 1) % 2].inc(sy.dma_start(
                        out=xn2_c[:, (NT_OWN - 1) * 128:NT_OWN * 128, :].rearrange(
                            "c t f -> t c f"),
                        in_=xn2t[(NT_OWN - 1) % 2].ap().rearrange(
                            "p (c f) -> p c f", f=128)), 16)
                block.sync(p3_sy)
                assert wWRp[0].n + wWRp[1].n == 32 * NT_OWN

                def p3_pe(t):
                    t.wait_ge(wLD.h, 16 * 17)
                    t.wait_ge(aAVD.h, H * NQC)
                    for tt in range(NT_OWN):
                        for c4i in range(4):
                            if tt >= 1:
                                t.wait_ge(wEVA.h, 4 * (tt - 1) + c4i + 1)
                            for hc in range(NFC):
                                ins = t.matmul(
                                    wo_ps[c4i][:],
                                    avcat[:, hc * NOWN + tt * 128: hc * NOWN + (tt + 1) * 128],
                                    wo_sb[:, hc * D + c4i * 512: hc * D + (c4i + 1) * 512],
                                    start=(hc == 0), stop=(hc == NFC - 1))
                            wMM.inc(ins)
                block.tensor(p3_pe)
                assert wMM.n == 4 * NT_OWN

                def p3_act(s):
                    for tt in range(NT_OWN):
                        p = tt % 2
                        s.wait_ge(wEVA.h, 4 * (tt + 1))
                        wSq1.inc(s.activation(w3sq[:], r1t[p][:], AF.Square,
                                              accum_out=ssq1[:]))
                        s.wait_ge(wD1.h, tt + 1)
                        wS1.inc(s.activation(s1v[:], r1c[:], AF.Sqrt))
                block.scalar(p3_act)
                assert wSq1.n == NT_OWN and wS1.n == NT_OWN

                def p3_vec(v):
                    def g(ins):
                        wDg.inc(ins)
                        v.wait_ge(wDg.h, wDg.n)
                        return ins
                    for tt in range(NT_OWN):
                        p = tt % 2
                        v.wait_ge(wXQp[p].h, 16 * (tt // 2 + 1))
                        if tt >= 2:
                            v.wait_ge(wWRp[tt % 2].h, 32 * (tt // 2))
                        for c4i in range(4):
                            v.wait_ge(wMM.h, 4 * tt + c4i + 1)
                            sl = slice(c4i * 512, (c4i + 1) * 512)
                            g(v.tensor_add(tmp512[:], wo_ps[c4i][:], xq_t[p][:, sl]))
                            wEVA.inc(v.tensor_add(r1t[p][:, sl], tmp512[:],
                                                  bo_bc[:, sl]))
                        v.wait_ge(wEVA.h, 4 * (tt + 1))
                        g(v.reduce_sum(sx1[:], r1t[p][:], axis=AX.X))
                        g(v.tensor_scalar_mul(m1v[:], sx1[:], 1.0 / D))
                        v.wait_ge(wSq1.h, tt + 1)
                        g(v.tensor_scalar_mul(t1v[:], ssq1[:], 1.0 / D))
                        g(v.tensor_mul(msq[:], m1v[:], m1v[:]))
                        g(v.tensor_sub(v1v[:], t1v[:], msq[:]))
                        g(v.tensor_scalar_add(vpe[:], v1v[:], EPS))
                        wD1.inc(v.reciprocal(r1c[:], vpe[:]))
                        v.wait_ge(wS1.h, tt + 1)
                        g(v.tensor_mul(nm1[:], m1v[:], s1v[:]))
                        g(v.tensor_scalar_mul(nm1n[:], nm1[:], -1.0))
                        wAPL.inc(v.tensor_scalar(xn2t[p][:], r1t[p][:], s1v[:],
                                                 nm1n[:], ALU.mult, ALU.add))
                block.vector(p3_vec)
                assert wAPL.n == NT_OWN and wEVA.n == 4 * NT_OWN

                barrier()


        with ExitStack() as c6:
            def s6b(name, shape, dt):
                return c6.enter_context(nc.sbuf_tensor(name, list(shape), dt))
            xnt2 = s6b("xnt2", [128, NFC * FFN_W], BF16)
            hid = s6b("hid", [128, NDC * FFN_W], BF16)
            wgu = [s6b("wg0", [128, NFC * 128], BF16), s6b("wg1", [128, NFC * 128], BF16),
                   s6b("wu0", [128, NFC * 128], BF16), s6b("wu1", [128, NFC * 128], BF16)]
            sbg4 = [s6b("sbg4_0", [8, 512], BF16), s6b("sbg4_1", [8, 512], BF16)]
            sbu4 = [s6b("sbu4_0", [8, 512], BF16), s6b("sbu4_1", [8, 512], BF16)]
            bcg4 = [s6b("bcg4_0", [64, 512], BF16), s6b("bcg4_1", [64, 512], BF16)]
            bcu4 = [s6b("bcu4_0", [64, 512], BF16), s6b("bcu4_1", [64, 512], BF16)]
            wd_t = [s6b("wd0", [128, 512], BF16), s6b("wd1", [128, 512], BF16),
                    s6b("wd2", [128, 512], BF16)]
            gs_sb = s6b("gs_sb", [128, FFN_W], BF16)
            wr_sb = s6b("wr_sb", [128, NFC * E8], BF16)
            sag_sb = s6b("sag_sb", [128, NFC * RK], BF16)
            sau_sb = s6b("sau_sb", [128, NFC * RK], BF16)
            acg_sb = s6b("acg_sb", [128, NFC * ER], BF16)
            acu_sb = s6b("acu_sb", [128, NFC * ER], BF16)
            sad72_sb = s6b("sad72_sb", [128, NDC * 72], BF16)
            sbbc_sb = s6b("sbbc_sb", [72, D], BF16)
            ident_sb = s6b("ident_sb", [128, 128], F32)
            rself_sb = s6b("rself_sb", [9, 72], F32)
            sg_sb = s6b("sg_sb", [128, E8], F32)
            bi_sb = s6b("bi_sb", [128, E8], F32)
            mk1 = s6b("mk1", [128, E8], F32)
            b2t = s6b("b2t", [128, E8], F32)
            mk2 = s6b("mk2", [128, E8], F32)
            mkt = s6b("mkt", [128, E8], F32)
            wsc = s6b("wsc", [128, E8], F32)
            m1e = s6b("m1e", [128, 1], F32)
            m2e = s6b("m2e", [128, 1], F32)
            sse = s6b("sse", [128, 1], F32)
            sie = s6b("sie", [128, 1], F32)
            w_tok = [s6b("w_tok0", [128, E8], F32), s6b("w_tok1", [128, E8], F32)]
            wt9_sb = s6b("wt9_sb", [9, FFN_W], F32)
            wbc72_sb = s6b("wbc72_sb", [72, FFN_W], F32)
            xawg_sb = s6b("xawg_sb", [64, FFN_W], BF16)
            xawu_sb = s6b("xawu_sb", [64, FFN_W], BF16)
            sxg_sb = s6b("sxg_sb", [8, FFN_W], BF16)
            sxu_sb = s6b("sxu_sb", [8, FFN_W], BF16)
            hxw72_sb = s6b("hxw72_sb", [72, FFN_W], BF16)
            r1r = [s6b("r1r0", [128, 512], F32), s6b("r1r1", [128, 512], F32)]
            o_sb = [s6b("o_sb0", [128, 512], F32), s6b("o_sb1", [128, 512], F32)]

            def p4_const(g):
                g.wait_ge(wWRp[0].h, wWRp[0].n)
                g.wait_ge(wWRp[1].h, wWRp[1].n)
                g.dma_start(out=ident_sb[:], in_=ident_c.ap()).then_inc(fLD.h, 16)
                g.dma_start(out=rself_sb[:], in_=rsel_c.ap()).then_inc(fLD.h, 16)
                g.dma_start(out=wr_sb.ap().rearrange("p (c e) -> p c e", e=E8),
                            in_=wr_d.rearrange("(c p) e -> p c e", p=128)).then_inc(fLD.h, 16)
                g.dma_start(out=sag_sb.ap().rearrange("p (c r) -> p c r", r=RK),
                            in_=sag_d.rearrange("(c p) r -> p c r", p=128)).then_inc(fLD.h, 16)
                g.dma_start(out=sau_sb.ap().rearrange("p (c r) -> p c r", r=RK),
                            in_=sau_d.rearrange("(c p) r -> p c r", p=128)).then_inc(fLD.h, 16)
                for e in range(E8):
                    g.dma_start(
                        out=acg_sb.ap().rearrange("p (c e r) -> p c e r", e=E8, r=RK)[:, :, e, :],
                        in_=ag_d[e].rearrange("(c p) r -> p c r", p=128)).then_inc(fLD.h, 16)
                    g.dma_start(
                        out=acu_sb.ap().rearrange("p (c e r) -> p c e r", e=E8, r=RK)[:, :, e, :],
                        in_=au_d[e].rearrange("(c p) r -> p c r", p=128)).then_inc(fLD.h, 16)
                    g.dma_start(
                        out=sad72_sb.ap().rearrange("p (c r) -> p c r", r=72)[:, :, e * RK:(e + 1) * RK],
                        in_=ad_d[e].rearrange("(c p) r -> p c r", p=128)).then_inc(fLD.h, 16)
                g.dma_start(out=sad72_sb.ap().rearrange("p (c r) -> p c r", r=72)[:, :, 64:72],
                            in_=sad_d.rearrange("(c p) r -> p c r", p=128)).then_inc(fLD.h, 16)
                g.dma_start(out=sbbc_sb[64:72, :], in_=sbd_d).then_inc(fLD.h, 16)
                g.dma_start(out=sbbc_sb[0:64, :],
                            in_=bd_d.rearrange("e r d -> (e r) d")).then_inc(fLD.h, 16)
            block.gpsimd(p4_const)
            fLD.n += 16 * 32

            def p4_vec0(v):
                v.wait_ge(wWRp[0].h, wWRp[0].n)
                v.wait_ge(wWRp[1].h, wWRp[1].n)
                fLD.inc(v.memset(wt9_sb[:], 1.0))
            block.vector(p4_vec0)
            FLD_ALL = fLD.n

            NI = FFN_W // 128

            for W in range(NWIN):
                with ExitStack() as cp:
                    rl_ps = cp.enter_context(nc.psum_tensor("rl_ps_w%d" % W, [128, 512], F32))
                    wt_ps = cp.enter_context(nc.psum_tensor("wt_ps_w%d" % W, [8, 512], F32))
                    wb_ps = cp.enter_context(nc.psum_tensor("wb_ps_w%d" % W, [72, FFN_W], F32))
                    xa_ps = cp.enter_context(nc.psum_tensor("xa_ps_w%d" % W, [64, FFN_W], F32))
                    sx_ps = cp.enter_context(nc.psum_tensor("sx_ps_w%d" % W, [8, FFN_W], F32))

                    fX2_0 = fX2.n
                    fRL0, fSG0, fTOP0, fWT0, fWTE0 = fRL.n, fSG.n, fTOP.n, fWT.n, fWTE.n
                    fWB0, fWBE0, fXA0, fXAW0, fSX0, fSXE0 = (
                        fWB.n, fWBE.n, fXA.n, fXAW.n, fSX.n, fSXE.n)

                    def p4a_sc(s, W=W, fX2_0=fX2_0):
                        if W == 0:
                            s.wait_ge(wWRp[0].h, wWRp[0].n)
                            s.wait_ge(wWRp[1].h, wWRp[1].n)
                        else:
                            s.wait_ge(fGMM.h, 86 * W)
                            s.wait_ge(fUMM.h, 86 * W)
                        for fc in range(NFC):
                            fX2.inc(s.dma_start_transpose(
                                xnt2[:, fc * FFN_W:(fc + 1) * FFN_W],
                                xn2_c[fc, W * FFN_W:(W + 1) * FFN_W, :]), 16)
                    block.scalar(p4a_sc)

                    def p4b_pe(t, W=W, fX2_0=fX2_0, fSG0=fSG0, fTOP0=fTOP0,
                              fWTE0=fWTE0):
                        t.wait_ge(fLD.h, FLD_ALL)
                        if W >= 1:
                            t.wait_ge(fOEV.h, 32 * W)
                        t.wait_ge(fX2.h, fX2_0 + 16 * 16)
                        for i in range(NI):
                            if i >= 1:
                                t.wait_ge(fSG.h, fSG0 + i)
                            for fc in range(NFC):
                                ins = t.matmul(
                                    rl_ps[:, 0:E8],
                                    xnt2[:, fc * FFN_W + i * 128: fc * FFN_W + (i + 1) * 128],
                                    wr_sb[:, fc * E8:(fc + 1) * E8],
                                    start=(fc == 0), stop=(fc == NFC - 1))
                            fRL.inc(ins)
                            t.wait_ge(fTOP.h, fTOP0 + i + 1)
                            if i >= 1:
                                t.wait_ge(fWTE.h, fWTE0 + i)
                            fWT.inc(t.transpose(wt_ps[:, 0:128], w_tok[i % 2][:],
                                                ident_sb[:]))
                    block.tensor(p4b_pe)

                    def p4b_act(s, fRL0=fRL0, fWT0=fWT0, fTOP0=fTOP0):
                        for i in range(NI):
                            if i >= 1:
                                s.wait_ge(fTOP.h, fTOP0 + i)
                            s.wait_ge(fRL.h, fRL0 + i + 1)
                            fSG.inc(s.activation(sg_sb[:], rl_ps[:, 0:E8], AF.Sigmoid))
                            s.wait_ge(fWT.h, fWT0 + i + 1)
                            fWTE.inc(s.copy(wt9_sb[0:8, i * 128:(i + 1) * 128],
                                            wt_ps[:, 0:128]))
                    block.scalar(p4b_act)

                    def p4b_vec(v, fSG0=fSG0, fWT0=fWT0):
                        v.wait_ge(sCON.h, 32)
                        for i in range(NI):
                            v.wait_ge(fSG.h, fSG0 + i + 1)
                            if i >= 2:
                                v.wait_ge(fWT.h, fWT0 + i - 1)
                            def g(ins):
                                fDg.inc(ins)
                                v.wait_ge(fDg.h, fDg.n)
                                return ins
                            g(v.tensor_add(bi_sb[:], sg_sb[:], ebias_sb[:]))
                            g(v.reduce_max(m1e[:], bi_sb[:], axis=AX.X))
                            g(v.tensor_scalar(mk1[:], bi_sb[:], m1e[:], None, ALU.is_ge))
                            g(v.scalar_tensor_tensor(b2t[:], mk1[:], -1e30, bi_sb[:],
                                                     ALU.mult, ALU.add))
                            g(v.reduce_max(m2e[:], b2t[:], axis=AX.X))
                            g(v.tensor_scalar(mk2[:], b2t[:], m2e[:], None, ALU.is_ge))
                            g(v.tensor_add(mkt[:], mk1[:], mk2[:]))
                            g(v.tensor_mul(wsc[:], mkt[:], sg_sb[:]))
                            g(v.reduce_sum(sse[:], wsc[:], axis=AX.X))
                            g(v.tensor_scalar_add(sse[:], sse[:], 1e-8))
                            g(v.reciprocal(sie[:], sse[:]))
                            fTOP.inc(v.tensor_scalar(w_tok[i % 2][:], wsc[:], sie[:],
                                                     None, ALU.mult))
                    block.vector(p4b_vec)

                    def p4c_pe(t, fWTE0=fWTE0, fXAW0=fXAW0, fSXE0=fSXE0):
                        t.wait_ge(fWTE.h, fWTE0 + NI)
                        for half in range(2):
                            fWB.inc(t.matmul(wb_ps[:, half * 512:(half + 1) * 512],
                                             rself_sb[:],
                                             wt9_sb[:, half * 512:(half + 1) * 512],
                                             start=True, stop=True))
                        for gi, asb in enumerate((acg_sb, acu_sb)):
                            if gi == 1:
                                t.wait_ge(fXAW.h, fXAW0 + 2)
                            for half in range(2):
                                for fc in range(NFC):
                                    ins = t.matmul(
                                        xa_ps[:, half * 512:(half + 1) * 512],
                                        asb[:, fc * ER:(fc + 1) * ER],
                                        xnt2[:, fc * FFN_W + half * 512: fc * FFN_W + half * 512 + 512],
                                        start=(fc == 0), stop=(fc == NFC - 1))
                                fXA.inc(ins)
                        for gi, ssb in enumerate((sag_sb, sau_sb)):
                            if gi == 1:
                                t.wait_ge(fSXE.h, fSXE0 + 2)
                            for half in range(2):
                                for fc in range(NFC):
                                    ins = t.matmul(
                                        sx_ps[:, half * 512:(half + 1) * 512],
                                        ssb[:, fc * RK:(fc + 1) * RK],
                                        xnt2[:, fc * FFN_W + half * 512: fc * FFN_W + half * 512 + 512],
                                        start=(fc == 0), stop=(fc == NFC - 1))
                                fSX.inc(ins)
                    block.tensor(p4c_pe)

                    def p4c_act(s, fWB0=fWB0, fSX0=fSX0):
                        s.wait_ge(fWB.h, fWB0 + 2)
                        fWBE.inc(s.copy(wbc72_sb[:], wb_ps[:]))
                        for gi, dst in enumerate((sxg_sb, sxu_sb)):
                            for half in range(2):
                                s.wait_ge(fSX.h, fSX0 + gi * 2 + half + 1)
                                fSXE.inc(s.activation(
                                    dst[:, half * 512:(half + 1) * 512],
                                    sx_ps[:, half * 512:(half + 1) * 512],
                                    AF.Identity, scale=1.0 / RK))
                    block.scalar(p4c_act)

                    def p4c_vec(v, fWBE0=fWBE0, fXA0=fXA0):
                        v.wait_ge(fWBE.h, fWBE0 + 1)
                        for gi, dst in enumerate((xawg_sb, xawu_sb)):
                            for half in range(2):
                                v.wait_ge(fXA.h, fXA0 + gi * 2 + half + 1)
                                fXAW.inc(v.tensor_mul(
                                    dst[:, half * 512:(half + 1) * 512],
                                    xa_ps[:, half * 512:(half + 1) * 512],
                                    wbc72_sb[0:64, half * 512:(half + 1) * 512]))
                    block.vector(p4c_vec)

                with ExitStack() as cq:
                    g_ps = cq.enter_context(nc.psum_tensor("g_ps_w%d" % W, [128, FFN_W], F32))
                    u_ps = [cq.enter_context(nc.psum_tensor("u_ps%d_w%d" % (i, W),
                                                            [128, FFN_W], F32))
                            for i in range(2)]
                    hx_ps = cq.enter_context(nc.psum_tensor("hx_ps_w%d" % W, [72, FFN_W], F32))

                    fGMM0, fUMM0, fSIL0, fUCP0, fHID0 = (
                        fGMM.n, fUMM.n, fSIL.n, fUCP.n, fHID.n)
                    fWGUp0 = [fWGUp[0].n, fWGUp[1].n]
                    fSBp0 = [fSBp[0].n, fSBp[1].n]

                    def p4d_gp(g, fGMM0=fGMM0, fUMM0=fUMM0, W=W):
                        for dc in range(NDC):
                            if dc >= 2:
                                tgt = fGMM0 + 2 * (dc - 1)
                            else:
                                tgt = fGMM0 if W > 0 else 0
                            if tgt > 0:
                                g.wait_ge(fGMM.h, tgt)
                                g.wait_ge(fUMM.h, tgt)
                            if W == 0 and dc == 0:
                                g.wait_ge(cWR.h, CWR_ALL)
                            if dc % 4 == 0:
                                dc4 = dc // 4
                                f0 = dc4 * 512
                                f1 = min(f0 + 512, DFF)
                                spb = fSBp[dc4 % 2]
                                spb.inc(g.dma_start(out=sbg4[dc4 % 2][:, 0:f1 - f0],
                                                    in_=sbg_d[:, f0:f1]), 16)
                                spb.inc(g.dma_start(out=sbu4[dc4 % 2][:, 0:f1 - f0],
                                                    in_=sbu_d[:, f0:f1]), 16)
                                spb.inc(g.dma_start(
                                    out=bcg4[dc4 % 2][:, 0:f1 - f0],
                                    in_=bg_d[:, :, f0:f1].rearrange("e r f -> (e r) f")), 16)
                                spb.inc(g.dma_start(
                                    out=bcu4[dc4 % 2][:, 0:f1 - f0],
                                    in_=bu_d[:, :, f0:f1].rearrange("e r f -> (e r) f")), 16)
                            sp = fWGUp[dc % 2]
                            sp.inc(g.dma_start(out=wgu[dc % 2][:], in_=wgb_c[dc]), 16)
                            sp.inc(g.dma_start(out=wgu[2 + dc % 2][:], in_=wub_c[dc]), 16)
                    block.gpsimd(p4d_gp)

                    def p4d_pe(t, W=W, fWGUp0=fWGUp0, fSBp0=fSBp0, fSIL0=fSIL0,
                              fUCP0=fUCP0, fHID0=fHID0, fXAW0=fXAW0, fSXE0=fSXE0):
                        t.wait_ge(fXAW.h, fXAW0 + 4)
                        t.wait_ge(fSXE.h, fSXE0 + 4)
                        for dc in range(NDC):
                            t.wait_ge(fWGUp[dc % 2].h, fWGUp0[dc % 2] + 32 * (dc // 2 + 1))
                            if dc % 4 == 0:
                                dc4 = dc // 4
                                t.wait_ge(fSBp[dc4 % 2].h,
                                          fSBp0[dc4 % 2] + 64 * (dc4 // 2 + 1))
                            for half in range(2):
                                if dc >= 1:
                                    t.wait_ge(fSIL.h, fSIL0 + 2 * (dc - 1) + half + 1)
                                sl = slice(half * 512, (half + 1) * 512)
                                for fc in range(NFC):
                                    t.matmul(g_ps[:, sl],
                                             wgu[dc % 2][:, fc * 128:(fc + 1) * 128],
                                             xnt2[:, fc * FFN_W + half * 512: fc * FFN_W + half * 512 + 512],
                                             start=(fc == 0), stop=False)
                                t.matmul(g_ps[:, sl],
                                         sbg4[(dc // 4) % 2][:, (dc % 4) * 128:(dc % 4 + 1) * 128],
                                         sxg_sb[:, sl], start=False, stop=False)
                                fGMM.inc(t.matmul(
                                    g_ps[:, sl],
                                    bcg4[(dc // 4) % 2][:, (dc % 4) * 128:(dc % 4 + 1) * 128],
                                    xawg_sb[:, sl], start=False, stop=True))
                            for half in range(2):
                                if dc >= 2:
                                    t.wait_ge((fUCP if half == 0 else fHID).h,
                                              (fUCP0 if half == 0 else fHID0) + dc - 1)
                                sl = slice(half * 512, (half + 1) * 512)
                                for fc in range(NFC):
                                    t.matmul(u_ps[dc % 2][:, sl],
                                             wgu[2 + dc % 2][:, fc * 128:(fc + 1) * 128],
                                             xnt2[:, fc * FFN_W + half * 512: fc * FFN_W + half * 512 + 512],
                                             start=(fc == 0), stop=False)
                                t.matmul(u_ps[dc % 2][:, sl],
                                         sbu4[(dc // 4) % 2][:, (dc % 4) * 128:(dc % 4 + 1) * 128],
                                         sxu_sb[:, sl], start=False, stop=False)
                                fUMM.inc(t.matmul(
                                    u_ps[dc % 2][:, sl],
                                    bcu4[(dc // 4) % 2][:, (dc % 4) * 128:(dc % 4 + 1) * 128],
                                    xawu_sb[:, sl], start=False, stop=True))
                            for dcx in ([dc - 1] if dc >= 1 else []) + ([NDC - 1] if dc == NDC - 1 else []):
                                t.wait_ge(fHID.h, fHID0 + dcx + 1)
                                for half in range(2):
                                    sl = slice(half * 512, (half + 1) * 512)
                                    ins = t.matmul(hx_ps[:, sl],
                                                   sad72_sb[:, dcx * 72:(dcx + 1) * 72],
                                                   hid[:, dcx * FFN_W + half * 512: dcx * FFN_W + half * 512 + 512],
                                                   start=(dcx == 0), stop=(dcx == NDC - 1))
                                if dcx == NDC - 1:
                                    fHX.inc(ins)
                    block.tensor(p4d_pe)

                    def p4d_act(s, W=W, fGMM0=fGMM0, fHID0=fHID0, fUCP0=fUCP0):
                        for dc in range(NDC):
                            for half in range(2):
                                s.wait_ge(fGMM.h, fGMM0 + 2 * dc + half + 1)
                                if dc >= 1:
                                    s.wait_ge((fUCP if half == 0 else fHID).h,
                                              (fUCP0 if half == 0 else fHID0) + dc)
                                sl = slice(half * 512, (half + 1) * 512)
                                fSIL.inc(s.activation(gs_sb[:, sl], g_ps[:, sl],
                                                      AF.Silu))
                    block.scalar(p4d_act)

                    def p4d_vec(v, W=W, fUMM0=fUMM0, fSIL0=fSIL0, fHID0=fHID0,
                                fWBE0=fWBE0, fUCP0=fUCP0):
                        for dc in range(NDC):
                            v.wait_ge(fUMM.h, fUMM0 + 2 * dc + 2)
                            v.wait_ge(fSIL.h, fSIL0 + 2 * dc + 2)
                            fUCP.inc(v.tensor_mul(
                                hid[:, dc * FFN_W:dc * FFN_W + 512],
                                gs_sb[:, 0:512], u_ps[dc % 2][:, 0:512]))
                            fHID.inc(v.tensor_mul(
                                hid[:, dc * FFN_W + 512:(dc + 1) * FFN_W],
                                gs_sb[:, 512:1024], u_ps[dc % 2][:, 512:1024]))
                        v.wait_ge(fHX.h, W + 1)
                        v.wait_ge(fWBE.h, fWBE0 + 1)
                        fHXW.inc(v.tensor_mul(hxw72_sb[:], hx_ps[:], wbc72_sb[:]))
                    block.vector(p4d_vec)

                    if dbg:
                        def p4d_dump(sy, W=W, fHID0=fHID0):
                            for dc in range(NDC):
                                sy.wait_ge(fHID.h, fHID0 + dc + 1)
                                sy.dma_start(out=hid_c[W, dc],
                                             in_=hid[:, dc * FFN_W:(dc + 1) * FFN_W]
                                             ).then_inc(cLD.h, 16)
                        block.sync(p4d_dump)

                with ExitStack() as cr:
                    dn_ps = [cr.enter_context(nc.psum_tensor("dn_ps%d_w%d" % (i, W),
                                                             [128, 512], F32))
                             for i in range(8)]
                    fWDU0, fDN0, fOEV0 = (fWDU.n, fDN.n, fOEV.n)
                    fWDp0 = [fWDp[0].n, fWDp[1].n, fWDp[2].n]
                    fR1p0 = [fR1p[0].n, fR1p[1].n]
                    fOUTp0 = [fOUTp[0].n, fOUTp[1].n]

                    def p4e_wd(s, fWDU0=fWDU0, W=W):
                        if W == 0:
                            s.wait_ge(cWR.h, CWR_ALL)
                        for d4 in range(4):
                            for dc in range(NDC):
                                didx = d4 * NDC + dc
                                tgt = fWDU0 + didx - 2
                                if tgt > 0:
                                    s.wait_ge(fWDU.h, tgt)
                                fWDp[didx % 3].inc(s.dma_start(
                                    out=wd_t[didx % 3][:],
                                    in_=wdb_c[dc, d4]), 16)
                    block.scalar(p4e_wd)

                    def p4e_pe(t, W=W, fWDp0=fWDp0, fOEV0=fOEV0):
                        t.wait_ge(fHXW.h, W + 1)
                        for d4 in range(4):
                            if d4 >= 1:
                                t.wait_ge(fOEV.h, fOEV0 + 8 * d4)
                            for dc in range(NDC):
                                didx = d4 * NDC + dc
                                t.wait_ge(fWDp[didx % 3].h,
                                          fWDp0[didx % 3] + 16 * (didx // 3 + 1))
                                for tt in range(8):
                                    ins = t.matmul(
                                        dn_ps[tt][:],
                                        hid[:, dc * FFN_W + tt * 128: dc * FFN_W + (tt + 1) * 128],
                                        wd_t[didx % 3][:],
                                        start=(dc == 0), stop=False,
                                        skip_group_check=True)
                                fWDU.inc(ins)
                            for tt in range(8):
                                fDN.inc(t.matmul(
                                    dn_ps[tt][:], hxw72_sb[:, tt * 128:(tt + 1) * 128],
                                    sbbc_sb[:, d4 * 512:(d4 + 1) * 512],
                                    start=False, stop=True, skip_group_check=True))
                    block.tensor(p4e_pe)

                    def p4e_sy(sy, W=W, fOEV0=fOEV0):
                        sy.wait_ge(wWRp[0].h, wWRp[0].n)
                        sy.wait_ge(wWRp[1].h, wWRp[1].n)
                        for d4 in range(4):
                            for tt in range(8):
                                i = d4 * 8 + tt
                                if i >= 2:
                                    sy.wait_ge(fOEV.h, fOEV0 + i - 1)
                                fR1p[i % 2].inc(sy.dma_start(
                                    out=r1r[i % 2][:],
                                    in_=r1_c[W * FFN_W + tt * 128: W * FFN_W + (tt + 1) * 128,
                                             d4 * 512:(d4 + 1) * 512]), 16)
                                sy.wait_ge(fOEV.h, fOEV0 + i + 1)
                                fOUTp[i % 2].inc(sy.dma_start(
                                    out=out_d[W * FFN_W + tt * 128: W * FFN_W + (tt + 1) * 128,
                                              d4 * 512:(d4 + 1) * 512],
                                    in_=o_sb[i % 2][:]), 16)
                    block.sync(p4e_sy)

                    def p4e_vec(v, W=W, fDN0=fDN0, fR1p0=fR1p0, fOEV0=fOEV0,
                                fOUTp0=fOUTp0):
                        for d4 in range(4):
                            for tt in range(8):
                                i = d4 * 8 + tt
                                v.wait_ge(fDN.h, fDN0 + d4 * 8 + tt + 1)
                                v.wait_ge(fR1p[i % 2].h, fR1p0[i % 2] + 16 * (i // 2 + 1))
                                if i >= 2:
                                    v.wait_ge(fOUTp[i % 2].h, fOUTp0[i % 2] + 16 * (i // 2))
                                fOEV.inc(v.tensor_add(o_sb[i % 2][:], dn_ps[tt][:],
                                                      r1r[i % 2][:]))
                    block.vector(p4e_vec)

            def p_final(sy):
                sy.wait_ge(fOUTp[0].h, fOUTp[0].n)
                sy.wait_ge(fOUTp[1].h, fOUTp[1].n)
            block.sync(p_final)
            assert fOUTp[0].n + fOUTp[1].n == 16 * 64

    return nc


def _make_masks(r):
    m = np.zeros((8, 128, 512), np.float32)
    tri = np.triu(np.ones((128, 128), np.float32))
    for d in range(8):
        for mcol in range(4):
            delta = d - 2 * mcol - r
            blk = m[d][:, mcol * 128:(mcol + 1) * 128]
            if delta < 0:
                blk[:] = 1.0
            elif delta == 0:
                blk[:] = tri
    return m.astype(ml_dtypes.bfloat16)


def kernel(**inputs):
    if "nc" not in _CACHE:
        _CACHE["nc"] = build_nc()
    nc = _CACHE["nc"]

    X = np.ascontiguousarray(inputs["L0_attn_X"], dtype=np.float32)

    def f32(k):
        return np.ascontiguousarray(inputs[k], dtype=np.float32)

    weights = {
        "ln_g": f32("L0_attn_ln_g"), "ln_b": f32("L0_attn_ln_b"),
        "wq": f32("L0_attn_Wq"), "wk": f32("L0_attn_Wk"),
        "wv": f32("L0_attn_Wv"), "wo": f32("L0_attn_Wo"),
        "bo": f32("L0_attn_bo"),
        "w_router": f32("L0_ffn_W_router"),
        "w_gate": f32("L0_ffn_W_gate"), "w_up": f32("L0_ffn_W_up"),
        "w_down": f32("L0_ffn_W_down"),
        "sa_gate": f32("L0_ffn_sA_gate"), "sb_gate": f32("L0_ffn_sB_gate"),
        "sa_up": f32("L0_ffn_sA_up"), "sb_up": f32("L0_ffn_sB_up"),
        "sa_down": f32("L0_ffn_sA_down"), "sb_down": f32("L0_ffn_sB_down"),
        "a_gate": f32("L0_ffn_A_gate"), "b_gate": f32("L0_ffn_B_gate"),
        "a_up": f32("L0_ffn_A_up"), "b_up": f32("L0_ffn_B_up"),
        "a_down": f32("L0_ffn_A_down"), "b_down": f32("L0_ffn_B_down"),
    }
    bias_pp = np.zeros((128, 64), np.float32)
    bias_pp[:, 0:16] = f32("L0_attn_bq").reshape(16, 128).T
    bias_pp[:, 16:32] = f32("L0_attn_bk").reshape(16, 128).T
    bias_pp[:, 32:48] = f32("L0_attn_bv").reshape(16, 128).T
    ebias = np.ascontiguousarray(
        np.broadcast_to(f32("L0_ffn_expert_bias"), (128, 8)))
    masks = [_make_masks(0), _make_masks(1)]

    in_maps = []
    for c in range(8):
        b, r = c // 2, c % 2
        m = dict(weights)
        m["xb"] = X[b]
        m["xq"] = np.ascontiguousarray(
            X[b].reshape(32, 128, 2048)[r::2].reshape(2048, 2048))
        m["mq"] = masks[r]
        m["bias_pp"] = bias_pp
        m["ebias_bc"] = ebias
        in_maps.append(m)

    res = run_bass_kernel_spmd(nc, in_maps, core_ids=list(range(8)))
    out = np.empty((4, 4096, 2048), np.float32)
    for c in range(8):
        b, r = c // 2, c % 2
        out[b].reshape(32, 128, 2048)[r::2] = \
            res.results[c]["out"].reshape(16, 128, 2048)
    return out


if __name__ == "__main__":
    nc = build_nc()
    print("build ok; instructions:", nc.n_instructions())
